# version 03
# baseline (speedup 1.0000x reference)
import math
import numpy as np
import ml_dtypes
import concourse.bass as bass
import concourse.mybir as mybir
from concourse.bass_utils import run_bass_kernel_spmd

F32 = mybir.dt.float32
BF16 = mybir.dt.bfloat16
AF = mybir.ActivationFunctionType
ALU = mybir.AluOpType
AX = mybir.AxisListType
bfd = ml_dtypes.bfloat16

NCORES = 8
SEQ = 2048
NSEQ = 2
T = SEQ * NSEQ
D = 1024
NT = T // 128
DCH = D // 128
H0 = 16
DK = 64
E0, H0FF = 8, 4096
E1, H1FF = 6, 2048
A4 = 4096
LN_EPS = 1e-5
DEBUG_MOE = False
ENGS = ("tensor", "scalar", "vector", "gpsimd", "sync")


class Sem:
    __slots__ = ("h", "n", "name")
    def __init__(self, h, name):
        self.h = h
        self.n = 0
        self.name = name


class Op:
    __slots__ = ("fn", "waits", "incs", "tag")
    def __init__(self, fn, waits, incs, tag):
        self.fn = fn
        self.waits = waits
        self.incs = incs
        self.tag = tag


class Prog:

    def __init__(self, nc):
        self.nc = nc
        self.ops = {e: [] for e in ENGS}
        self.sems = {}
        self._exit = []

    def sem(self, name):
        if name not in self.sems:
            h = self.nc.semaphore(name).__enter__()
            self.sems[name] = Sem(h, name)
        return self.sems[name]

    def op(self, engine, fn, waits=(), incs=(), tag=""):
        ws = []
        for w in waits:
            if isinstance(w, Sem):
                ws.append((w, w.n))
            else:
                ws.append(w)
        evs = []
        rincs = []
        for s, amt in incs:
            s.n += amt
            rincs.append((s, amt))
            evs.append((s, s.n))
        self.ops[engine].append(Op(fn, ws, rincs, tag))
        return evs

    def barrier(self, tag="bar"):
        snap = [(s, s.n) for s in self.sems.values() if s.n > 0]
        bar = self.sem("bar")
        target = bar.n + len(ENGS)
        for e in ENGS:
            self.ops[e].append(Op(None, list(snap), [], tag + ":snap"))
            self.ops[e].append(Op(lambda eng: eng.nop(), [], [(bar, 1)], tag + ":inc"))
        bar.n = target
        for e in ENGS:
            self.ops[e].append(Op(None, [(bar, target)], [], tag + ":wait"))

    def dma(self, qeng, dst, src, waits=(), tag="dma", sem=None):
        sem = sem or self.sem("dma_" + qeng)
        return self.op(qeng, lambda eng: eng.dma_start(dst, src), waits,
                       [(sem, 16)], tag)

    def emit(self, block):
        def mk(engine):
            def run(eng):
                for o in self.ops[engine]:
                    for s, v in o.waits:
                        eng.wait_ge(s.h, v)
                    if o.fn is None:
                        continue
                    inst = o.fn(eng)
                    for s, amt in o.incs:
                        inst.then_inc(s.h, amt)
            return run
        block.tensor(mk("tensor"))
        block.scalar(mk("scalar"))
        block.vector(mk("vector"))
        block.gpsimd(mk("gpsimd"))
        block.sync(mk("sync"))

    def counts(self):
        return {e: len(v) for e, v in self.ops.items()}


class Pool:

    def __init__(self, t, n):
        self.t = t
        self.n = n
        self.off = 0
        self.hi = n

    def alloc(self, cols, name=""):
        off = self.off
        self.off += cols
        assert self.off <= self.hi, f"pool overflow {name}: {self.off} > {self.hi}"
        return self.t[:, off:off + cols]

    def persist(self, cols, name=""):
        self.hi -= cols
        assert self.off <= self.hi, f"pool overflow (persist) {name}"
        return self.t[:, self.hi:self.hi + cols]

    def reset(self):
        self.off = 0


def host_constants():
    d = np.arange(64)
    j = np.where(d < 32, d, d - 32)
    theta = 10000.0 ** (2.0 * j / 64.0)
    pos = np.arange(SEQ, dtype=np.float64)
    ang = pos[None, :] / theta[:, None]
    cos = np.cos(ang)
    sin = np.sin(ang)
    cos_fm = np.tile(cos, (2, 1)).astype(bfd)
    sin_fm = np.tile(sin, (2, 1)).astype(bfd)
    P = np.zeros((128, 128), np.float32)
    for b in range(2):
        for i in range(0, 64, 2):
            P[b * 64 + i + 1, b * 64 + i] = -1.0
            P[b * 64 + i, b * 64 + i + 1] = +1.0
    rope_perm = P.astype(bfd)
    ones2_64 = np.ones((2, 64), np.float32).astype(bfd)
    return {"rope_cos": cos_fm, "rope_sin": sin_fm, "rope_perm": rope_perm,
            "ones2_64": ones2_64}


def emit_setup(P, C):
    nc = P.nc
    s1 = P.sem("setup")
    g = P.op
    idb, idf = C["ident_bf"], C["ident_f32"]
    g("gpsimd", lambda e: e.memset(idb[:], 1.0), (), ((s1, 1),), "idb0")
    g("gpsimd", lambda e: e.affine_select(idb[:], idb[:], pattern=[[1, 128]],
      compare_op=ALU.is_equal, fill=0.0, base=0, channel_multiplier=-1),
      [(s1, s1.n)], ((s1, 1),), "idb1")
    g("gpsimd", lambda e: e.memset(idf[:], 1.0), [(s1, s1.n)], ((s1, 1),), "idf0")
    g("gpsimd", lambda e: e.affine_select(idf[:], idf[:], pattern=[[1, 128]],
      compare_op=ALU.is_equal, fill=0.0, base=0, channel_multiplier=-1),
      [(s1, s1.n)], ((s1, 1),), "idf1")
    qpos = C["qpos_row"]
    g("gpsimd", lambda e: e.iota(qpos[:], pattern=[[1, SEQ]], base=0,
      channel_multiplier=0, allow_small_or_imprecise_dtypes=True),
      (), ((s1, 1),), "iota")
    C["ev_setup"] = [(s1, s1.n)]


def load_weight_T(P, C, w_ap, n_out, n_in, dst_sb, tag):
    ld_sems = [P.sem("wt_ld0"), P.sem("wt_ld1")]
    mm = P.sem("wt_mm")
    ev = P.sem("wt_ev")
    ring = C.setdefault("wt_ring", Ring(list(C["wt_stage"])))
    psr = C.setdefault("wt_psr", Ring(list(C["ps_bf"])))
    idb = C["ident_bf"]
    su = P.sem("setup")
    id_ev = [(su, su.n)]
    n_oc = n_out // 128
    n_ic = n_in // 128
    for oc in range(n_oc):
        i, buf, rel = ring.take()
        ldi = ld_sems[i % 2]
        P.dma("gpsimd", buf[:, 0:n_in],
              w_ap[oc * 128:(oc + 1) * 128, :], rel, f"wT:{tag}:ld{oc}", ldi)
        lev = (ldi, ldi.n)
        last_mm = None
        for ic in range(n_ic):
            j, pb, relp = psr.take()
            P.op("tensor", (lambda b=buf, icc=ic, p=pb: (lambda e: e.transpose(
                p[:, 0:128], b[:, icc * 128:(icc + 1) * 128], idb[:])))(),
                [lev] + relp + id_ev, ((mm, 1),), f"wT:{tag}:tr{oc}.{ic}")
            last_mm = (mm, mm.n)
            dst = dst_sb[:, ic * n_out + oc * 128: ic * n_out + (oc + 1) * 128]
            e1 = P.op("vector", (lambda d=dst, p=pb: (lambda e: e.tensor_copy(
                d, p[:, 0:128])))(), [last_mm], ((ev, 1),),
                f"wT:{tag}:ev{oc}.{ic}")
            psr.release(j, [e1[0]])
        ring.release(i, [last_mm])


def per_partition_bias(P, C, vec_ap, n, dst_sb, tag):
    ap = vec_ap.rearrange("(c p) -> p c", p=128)
    sem = P.sem("dma_sync")
    P.op("sync", lambda e: e.dma_start(dst_sb[:, 0:n // 128], ap,
         allow_slow_non_contiguous=True), (), ((sem, 16),), "ppb:" + tag)
    return [(sem, sem.n)]


def broadcast_row(P, C, row_ap, n, dst_sb, tag, waits=()):
    src = row_ap if row_ap.ndim == 2 else row_ap.unsqueeze(0)
    sem = P.sem("dma_sync")
    P.op("sync", lambda e: e.dma_start(dst_sb[:, 0:n],
         src.partition_broadcast(128)), waits, ((sem, 16),), "bc:" + tag)
    return [(sem, sem.n)]


def emit_rstd(P, sums_col, sumsq_col, rstd_col, nmurstd_col, n, eps, waits, tag):
    g = P.op
    dv = P.sem("dve")
    sc = P.sem("act")
    inv = 1.0 / n
    g("vector", lambda e: e.tensor_scalar(nmurstd_col, sums_col, inv, None,
      ALU.mult), waits, ((dv, 1),), tag + ":mu")
    g("vector", lambda e: e.scalar_tensor_tensor(rstd_col, nmurstd_col, 1.0,
      nmurstd_col, ALU.mult, ALU.mult), [(dv, dv.n)], ((dv, 1),), tag + ":mu2")
    g("vector", lambda e: e.scalar_tensor_tensor(rstd_col, sumsq_col, inv,
      rstd_col, ALU.mult, ALU.subtract), [(dv, dv.n)], ((dv, 1),), tag + ":var")
    g("vector", lambda e: e.tensor_scalar(rstd_col, rstd_col, float(eps), None,
      ALU.add), [(dv, dv.n)], ((dv, 1),), tag + ":eps")
    g("scalar", lambda e: e.activation(rstd_col, rstd_col, AF.Ln, bias=0.0,
      scale=1.0), [(dv, dv.n)], ((sc, 1),), tag + ":ln")
    g("scalar", lambda e: e.activation(rstd_col, rstd_col, AF.Exp, bias=0.0,
      scale=-0.5), [(sc, sc.n)], ((sc, 1),), tag + ":exp")
    g("vector", lambda e: e.scalar_tensor_tensor(nmurstd_col, nmurstd_col, -1.0,
      rstd_col, ALU.mult, ALU.mult), [(sc, sc.n)], ((dv, 1),), tag + ":nmr")
    return [(P.sems["dve"], P.sems["dve"].n)]


def emit_cw_router(P, C, pb, mm_ev, scr, cols, tag):
    ac = P.sem("act"); dv = P.sem("dve")
    ne = 8
    pp = scr[:, 0:ne]; eqt = scr[:, ne:2 * ne]; cw = scr[:, 2 * ne:3 * ne]
    c_s = cols[:, 0:1]; c_m1 = cols[:, 1:2]; c_m2 = cols[:, 2:3]
    c_d = cols[:, 3:4]; c_r = cols[:, 4:5]
    e1 = P.op("scalar", (lambda p=pb: (lambda e: e.activation(pp, p[0:128, 0:ne],
        AF.Exp, bias=0.0, scale=1.0, accum_out=c_s)))(),
        [mm_ev, (dv, dv.n)], ((ac, 1),), tag + ":exp")
    P.op("vector", lambda e: e.reduce_max(c_m1, pp, axis=AX.X),
         [e1[0]], ((dv, 1),), tag + ":m1")
    P.op("vector", lambda e: e.tensor_scalar(eqt, pp, c_m1, None, ALU.is_ge),
         [(dv, dv.n)], ((dv, 1),), tag + ":eq")
    P.op("vector", lambda e: e.tensor_scalar(eqt, eqt, -1.0, 1.0, ALU.mult,
         ALU.add), [(dv, dv.n)], ((dv, 1),), tag + ":om")
    P.op("vector", lambda e: e.tensor_tensor(eqt, pp, eqt, ALU.mult),
         [(dv, dv.n)], ((dv, 1),), tag + ":pm")
    P.op("vector", lambda e: e.reduce_max(c_m2, eqt, axis=AX.X),
         [(dv, dv.n)], ((dv, 1),), tag + ":m2")
    P.op("vector", lambda e: e.tensor_scalar(c_d, c_s, 1e-9, None, ALU.mult),
         [(dv, dv.n)], ((dv, 1),), tag + ":ds")
    P.op("vector", lambda e: e.tensor_tensor(c_d, c_d, c_m1, ALU.add),
         [(dv, dv.n)], ((dv, 1),), tag + ":d1")
    P.op("vector", lambda e: e.tensor_tensor(c_d, c_d, c_m2, ALU.add),
         [(dv, dv.n)], ((dv, 1),), tag + ":d2")
    P.op("vector", lambda e: e.reciprocal(c_r, c_d), [(dv, dv.n)],
         ((dv, 1),), tag + ":rc")
    P.op("vector", lambda e: e.tensor_scalar(cw, pp, c_m2, None, ALU.is_ge),
         [(dv, dv.n)], ((dv, 1),), tag + ":sel")
    P.op("vector", lambda e: e.tensor_tensor(cw, cw, pp, ALU.mult),
         [(dv, dv.n)], ((dv, 1),), tag + ":cwp")
    ecw = P.op("vector", lambda e: e.tensor_scalar(cw, cw, c_r, None,
         ALU.mult), [(dv, dv.n)], ((dv, 1),), tag + ":cw")
    return cw, ecw[0]


def emit_cw_gate(P, C, pb, mm_ev, scr, cols, tag):
    ac = P.sem("act"); dv = P.sem("dve")
    ne = 6
    ll = scr[:, 0:ne]; eqt = scr[:, 8:8 + ne]; cw = scr[:, 16:16 + ne]
    c_m1 = cols[:, 1:2]; c_m2 = cols[:, 2:3]
    c_d = cols[:, 3:4]; c_r = cols[:, 4:5]; c_nm = cols[:, 5:6]
    e0 = P.op("scalar", (lambda p=pb: (lambda e: e.activation(ll,
        p[0:128, 0:ne], AF.Copy, bias=0.0, scale=1.0)))(),
        [mm_ev, (dv, dv.n)], ((ac, 1),), tag + ":cp")
    P.op("vector", lambda e: e.reduce_max(c_m1, ll, axis=AX.X),
         [e0[0]], ((dv, 1),), tag + ":m1")
    P.op("vector", lambda e: e.tensor_scalar(eqt, ll, c_m1, None, ALU.is_ge),
         [(dv, dv.n)], ((dv, 1),), tag + ":eq")
    P.op("vector", lambda e: e.tensor_scalar(eqt, eqt, -1e9, None, ALU.mult),
         [(dv, dv.n)], ((dv, 1),), tag + ":sc")
    P.op("vector", lambda e: e.tensor_tensor(eqt, eqt, ll, ALU.add),
         [(dv, dv.n)], ((dv, 1),), tag + ":lm")
    P.op("vector", lambda e: e.reduce_max(c_m2, eqt, axis=AX.X),
         [(dv, dv.n)], ((dv, 1),), tag + ":m2")
    P.op("vector", lambda e: e.tensor_scalar(c_nm, c_m1, -1.0, None, ALU.mult),
         [(dv, dv.n)], ((dv, 1),), tag + ":nm")
    P.op("vector", lambda e: e.tensor_tensor(c_d, c_m2, c_nm, ALU.add),
         [(dv, dv.n)], ((dv, 1),), tag + ":dm")
    e2 = P.op("scalar", lambda e: e.activation(c_d, c_d, AF.Exp, bias=0.0,
         scale=1.0), [(dv, dv.n)], ((ac, 1),), tag + ":ex")
    P.op("vector", lambda e: e.tensor_scalar(c_d, c_d, 1.0, None, ALU.add),
         [e2[0]], ((dv, 1),), tag + ":p1")
    P.op("vector", lambda e: e.reciprocal(c_r, c_d), [(dv, dv.n)],
         ((dv, 1),), tag + ":rc")
    e3 = P.op("scalar", (lambda: (lambda e: e.activation(ll, ll, AF.Exp,
        bias=c_nm, scale=1.0)))(), [(dv, dv.n)], ((ac, 1),), tag + ":z")
    P.op("vector", lambda e: e.tensor_scalar(cw, ll, c_m2, None, ALU.is_ge),
         [e3[0]], ((dv, 1),), tag + ":sel") if False else None
    P.op("vector", lambda e: e.tensor_scalar(c_d, c_d, -1.0, None, ALU.add),
         [(dv, dv.n)], ((dv, 1),), tag + ":em2")
    P.op("vector", lambda e: e.tensor_scalar(cw, ll, c_d, None, ALU.is_ge),
         [(dv, dv.n)], ((dv, 1),), tag + ":sel2")
    P.op("vector", lambda e: e.tensor_tensor(cw, cw, ll, ALU.mult),
         [(dv, dv.n)], ((dv, 1),), tag + ":cwz")
    ecw = P.op("vector", lambda e: e.tensor_scalar(cw, cw, c_r, None, ALU.mult),
         [(dv, dv.n)], ((dv, 1),), tag + ":cw")
    return cw, ecw[0]


def emit_p1_ln0(P, C):
    nc = P.nc
    bf = C["bf"]; f32 = C["f32"]
    x = C["in"]["x"]
    x0T = C["sc"]["x0T"]
    xn0 = C["sc"]["xn0"]
    idf, idb = C["ident_f32"], C["ident_bf"]
    ld = P.sem("p1_ld"); mm = P.sem("p1_mm"); ev = P.sem("p1_ev")
    ac = P.sem("act"); dv = P.sem("dve"); st = P.sem("p1_st")
    su = P.sem("setup")
    xt = [f32.alloc(1024, "xt0"), f32.alloc(1024, "xt1")]
    scr = f32.alloc(1024, "p1scr")
    xnf = [f32.alloc(1024, "xnf0"), f32.alloc(1024, "xnf1")]
    xnt = [bf.alloc(1024, "xnt0"), bf.alloc(1024, "xnt1")]
    cols = f32.alloc(8, "p1cols")
    evf = [bf.alloc(512, "evb0"), bf.alloc(512, "evb1")]
    evv = [f32.alloc(512, "evf0"), f32.alloc(512, "evf1")]
    xnfm = f32.alloc(1024, "xnfm")
    rt_scr = f32.alloc(24, "rt_scr")
    rt_cols = f32.alloc(8, "rt_cols")
    cwT_sb = f32.alloc(1024, "cwT")
    rw_nat = f32.alloc(1024, "rw_nat")
    rwT = f32.alloc(64, "rwT")
    psf = C["ps_f32"]; psb = C["ps_bf"]
    bank_ev = C.setdefault("bank_ev", [[] for _ in range(6)])

    ldrw = P.sem("p1_ldrw")
    P.dma("sync", rw_nat[0:8, :], C["in"]["L0_ffn_router_w"], (), "p1:rw", ldrw)
    rw_ld = (ldrw, ldrw.n)
    for c in range(DCH):
        P.op("tensor", (lambda cc=c: (lambda e: e.transpose(
            psf[5][0:128, 0:8], rw_nat[0:8, cc * 128:(cc + 1) * 128],
            idf[0:8, 0:8])))(), [rw_ld, (su, su.n)] + bank_ev[5], ((mm, 1),),
            f"p1:rwt{c}")
        e1 = P.op("vector", (lambda cc=c: (lambda e: e.tensor_copy(
            rwT[:, cc * 8:(cc + 1) * 8], psf[5][0:128, 0:8])))(),
            [(mm, mm.n)], ((dv, 1),), f"p1:rwe{c}")
        bank_ev[5] = [e1[0]]
    rw_ev = [(dv, dv.n)]

    lds = [P.sem("p1_ld0"), P.sem("p1_ld1")]
    for it in range(NT):
        b = it % 2
        xbuf = xt[b]
        ld = lds[b]
        w = [(mm, mm.n), (ac, ac.n)] if it >= 2 else ()
        P.dma("sync", xbuf[:, :], x[it * 128:(it + 1) * 128, :], w, f"p1:ld{it}", ld)
        lev = (ld, ld.n)
        P.op("scalar", (lambda xb=xbuf, c=cols: (lambda e: e.activation(
            scr[:, 0:1024], xb[:, :], AF.Copy, bias=0.0, scale=1.0,
            accum_out=c[:, 0:1])))(), [lev, (dv, dv.n)], ((ac, 1),), f"p1:sx{it}")
        P.op("scalar", (lambda xb=xbuf, c=cols: (lambda e: e.activation(
            scr[:, 0:1024], xb[:, :], AF.Square, bias=0.0, scale=1.0,
            accum_out=c[:, 1:2])))(), [(ac, ac.n)], ((ac, 1),), f"p1:sx2{it}")
        emit_rstd(P, cols[:, 0:1], cols[:, 1:2], cols[:, 2:3], cols[:, 3:4],
                  D, LN_EPS, [(ac, ac.n)], f"p1:st{it}")
        xnb = xnt[b]; xnfb = xnf[b]
        ef = P.op("scalar", (lambda xb=xbuf, xn=xnfb, c=cols: (lambda e:
            e.activation(xn[:, :], xb[:, :], AF.Identity, bias=c[:, 3:4],
                         scale=c[:, 2:3])))(), [(dv, dv.n)], ((ac, 1),),
            f"p1:xnf{it}")
        eb = P.op("scalar", (lambda xf=xnfb, xn=xnb: (lambda e: e.activation(
            xn[:, :], xf[:, :], AF.Copy, bias=0.0, scale=1.0)))(),
            [ef[0]], ((ac, 1),), f"p1:xnb{it}")
        xnev = eb[0]
        xnfev = ef[0]
        for half in range(2):
            pf = psf[half]
            pfn = psf[2 + half]
            pbz = psb[half]
            for q in range(4):
                c = half * 4 + q
                P.op("tensor", (lambda xb=xbuf, cc=c, p=pf, qq=q: (lambda e:
                    e.transpose(p[:, qq * 128:(qq + 1) * 128],
                                xb[:, cc * 128:(cc + 1) * 128], idf[:])))(),
                    [lev, (ev, ev.n)] if it * 2 + half >= 2 else [lev, (su, su.n)],
                    ((mm, 1),), f"p1:trf{it}.{c}")
                P.op("tensor", (lambda xn=xnb, cc=c, p=pbz, qq=q: (lambda e:
                    e.transpose(p[:, qq * 128:(qq + 1) * 128],
                                xn[:, cc * 128:(cc + 1) * 128], idb[:])))(),
                    [xnev, (ev, ev.n)] if it * 2 + half >= 2 else [xnev],
                    ((mm, 1),), f"p1:trb{it}.{c}")
                P.op("tensor", (lambda xn=xnfb, cc=c, p=pfn, qq=q: (lambda e:
                    e.transpose(p[:, qq * 128:(qq + 1) * 128],
                                xn[:, cc * 128:(cc + 1) * 128], idf[:])))(),
                    [xnfev, (ev, ev.n)] if it * 2 + half >= 2 else [xnfev],
                    ((mm, 1),), f"p1:trn{it}.{c}")
            mev = (mm, mm.n)
            ef2 = evv[half]
            eb2 = evf[half]
            sth_sems = [P.sem("p1_st0"), P.sem("p1_st1")]
            P.op("scalar", (lambda p=pf, d=ef2: (lambda e: e.activation(
                d[:, :], p[:, 0:512], AF.Copy, bias=0.0, scale=1.0)))(),
                [mev, (sth_sems[half], sth_sems[half].n)] if it * 2 + half >= 2 else [mev],
                ((ev, 1),), f"p1:evf{it}.{half}")
            P.op("vector", (lambda p=pbz, d=eb2: (lambda e: e.tensor_copy(
                d[:, :], p[:, 0:512])))(),
                [mev, (sth_sems[half], sth_sems[half].n)] if it * 2 + half >= 2 else [mev],
                ((ev, 1),), f"p1:evb{it}.{half}")
            P.op("vector", (lambda p=pfn, d=xnfm, hh=half: (lambda e:
                e.tensor_copy(d[:, hh * 512:(hh + 1) * 512], p[:, 0:512])))(),
                [mev, (mm, mm.n)], ((ev, 1),), f"p1:evn{it}.{half}")
            eev = (ev, ev.n)
            sth = sth_sems[half]
            for q in range(4):
                c = half * 4 + q
                P.dma("scalar", x0T[c, :, it * 128:(it + 1) * 128],
                      ef2[:, q * 128:(q + 1) * 128], [eev], f"p1:so{it}.{c}", sth)
                P.dma("scalar", xn0[c, :, it * 128:(it + 1) * 128],
                      eb2[:, q * 128:(q + 1) * 128], [eev], f"p1:sn{it}.{c}", sth)
        fm_ev = (ev, ev.n)
        rpb = psf[4]
        for c in range(DCH):
            P.op("tensor", (lambda cc=c: (lambda e: e.matmul(
                rpb[0:128, 0:8], xnfm[:, cc * 128:(cc + 1) * 128],
                rwT[:, cc * 8:(cc + 1) * 8], start=(cc == 0),
                stop=(cc == DCH - 1), skip_group_check=True)))(),
                ([fm_ev] + rw_ev + bank_ev[4]) if c == 0 else (),
                ((mm, 1),) if c == DCH - 1 else (), f"p1:rt{it}.{c}")
        rt_mm = (mm, mm.n)
        cw_ap, cw_ev = emit_cw_router(P, C, rpb, rt_mm, rt_scr, rt_cols,
                                      f"p1rt{it}")
        bank_ev[4] = [cw_ev]
        P.op("tensor", (lambda: (lambda e: e.transpose(psf[5][0:8, 0:128],
            cw_ap, idf[:])))(), [cw_ev] + bank_ev[5], ((mm, 1),),
            f"p1:ct{it}")
        e9 = P.op("vector", (lambda itt=it: (lambda e: e.tensor_copy(
            cwT_sb[0:8, (itt % 8) * 128:(itt % 8 + 1) * 128],
            psf[5][0:8, 0:128])))(),
            [(mm, mm.n), (P.sem("p1_stc"), P.sem("p1_stc").n)], ((dv, 1),),
            f"p1:ce{it}")
        bank_ev[5] = [e9[0]]
        if it % 8 == 7:
            ts8 = it // 8
            stc = P.sem("p1_stc")
            P.dma("scalar", C["sc"]["cw0"][:, ts8 * 1024:(ts8 + 1) * 1024],
                  cwT_sb[0:8, 0:1024], [e9[0]], f"p1:cs{ts8}", stc)
    P.op("sync", None, [(P.sem("p1_st0"), P.sem("p1_st0").n),
         (P.sem("p1_st1"), P.sem("p1_st1").n),
         (P.sem("p1_stc"), P.sem("p1_stc").n),
         (P.sem("dma_scalar"), P.sem("dma_scalar").n)], (), "p1:fin")
    P.barrier("p1")


class Ring:
    def __init__(self, slices):
        self.s = slices
        self.ev = [[] for _ in slices]
        self.i = 0

    def take(self):
        i = self.i % len(self.s)
        self.i += 1
        return i, self.s[i], list(self.ev[i])

    def release(self, i, evs):
        self.ev[i] = list(evs)


def emit_qkv(P, C, layer):
    nc = P.nc
    bf = C["bf"]; f32 = C["f32"]
    IN = C["in"]; SC = C["sc"]
    xn_src = SC["xn0"] if layer == 0 else SC["zn1"]
    qsc = SC["q0"] if layer == 0 else SC["q1"]
    ksc = SC["k0"] if layer == 0 else SC["k1"]
    vsc = SC["v0"] if layer == 0 else SC["v1"]
    mm = P.sem("qkv_mm")
    ac = P.sem("act"); dv = P.sem("dve")
    su = P.sem("setup")
    xn_sems = [P.sem("qkv_xn0"), P.sem("qkv_xn1")]
    qkst_sems = [P.sem("qkv_qs0"), P.sem("qkv_qs1"), P.sem("qkv_qs2")]
    vst_sems = [P.sem("qkv_vs0"), P.sem("qkv_vs1")]
    aug_st = P.sem("qkv_aug")
    psf = C["ps_f32"]
    bank_ev = C.setdefault("bank_ev", [[] for _ in range(6)])

    if layer == 0:
        wqT = bf.alloc(8192, "wqT"); wkT = bf.alloc(8192, "wkT")
        wvT = bf.alloc(8192, "wvT")
        load_weight_T(P, C, IN["L0_attn_wq"], 1024, 1024, wqT, "wq")
        load_weight_T(P, C, IN["L0_attn_wk"], 1024, 1024, wkT, "wk")
        load_weight_T(P, C, IN["L0_attn_wv"], 1024, 1024, wvT, "wv")
        def lhsT_tile(which, ic, oc):
            t = {"q": wqT, "k": wkT, "v": wvT}[which]
            return t[:, ic * 1024 + oc * 128: ic * 1024 + (oc + 1) * 128]
        def v_rhs(ic, half):
            return wvT[:, ic * 1024 + half * 512: ic * 1024 + half * 512 + 512]
        bq_pp = f32.alloc(8, "bq"); bk_pp = f32.alloc(8, "bk")
        per_partition_bias(P, C, IN["L0_attn_bq"], 1024, bq_pp, "bq")
        per_partition_bias(P, C, IN["L0_attn_bk"], 1024, bk_pp, "bk")
        bvB = f32.alloc(1024, "bvB")
        broadcast_row(P, C, IN["L0_attn_bv"], 1024, bvB, "bvB")
        cs_sb = bf.alloc(2048, "cos"); sn_sb = bf.alloc(2048, "sin")
        perm_sb = bf.alloc(128, "perm")
        P.dma("sync", cs_sb[:, :], IN["rope_cos"], (), "cos")
        P.dma("sync", sn_sb[:, :], IN["rope_sin"], (), "sin")
        P.dma("sync", perm_sb[:, :], IN["rope_perm"], (), "perm")
    else:
        wT = bf.alloc(24576, "wqkvT")
        load_weight_T(P, C, IN["L1_attn_w_qkv"], 3072, 1024, wT, "wqkv")
        def lhsT_tile(which, ic, oc):
            base = {"q": 0, "k": 1024, "v": 2048}[which]
            return wT[:, ic * 3072 + base + oc * 128: ic * 3072 + base + (oc + 1) * 128]
        def v_rhs(ic, half):
            return wT[:, ic * 3072 + 2048 + half * 512: ic * 3072 + 2048 + half * 512 + 512]
        slopes_sb = f32.alloc(16, "slopes")
        P.op("sync", lambda e: e.dma_start(slopes_sb[0:1, 0:16],
             IN["L1_attn_alibi_slopes"].unsqueeze(0)), (),
             ((P.sem("dma_sync"), 16),), "slopes")
    wt_ev = P.sem("wt_ev")
    dsy = P.sem("dma_sync")
    w_ready = [(wt_ev, wt_ev.n), (dsy, dsy.n)]

    xnr = Ring([bf.alloc(DCH * 512, f"xnc{i}") for i in range(2)])

    stg = Ring([bf.alloc(512, f"qk_s{i}") for i in range(3)])
    rtg = Ring([bf.alloc(512, f"qk_r{i}") for i in range(2)])
    t1g = Ring([bf.alloc(512, f"qk_t{i}") for i in range(2)])
    outg = Ring([bf.alloc(512, f"qk_o{i}") for i in range(3)])
    vst = Ring([bf.alloc(1040, f"vst{i}") for i in range(2)])
    for i in range(2):
        P.op("gpsimd", (lambda t=vst.s[i]: (lambda e: e.memset(t[:, :], 1.0)))(),
             (), ((su, 1),), f"vones{i}")
    ones_ev = [(su, su.n)]
    bias_ev = [(dsy, dsy.n)]

    for tc in range(8):
        s = tc // 4
        pos0 = (tc % 4) * 512
        ix, xn_c, relx = xnr.take()
        ldx = xn_sems[ix % 2]
        for c in range(DCH):
            P.dma("sync", xn_c[:, c * 512:(c + 1) * 512],
                  xn_src[c][:, tc * 512:(tc + 1) * 512],
                  relx if c == 0 else (), f"qkv{layer}:xn{tc}.{c}", ldx)
        xn_ev = [(ldx, ldx.n)]
        xn_consumers = []

        def xn_rhs(ic):
            return xn_c[:, ic * 512:(ic + 1) * 512]

        def xn_lhsT(ic, q):
            return xn_c[:, ic * 512 + q * 128: ic * 512 + q * 128 + 128]

        for which in ("q", "k"):
            for oc in range(8):
                bank = (oc + (0 if which == "q" else 1)) % 4
                pb = psf[bank]
                for ic in range(DCH):
                    waits = []
                    if ic == 0:
                        waits = w_ready + xn_ev + bank_ev[bank]
                    P.op("tensor", (lambda l=lhsT_tile(which, ic, oc),
                        r=xn_rhs(ic), p=pb, st_=(ic == 0), sp=(ic == DCH - 1):
                        (lambda e: e.matmul(p[:], l, r, start=st_, stop=sp)))(),
                        waits, ((mm, 1),) if ic == DCH - 1 else (),
                        f"qkv{layer}:{which}mm{oc}.{tc}")
                mm_ev = (mm, mm.n)
                xn_consumers.append(mm_ev)
                i0, sbuf_o, rel = stg.take()
                if layer == 0:
                    b_pp = bq_pp if which == "q" else bk_pp
                    fn = (lambda p=pb, d=sbuf_o, b=b_pp, o=oc: (lambda e:
                        e.activation(d[:, :], p[:], AF.Identity,
                                     bias=b[:, o:o + 1], scale=1.0)))()
                else:
                    sc_ = 0.125 if which == "q" else 1.0
                    fn = (lambda p=pb, d=sbuf_o, sc2=sc_: (lambda e:
                        e.activation(d[:, :], p[:], AF.Copy, bias=0.0,
                                     scale=sc2)))()
                eva = P.op("scalar", fn, [mm_ev] + rel + bias_ev, ((ac, 1),),
                           f"qkv{layer}:{which}ev{oc}.{tc}")
                bank_ev[bank] = [eva[0]]
                if layer == 0:
                    rbank = 4 + oc % 2
                    rpb = psf[rbank]
                    P.op("tensor", (lambda d=sbuf_o, p=rpb: (lambda e:
                        e.matmul(p[:], perm_sb[:, :], d[:, :], start=True,
                                 stop=True)))(),
                        [eva[0]] + bank_ev[rbank], ((mm, 1),),
                        f"qkv0:{which}rot{oc}.{tc}")
                    rot_mm = (mm, mm.n)
                    i1, rot_o, rel1 = rtg.take()
                    ebv = P.op("scalar", (lambda p=rpb, d=rot_o: (lambda e:
                        e.activation(d[:, :], p[:], AF.Copy, bias=0.0,
                                     scale=1.0)))(), [rot_mm] + rel1,
                        ((ac, 1),), f"qkv0:{which}rev{oc}.{tc}")
                    bank_ev[rbank] = [ebv[0]]
                    i2, t1_o, rel2 = t1g.take()
                    e1 = P.op("vector", (lambda a=sbuf_o, d=t1_o,
                        p0=pos0: (lambda e: e.tensor_tensor(d[:, :], a[:, :],
                        cs_sb[:, p0:p0 + 512], ALU.mult)))(),
                        [eva[0]] + rel2, ((dv, 1),), f"qkv0:{which}c{oc}.{tc}")
                    stg.release(i0, [rot_mm, e1[0]])
                    i3, out_o, rel3 = outg.take()
                    e2 = P.op("vector", (lambda a=rot_o, d=out_o,
                        p0=pos0: (lambda e: e.tensor_tensor(d[:, :], a[:, :],
                        sn_sb[:, p0:p0 + 512], ALU.mult)))(),
                        [ebv[0]] + rel3, ((dv, 1),), f"qkv0:{which}s{oc}.{tc}")
                    rtg.release(i1, [e2[0]])
                    e3 = P.op("vector", (lambda a=t1_o, b=out_o: (lambda e:
                        e.tensor_tensor(a[:, :], a[:, :], b[:, :], ALU.add)))(),
                        [e1[0], e2[0]], ((dv, 1),), f"qkv0:{which}a{oc}.{tc}")
                    outg.release(i3, [e3[0]])
                    final = t1_o
                    fev = e3[0]
                    out_slot = i2
                    def relf(evs, _i=i2):
                        t1g.release(_i, evs)
                else:
                    final = sbuf_o
                    fev = eva[0]
                    out_slot = i0
                    def relf(evs, _i=i0):
                        stg.release(_i, evs)
                dst = qsc if which == "q" else ksc
                slot_sem = qkst_sems[out_slot % 3]
                d1 = P.dma("sync", dst[s, 2 * oc, 0:64, pos0:pos0 + 512],
                           final[0:64, :], [fev], f"qkv{layer}:{which}o1",
                           slot_sem)
                d2 = P.dma("sync", dst[s, 2 * oc + 1, 0:64, pos0:pos0 + 512],
                           final[64:128, :], [fev], f"qkv{layer}:{which}o2",
                           slot_sem)
                relf([(slot_sem, slot_sem.n)])

        for q in range(4):
            tt = tc * 4 + q
            iv, vbuf, relv = vst.take()
            evs_half = []
            for half in range(2):
                bank = (tt * 2 + half) % 4
                pb = psf[bank]
                for ic in range(DCH):
                    waits = []
                    if ic == 0:
                        waits = w_ready + xn_ev + bank_ev[bank]
                    P.op("tensor", (lambda l=xn_lhsT(ic, q), r=v_rhs(ic, half),
                        p=pb, st_=(ic == 0), sp=(ic == DCH - 1): (lambda e:
                        e.matmul(p[:], l, r, start=st_, stop=sp)))(),
                        waits, ((mm, 1),) if ic == DCH - 1 else (),
                        f"qkv{layer}:vmm{tt}.{half}")
                mm_ev = (mm, mm.n)
                xn_consumers.append(mm_ev)
                dstv = vbuf[:, half * 520:(half + 1) * 520].rearrange(
                    "p (h j) -> p h j", h=8, j=65)[:, :, 0:64]
                if layer == 0:
                    fn = (lambda p=pb, d=dstv, h=half: (lambda e:
                        e.tensor_tensor(d, p[:], bvB[:, h * 512:(h + 1) * 512],
                                        ALU.add)))()
                    eng = "vector"; sem_ = dv
                else:
                    fn = (lambda p=pb, d=dstv: (lambda e: e.activation(
                        d, p[:], AF.Copy, bias=0.0, scale=1.0)))()
                    eng = "scalar"; sem_ = ac
                w2 = [mm_ev] + relv + ones_ev + (bias_ev if layer == 0 else [])
                eh = P.op(eng, fn, w2, ((sem_, 1),), f"qkv{layer}:vev{tt}.{half}")
                bank_ev[bank] = [eh[0]]
                evs_half.append(eh[0])
                relv = []
            vs_sem = vst_sems[iv % 2]
            dd = P.dma("sync", vsc[tt], vbuf[:, :], evs_half,
                       f"qkv{layer}:vo{tt}", vs_sem)
            vst.release(iv, [dd[0]])
        xnr.release(ix, xn_consumers)

    if layer == 1:
        qpos = C["qpos_row"]
        ar = f32.alloc(4096, "augf")
        ab = bf.alloc(4096, "augb")
        ones_row = bf.alloc(2048, "ones_row")
        P.op("gpsimd", lambda e: e.memset(ones_row[0:1, :], 1.0), (),
             ((su, 1),), "aug:ones")
        one_ev = (su, su.n)
        for h in range(H0):
            for side, sgn in (("k", 1.0), ("q", -1.0)):
                rf = ar[0:1, 0:2048]
                hf = ar[0:1, 2048:4096]
                hb = ab[0:1, 0:2048]
                lb = ab[0:1, 2048:4096]
                sc_ap = slopes_sb[0:1, h:h + 1]
                prev = [(ac, ac.n), (dv, dv.n), (aug_st, aug_st.n),
                        (dsy, dsy.n)]
                P.op("scalar", (lambda r=rf, sa=sc_ap: (lambda e:
                    e.activation(r, qpos[0:1, :], AF.Identity, bias=0.0,
                                 scale=sa)))(), prev, ((ac, 1),),
                    f"aug:{side}{h}a")
                if sgn < 0:
                    P.op("scalar", (lambda r=rf: (lambda e: e.activation(
                        r, r, AF.Copy, bias=0.0, scale=-1.0)))(),
                        [(ac, ac.n)], ((ac, 1),), f"aug:{side}{h}n")
                P.op("scalar", (lambda r=rf, d=hb: (lambda e: e.activation(
                    d, r, AF.Copy, bias=0.0, scale=1.0)))(), [(ac, ac.n)],
                    ((ac, 1),), f"aug:{side}{h}h")
                P.op("scalar", (lambda r2=hb, d=hf: (lambda e: e.activation(
                    d, r2, AF.Copy, bias=0.0, scale=1.0)))(), [(ac, ac.n)],
                    ((ac, 1),), f"aug:{side}{h}hf")
                P.op("vector", (lambda a=rf, b=hf, d=lb: (lambda e:
                    e.tensor_tensor(d, a, b, ALU.subtract)))(),
                    [(ac, ac.n)], ((dv, 1),), f"aug:{side}{h}lo")
                dvev = (dv, dv.n)
                dstt = ksc if side == "k" else qsc
                r0 = 64 if side == "k" else 66
                for s in range(NSEQ):
                    P.dma("sync", dstt[s, h, r0:r0 + 1, :], hb, [dvev],
                          f"aug:{side}{h}s{s}h", aug_st)
                    P.dma("sync", dstt[s, h, r0 + 1:r0 + 2, :], lb, [dvev],
                          f"aug:{side}{h}s{s}l", aug_st)
        for s in range(NSEQ):
            for h in range(H0):
                for r in (66, 67):
                    P.dma("sync", ksc[s, h, r:r + 1, :], ones_row[0:1, :],
                          [one_ev], f"aug:ok{s}.{h}.{r}", aug_st)
                for r in (64, 65):
                    P.dma("sync", qsc[s, h, r:r + 1, :], ones_row[0:1, :],
                          [one_ev], f"aug:oq{s}.{h}.{r}", aug_st)
    fin_waits = [(sx, sx.n) for sx in qkst_sems + vst_sems + [aug_st]]
    P.op("sync", None, fin_waits, (), f"qkv{layer}:fin")
    P.barrier(f"qkv{layer}")


def emit_attn(P, C, layer):
    nc = P.nc
    bf = C["bf"]; f32 = C["f32"]
    IN = C["in"]; SC = C["sc"]
    qsc = SC["q0" if layer == 0 else "q1"]
    ksc = SC["k0" if layer == 0 else "k1"]
    vsc = SC["v0" if layer == 0 else "v1"]
    ssc = SC["s0" if layer == 0 else "s1"]
    KR = 64 if layer == 0 else 68
    mm = P.sem("at_mm")
    ac = P.sem("act"); dv = P.sem("dve")
    ldv = P.sem("at_v")
    q_sems = [P.sem("at_q0"), P.sem("at_q1")]
    k_sems = [P.sem("at_k0"), P.sem("at_k1")]
    s_sems = [P.sem("at_s0"), P.sem("at_s1")]
    su = P.sem("setup"); dsy = P.sem("dma_sync")
    psf = C["ps_f32"]
    bank_ev = C.setdefault("bank_ev", [[] for _ in range(6)])

    woT = bf.alloc(8192, "woT")
    if layer == 0:
        load_weight_T(P, C, IN["L0_attn_wo"], 1024, 1024, woT, "wo")
        bo_pp = f32.alloc(8, "bo")
        per_partition_bias(P, C, IN["L0_attn_bo"], 1024, bo_pp, "bo")
    else:
        load_weight_T(P, C, IN["L1_attn_w_out"], 1024, 1024, woT, "wout")
        bo_pp = None
    wt_ev = P.sem("wt_ev")
    w_ready = [(wt_ev, wt_ev.n), (dsy, dsy.n)]

    v_sb = bf.alloc(NT * 1040, "v_sb")
    for tt in range(NT):
        P.dma("sync", v_sb[:, tt * 1040:(tt + 1) * 1040],
              vsc[tt], (), f"at{layer}:v{tt}", ldv)
    v_ev = [(ldv, ldv.n)]

    def v_tile(s, kt, h):
        col = (s * 16 + kt) * 1040 + h * 65
        return v_sb[:, col:col + 65]

    ones_bb = bf.alloc(64, "ones_bb")
    P.op("gpsimd", lambda e: e.memset(ones_bb[0:2, :], 1.0), [(su, su.n)],
         ((su, 1),), "at:ones")

    den = f32.alloc(1536, "den")
    rec2 = bf.alloc(1024, "rec2")
    recS = f32.alloc(512, "recS")
    attn_fm = bf.alloc(16384, "attn_fm")
    sev = Ring([f32.alloc(512, "soev0")])

    qg = Ring([bf.alloc(2048, f"qg{i}") for i in range(2)])
    kg = Ring([bf.alloc(2048, f"kg{i}") for i in range(2)])

    def load_qk(s, h):
        iq, qb, relq = qg.take()
        ik, kb, relk = kg.take()
        if not relq:
            relq = [(mm, mm.n)]
        if not relk:
            relk = [(mm, mm.n)]
        P.dma("sync", qb[0:KR, :], qsc[s, h, 0:KR, :], relq, f"at:q{s}.{h}",
              q_sems[iq % 2])
        P.dma("sync", kb[0:KR, :], ksc[s, h, 0:KR, :], relk, f"at:k{s}.{h}",
              k_sems[ik % 2])
        return iq, qb, ik, kb, [(q_sems[iq % 2], q_sems[iq % 2].n),
                                (k_sems[ik % 2], k_sems[ik % 2].n)]

    pend = load_qk(0, 0)
    s4r = Ring([f32.alloc(2048, "s4_0")])
    u4r = Ring([f32.alloc(2048, f"u4_{i}") for i in range(2)])
    a4r = Ring([bf.alloc(2048, f"a4_{i}") for i in range(2)])
    if layer == 1:
        mask4 = f32.alloc(2048, "mask4")
        for dlt in range(4):
            P.op("gpsimd", (lambda d=dlt: (lambda e: e.memset(
                 mask4[:, d * 512:(d + 1) * 512], 1e9)))(), [(su, su.n)],
                 ((su, 1),), f"at:M{dlt}a")
            P.op("gpsimd", (lambda d=dlt: (lambda e:
                 e.affine_select(mask4[:, d * 512:(d + 1) * 512],
                 mask4[:, d * 512:(d + 1) * 512], pattern=[[1, 512]],
                 compare_op=ALU.is_ge, fill=-10000.0, base=-128 * d,
                 channel_multiplier=-1)))(), [(su, su.n)], ((su, 1),),
                 f"at:M{dlt}b")
    mask_ev = [(su, su.n)]

    if layer == 1:
        negt = f32.alloc(512, "negt")
        nt_u = s4r.s[0][:, 0:512]
        nt_r = s4r.s[0][:, 512:1024]
        nt_a = u4r.s[0][:, 0:512]
        nt_b = u4r.s[0][:, 512:1024]
        atn_neg = bf.alloc(512, "atn_neg")
        P.op("gpsimd", lambda e: e.memset(negt[:, :], -10000.0), [(su, su.n)],
             ((su, 1),), "at:neg0")
        nev = [(su, su.n)]
        P.op("scalar", lambda e: e.activation(nt_u, negt[:, :],
             AF.Square, bias=0.0, scale=1.0), nev, ((ac, 1),), "at:neg1")
        P.op("scalar", lambda e: e.activation(nt_r, nt_u,
             AF.Ln, bias=1.0, scale=1.0), [(ac, ac.n)], ((ac, 1),), "at:neg2")
        P.op("scalar", lambda e: e.activation(nt_r, nt_r,
             AF.Exp, bias=0.0, scale=-0.5), [(ac, ac.n)], ((ac, 1),), "at:neg3")
        P.op("vector", lambda e: e.tensor_tensor(nt_a, nt_r, nt_r, ALU.mult),
             [(ac, ac.n)], ((dv, 1),), "at:nw1")
        P.op("vector", lambda e: e.tensor_tensor(nt_b, nt_a, nt_u, ALU.mult),
             [(dv, dv.n)], ((dv, 1),), "at:nw2")
        P.op("vector", lambda e: e.tensor_tensor(nt_b, nt_b, nt_a, ALU.add),
             [(dv, dv.n)], ((dv, 1),), "at:nw3")
        P.op("vector", lambda e: e.tensor_scalar(nt_b, nt_b, -0.5,
             1.5, ALU.mult, ALU.add), [(dv, dv.n)], ((dv, 1),), "at:nw4")
        P.op("vector", lambda e: e.tensor_tensor(nt_r, nt_r, nt_b, ALU.mult),
             [(dv, dv.n)], ((dv, 1),), "at:nw5")
        P.op("vector", lambda e: e.tensor_tensor(negt[:, :], negt[:, :],
             nt_r, ALU.mult), [(dv, dv.n)], ((dv, 1),), "at:neg4")
        ex9 = P.op("scalar", lambda e: e.activation(atn_neg[:, :], negt[:, :],
             AF.Relu, bias=1.0, scale=1.0), [(dv, dv.n)], ((ac, 1),), "at:neg5")
        neg_ready = [(ac, ac.n)]
        s4r.ev[0] = [(dv, dv.n)]
        u4r.ev[0] = [(dv, dv.n)]

    for s in range(NSEQ):
        fm_guard = [(mm, mm.n)] if s else []
        for h in range(H0):
            iq, qb, ik, kb, qk_ev = pend
            if not (s == NSEQ - 1 and h == H0 - 1):
                nh = h + 1 if h < H0 - 1 else 0
                ns = s if h < H0 - 1 else s + 1
                pend = load_qk(ns, nh)
            q_consumers = []
            for qc in range(4):
                kmax = 4 * (qc + 1)
                nb = qc + 1
                ot_bank = 4
                otp = psf[ot_bank]
                batch_state = []
                rela_by_slot = {}
                for b in range(nb + 1):
                    if b < nb:
                        for j in range(4):
                            kt = 4 * b + j
                            pb = psf[j]
                            P.op("tensor", (lambda
                                l=kb[0:KR, kt * 128:(kt + 1) * 128],
                                r=qb[0:KR, qc * 512:(qc + 1) * 512], p=pb:
                                (lambda e: e.matmul(p[:], l, r, start=True,
                                 stop=True, skip_group_check=True)))(),
                                qk_ev + bank_ev[j] + v_ev + mask_ev,
                                ((mm, 1),), f"at{layer}:s{s}.{h}.{qc}.{kt}")
                            q_consumers.append((mm, mm.n))
                        sc_mm = (mm, mm.n)
                        diag = (b == nb - 1)
                        isf, s4, rels = s4r.take()
                        iu, u4, relu = u4r.take()
                        ia, a4, rela = a4r.take()
                        if layer == 1:
                            rela_by_slot[ia] = rela
                        if layer == 0:
                            ee = None
                            for j in range(4):
                                ee = P.op("scalar", (lambda p=psf[j],
                                    d=a4[:, j * 512:(j + 1) * 512]: (lambda e:
                                    e.activation(d, p[:], AF.Exp, bias=0.0,
                                    scale=0.125)))(),
                                    ([sc_mm] + rela + rels + relu) if j == 0
                                    else [sc_mm], ((ac, 1),),
                                    f"at0:e{s}.{h}.{qc}.{b}.{j}")
                                bank_ev[j] = [ee[0]]
                            aev = ee[0]
                            if diag:
                                em = P.op("gpsimd", (lambda t=a4: (lambda e:
                                    e.affine_select(t[:, :], t[:, :],
                                    pattern=[[-128, 4], [1, 512]],
                                    compare_op=ALU.is_ge, fill=0.0, base=0,
                                    channel_multiplier=-1)))(),
                                    [aev], ((su, 1),),
                                    f"at0:m{s}.{h}.{qc}.{b}")
                                aev = em[0]
                            s4r.release(isf, [aev])
                            u4r.release(iu, [aev])
                            batch_state.append((a4, ia, aev, b))
                        else:
                            eev = None
                            for j in range(4):
                                msl = (mask4[:, j * 512:(j + 1) * 512] if diag
                                       else None)
                                if diag:
                                    eev = P.op("vector", (lambda p=psf[j],
                                        d=s4[:, j * 512:(j + 1) * 512], m_=msl:
                                        (lambda e: e.tensor_tensor(d, p[:], m_,
                                        ALU.min)))(),
                                        ([sc_mm] + rels) if j == 0 else [sc_mm],
                                        ((dv, 1),),
                                        f"at1:mn{s}.{h}.{qc}.{b}.{j}")
                                else:
                                    eev = P.op("vector", (lambda p=psf[j],
                                        d=s4[:, j * 512:(j + 1) * 512]:
                                        (lambda e: e.tensor_copy(d, p[:])))(),
                                        ([sc_mm] + rels) if j == 0 else [sc_mm],
                                        ((dv, 1),),
                                        f"at1:cp{s}.{h}.{qc}.{b}.{j}")
                                bank_ev[j] = [eev[0]]
                            s_ev = eev[0]
                            eu = P.op("scalar", (lambda a=s4, d=u4: (lambda e:
                                e.activation(d[:, :], a[:, :], AF.Square,
                                bias=0.0, scale=1.0)))(), [s_ev] + relu,
                                ((ac, 1),), f"at1:sq{s}.{h}.{qc}.{b}")
                            el = P.op("scalar", (lambda d=u4: (lambda e:
                                e.activation(d[:, :], d[:, :], AF.Ln, bias=1.0,
                                scale=1.0)))(), [eu[0]], ((ac, 1),),
                                f"at1:ln{s}.{h}.{qc}.{b}")
                            er = P.op("scalar", (lambda d=u4: (lambda e:
                                e.activation(d[:, :], d[:, :], AF.Exp,
                                bias=0.0, scale=-0.5)))(), [el[0]], ((ac, 1),),
                                f"at1:ex{s}.{h}.{qc}.{b}")
                            et = P.op("vector", (lambda a=s4, b_=u4: (lambda e:
                                e.tensor_tensor(b_[:, :], a[:, :], b_[:, :],
                                ALU.mult)))(), [er[0]], ((dv, 1),),
                                f"at1:t{s}.{h}.{qc}.{b}")
                            s4r.release(isf, [et[0]])
                            batch_state.append((a4, ia, u4, iu, et[0], b, diag))

                    if batch_state and (b == nb or len(batch_state) > 1):
                        stt = batch_state.pop(0)
                        if layer == 0:
                            a4p, iap, evp, bp = stt
                        else:
                            a4p, iap, u4p, iup, etp, bp, diagp = stt
                            ep = P.op("scalar", (lambda a=u4p, d=a4p:
                                (lambda e: e.activation(d[:, :], a[:, :],
                                AF.Relu, bias=1.0, scale=1.0)))(),
                                [etp] + rela_by_slot.get(iap, []), ((ac, 1),),
                                f"at1:p{s}.{h}.{qc}.{bp}")
                            u4r.release(iup, [ep[0]])
                            evp = ep[0]
                            if diagp:
                                em = P.op("gpsimd", (lambda t=a4p: (lambda e:
                                    e.affine_select(t[:, :], t[:, :],
                                    pattern=[[-128, 4], [1, 512]],
                                    compare_op=ALU.is_ge, fill=0.0, base=0,
                                    channel_multiplier=-1)))(),
                                    [evp], ((su, 1),),
                                    f"at1:z{s}.{h}.{qc}.{bp}")
                                evp = em[0]
                        emv = None
                        last_b = (bp == nb - 1) and (layer == 0 or kmax == 16)
                        for j in range(4):
                            kt = 4 * bp + j
                            emv = P.op("tensor", (lambda
                                l=v_tile(s, kt, h),
                                r=a4p[:, j * 512:(j + 1) * 512], p=otp,
                                st_=(kt == 0), sp=(last_b and j == 3):
                                (lambda e: e.matmul(
                                p[0:65, :], l, r, start=st_, stop=sp,
                                skip_group_check=True)))(),
                                ([evp] + (bank_ev[ot_bank] if kt == 0 else [])
                                 + v_ev), ((mm, 1),),
                                f"at{layer}:av{s}.{h}.{qc}.{kt}")
                        a4r.release(iap, [emv[0]])
                if layer == 1:
                    for kt in range(kmax, 16):
                        P.op("tensor", (lambda l=v_tile(s, kt, h), p=otp,
                            sp=(kt == 15): (lambda e: e.matmul(
                            p[0:65, :], l, atn_neg[:, :],
                            start=False, stop=sp,
                            skip_group_check=True)))(),
                            neg_ready + v_ev, ((mm, 1),),
                            f"at1:avm{s}.{h}.{qc}.{kt}")
                ot_mm = (mm, mm.n)
                e1 = P.op("scalar", (lambda p=otp: (lambda e: e.activation(
                    den[0:1, 0:512], p[64:65, 0:512], AF.Copy, bias=0.0,
                    scale=1.0)))(), [ot_mm, (dv, dv.n)], ((ac, 1),),
                    f"at{layer}:dn{s}.{h}.{qc}")
                if layer == 1:
                    e1 = P.op("vector", lambda e: e.tensor_scalar(
                        den[0:1, 0:512], den[0:1, 0:512], 2e-6, None, ALU.add),
                        [e1[0]], ((dv, 1),), f"at1:eps{s}.{h}.{qc}")
                e2 = P.op("vector", lambda e: e.reciprocal(den[0:1, 512:1024],
                    den[0:1, 0:512]), [e1[0]], ((dv, 1),),
                    f"at{layer}:rc{s}.{h}.{qc}")
                e3 = P.op("scalar", lambda e: e.activation(rec2[0:1, 0:512],
                    den[0:1, 512:1024], AF.Copy, bias=0.0, scale=1.0),
                    [e2[0]] + bank_ev[2], ((ac, 1),), f"at{layer}:rh{s}.{h}.{qc}")
                e4 = P.op("scalar", lambda e: e.activation(den[0:1, 1024:1536],
                    rec2[0:1, 0:512], AF.Copy, bias=0.0, scale=1.0),
                    [e3[0]], ((ac, 1),), f"at{layer}:rhf{s}.{h}.{qc}")
                e5 = P.op("vector", lambda e: e.tensor_tensor(rec2[0:1,
                    512:1024], den[0:1, 512:1024], den[0:1, 1024:1536],
                    ALU.subtract), [e4[0]], ((dv, 1),),
                    f"at{layer}:rl{s}.{h}.{qc}")
                rb = psf[2]
                P.op("tensor", lambda e: e.matmul(rb[0:64, 0:512],
                    ones_bb[0:1, 0:64], rec2[0:1, 0:512], start=True,
                    stop=False, skip_group_check=True),
                    [e3[0]] + bank_ev[2], ((mm, 1),), f"at{layer}:b1{s}.{h}.{qc}")
                P.op("tensor", lambda e: e.matmul(rb[0:64, 0:512],
                    ones_bb[0:1, 0:64], rec2[0:1, 512:1024], start=False,
                    stop=True, skip_group_check=True),
                    [e5[0]], ((mm, 1),), f"at{layer}:b2{s}.{h}.{qc}")
                bmv = (mm, mm.n)
                e6 = P.op("scalar", lambda e: e.activation(recS[0:64, 0:512],
                    rb[0:64, 0:512], AF.Copy, bias=0.0, scale=1.0),
                    [bmv, (dv, dv.n)], ((ac, 1),), f"at{layer}:re{s}.{h}.{qc}")
                bank_ev[2] = [e6[0]]
                dstp = attn_fm[64 * (h % 2):64 * (h % 2) + 64,
                               (h // 2) * 2048 + qc * 512:
                               (h // 2) * 2048 + qc * 512 + 512]
                e7 = P.op("vector", (lambda p=otp, d=dstp: (lambda e:
                    e.tensor_tensor(d, p[0:64, 0:512], recS[0:64, 0:512],
                    ALU.mult)))(), [e6[0]] + fm_guard, ((dv, 1),),
                    f"at{layer}:o{s}.{h}.{qc}")
                fm_guard = []
                bank_ev[ot_bank] = [e7[0]]
            qg.release(iq, [q_consumers[-1]])
            kg.release(ik, [q_consumers[-1]])
        fm_ev = [(dv, dv.n)]
        for oc in range(8):
            for tc2 in range(4):
                bank = 4 + (oc * 4 + tc2) % 2
                pb = psf[bank]
                for ic in range(DCH):
                    waits = []
                    if ic == 0:
                        waits = w_ready + fm_ev + bank_ev[bank]
                    P.op("tensor", (lambda l=woT[:, ic * 1024 + oc * 128:
                        ic * 1024 + (oc + 1) * 128],
                        r=attn_fm[:, ic * 2048 + tc2 * 512:
                                  ic * 2048 + tc2 * 512 + 512],
                        p=pb, st_=(ic == 0), sp=(ic == DCH - 1):
                        (lambda e: e.matmul(p[:], l, r, start=st_, stop=sp,
                         skip_group_check=True)))(), waits,
                        ((mm, 1),) if ic == DCH - 1 else (),
                        f"at{layer}:wo{s}.{oc}.{tc2}")
                wo_mm = (mm, mm.n)
                iw, wbuf, relw = sev.take()
                if layer == 0:
                    fn = (lambda p=pb, d=wbuf, o=oc: (lambda e: e.activation(
                        d[:, :], p[:], AF.Identity, bias=bo_pp[:, o:o + 1],
                        scale=1.0)))()
                else:
                    fn = (lambda p=pb, d=wbuf: (lambda e: e.activation(
                        d[:, :], p[:], AF.Copy, bias=0.0, scale=1.0)))()
                ew = P.op("scalar", fn, [wo_mm] + relw, ((ac, 1),),
                          f"at{layer}:we{s}.{oc}.{tc2}")
                bank_ev[bank] = [ew[0]]
                ssem = s_sems[iw % 2]
                dd = P.dma("scalar", ssc[oc][:, s * 2048 + tc2 * 512:
                           s * 2048 + tc2 * 512 + 512], wbuf[:, :], [ew[0]],
                           f"at{layer}:ws{s}.{oc}.{tc2}", ssem)
                sev.release(iw, [(ssem, ssem.n)])
    P.op("sync", None, [(s_sems[0], s_sems[0].n), (s_sems[1], s_sems[1].n),
         (P.sem("dma_scalar"), P.sem("dma_scalar").n)], (), f"at{layer}:fin")
    P.barrier(f"at{layer}")


def emit_moe0(P, C):
    nc = P.nc
    bf = C["bf"]; f32 = C["f32"]
    IN = C["in"]; SC = C["sc"]
    w1 = IN["L0_ffn_w1"]; w2 = IN["L0_ffn_w2"]
    ld = P.sem("moe_lds"); mm = P.sem("moe_mm")
    cw_sem = P.sem("moe_cw")
    w1_sems = [P.sem("moe_w10"), P.sem("moe_w11")]
    w2_sems = [P.sem("moe_w20"), P.sem("moe_w21")]
    ac = P.sem("act"); dv = P.sem("dve"); st = P.sem("moe_st")
    su = P.sem("setup"); dsy = P.sem("dma_sync"); dgp = P.sem("dma_gpsimd")
    dsc = P.sem("dma_scalar")
    psf = C["ps_f32"]
    bank_ev = C.setdefault("bank_ev", [[] for _ in range(6)])

    xn_sb = Ring([bf.alloc(8192, "mxn0")])
    w1r = Ring([bf.alloc(8192, f"mw1{i}") for i in range(2)])
    w2r = Ring([bf.alloc(4096, f"mw2{i}") for i in range(2)])
    h_sb = bf.alloc(32768, "h_sb")
    y_acc = f32.alloc(8192, "y_acc")
    cwB = Ring([f32.alloc(1024, "cwB0")])
    gtmp = f32.alloc(1024, "gtmp")
    ytmp = gtmp[:, 0:512]
    stmp = f32.alloc(1024, "stmp")
    xtmp = f32.alloc(1024, "xtmp")

    xn0 = SC["xn0"]
    for ts in range(4):
        tsl = slice(ts * 1024, (ts + 1) * 1024)
        ixn, xn_ts, relx = xn_sb.take()
        for c in range(DCH):
            P.dma("sync", xn_ts[:, c * 1024:(c + 1) * 1024],
                  xn0[c][:, tsl], relx if c == 0 else (), f"m0:xn{ts}.{c}", ld)
        xn_ev = [(ld, ld.n)]
        cw_done = []
        ez = P.op("gpsimd", lambda e: e.memset(y_acc[:, :], 0.0),
                  [(dv, dv.n), (ac, ac.n)], ((su, 1),), f"m0:z{ts}")
        acc_ev = [(su, su.n)]
        xn_consumers = []
        for ex in range(E0):
            icb, cwb, relc = cwB.take()
            P.dma("gpsimd", cwb[:, :],
                  SC["cw0"][ex:ex + 1, tsl].partition_broadcast(128),
                  cw_done + relc, f"m0:cwb{ts}.{ex}", cw_sem)
            cw_ev = [(cw_sem, cw_sem.n)]
            h_done = []
            for hb in range(4):
                iw, w1b, rel1 = w1r.take()
                src = w1[ex][:, hb * 1024:(hb + 1) * 1024].rearrange(
                    "(ic p) c -> p ic c", p=128)
                w1s = w1_sems[iw % 2]
                P.dma("gpsimd", w1b[:, :].rearrange("p (ic c) -> p ic c", ic=8),
                      src, rel1, f"m0:w1{ts}.{ex}.{hb}", w1s)
                w1_ev = (w1s, w1s.n)
                for ht in range(8):
                    for tcl in range(2):
                        bank = (ht * 2 + tcl) % 2
                        pb = psf[bank]
                        for ic in range(DCH):
                            P.op("tensor", (lambda l=w1b[:, ic * 1024 + ht * 128:
                                ic * 1024 + ht * 128 + 128],
                                r=xn_ts[:, ic * 1024 + tcl * 512:
                                        ic * 1024 + tcl * 512 + 512],
                                p=pb, st_=(ic == 0), sp=(ic == DCH - 1):
                                (lambda e: e.matmul(p[:], l, r, start=st_,
                                 stop=sp, skip_group_check=True)))(),
                                ([w1_ev] + xn_ev + bank_ev[bank]) if ic == 0 else (),
                                ((mm, 1),) if ic == DCH - 1 else (),
                                f"m0:h{ts}.{ex}.{hb}.{ht}.{tcl}")
                        h_mm = (mm, mm.n)
                        if ic == DCH - 1:
                            xn_consumers.append(h_mm)
                        dst = h_sb[:, (hb * 8 + ht) * 1024 + tcl * 512:
                                   (hb * 8 + ht) * 1024 + tcl * 512 + 512]
                        eg = P.op("scalar", (lambda p=pb, d=dst: (lambda e:
                            e.activation(d, p[:], AF.Gelu, bias=0.0,
                            scale=1.0)))(), [h_mm], ((ac, 1),),
                            f"m0:g{ts}.{ex}.{hb}.{ht}.{tcl}")
                        bank_ev[bank] = [eg[0]]
                        h_done.append(eg[0])
                w1r.release(iw, [h_done[-1]])
            h_ev = [(ac, ac.n)]
            if DEBUG_MOE == 'h' and ts == 0 and ex == 0:
                for hcd in range(32):
                    P.dma("scalar", SC["dbg"][hcd],
                          h_sb[:, hcd * 1024:(hcd + 1) * 1024], h_ev,
                          f"dbg{hcd}", st)
            for dt in range(8):
                iw2, w2b, rel2 = w2r.take()
                src = w2[ex][:, dt * 128:(dt + 1) * 128].rearrange(
                    "(hc p) c -> p hc c", p=128)
                w2s = w2_sems[iw2 % 2]
                P.dma("gpsimd", w2b[:, :].rearrange("p (hc c) -> p hc c", hc=32),
                      src, rel2, f"m0:w2{ts}.{ex}.{dt}", w2s)
                w2_ev = (w2s, w2s.n)
                y_mms = []
                for tcl in range(2):
                    bank = 2 + (dt * 2 + tcl) % 2
                    pb = psf[bank]
                    for hc in range(32):
                        P.op("tensor", (lambda l=w2b[:, hc * 128:(hc + 1) * 128],
                            r=h_sb[:, hc * 1024 + tcl * 512:
                                   hc * 1024 + tcl * 512 + 512],
                            p=pb, st_=(hc == 0), sp=(hc == 31): (lambda e:
                            e.matmul(p[:], l, r, start=st_, stop=sp,
                             skip_group_check=True)))(),
                            ([w2_ev] + h_ev + bank_ev[bank]) if hc == 0 else (),
                            ((mm, 1),) if hc == 31 else (),
                            f"m0:y{ts}.{ex}.{dt}.{tcl}")
                    y_mm = (mm, mm.n)
                    y_mms.append(y_mm)
                    et = P.op("vector", (lambda p=pb,
                        c_=cwb[:, tcl * 512:(tcl + 1) * 512]: (lambda e:
                        e.tensor_tensor(ytmp, p[:], c_, ALU.mult)))(),
                        [y_mm, (st, st.n)] + cw_ev + acc_ev + [(dv, dv.n)],
                        ((dv, 1),), f"m0:ym{ts}.{ex}.{dt}.{tcl}")
                    ea = P.op("vector", (lambda d=y_acc[:, dt * 1024 + tcl * 512:
                        dt * 1024 + tcl * 512 + 512]: (lambda e:
                        e.tensor_tensor(d, d, ytmp, ALU.add)))(),
                        [(dv, dv.n)], ((dv, 1),), f"m0:ya{ts}.{ex}.{dt}.{tcl}")
                    bank_ev[bank] = [ea[0]]
                w2r.release(iw2, y_mms)
            cwB.release(icb, [(P.sems["dve"], P.sems["dve"].n)])
            if DEBUG_MOE and ts == 0 and ex < 4:
                for dtd in range(8):
                    P.dma("scalar", SC["dbg"][ex * 8 + dtd],
                          y_acc[:, dtd * 1024:(dtd + 1) * 1024],
                          [(dv, dv.n)], f"dbgy{ex}.{dtd}", st)
        xn_sb.release(ixn, xn_consumers[-8:])
        for dt in range(8):
            esl = P.op("scalar", (lambda s_=y_acc[:, dt * 1024:(dt + 1) * 1024]:
                (lambda e: e.activation(gtmp[:, :], s_, AF.Silu, bias=0.0,
                scale=1.0)))(), [(dv, dv.n), (st, st.n)], ((ac, 1),),
                f"m0:si{ts}.{dt}")
            cmb = P.sem("moe_cmb")
            P.dma("sync", stmp[:, :], SC["s0"][dt][:, tsl], [(dv, dv.n)],
                  f"m0:ls{ts}.{dt}", cmb)
            P.dma("sync", xtmp[:, :], SC["x0T"][dt][:, tsl], [(dv, dv.n)],
                  f"m0:lx{ts}.{dt}", cmb)
            lev = (cmb, cmb.n)
            em = P.op("vector", lambda e: e.tensor_tensor(gtmp[:, :], gtmp[:, :],
                stmp[:, :], ALU.mult), [esl[0], lev], ((dv, 1),),
                f"m0:sg{ts}.{dt}")
            ex2 = P.op("vector", lambda e: e.tensor_tensor(gtmp[:, :], gtmp[:, :],
                xtmp[:, :], ALU.add), [(dv, dv.n)], ((dv, 1),),
                f"m0:x1{ts}.{dt}")
            P.dma("scalar", SC["x1T"][dt][:, tsl], gtmp[:, :], [ex2[0]],
                  f"m0:so{ts}.{dt}", st)
    P.op("sync", None, [(st, st.n), (dsc, dsc.n)], (), "m0:fin")
    P.barrier("moe0")


def emit_p3_attn0(P, C):
    emit_attn(P, C, 0)


def emit_p7_attn1(P, C):
    emit_attn(P, C, 1)


def emit_p2_qkv0(P, C):
    emit_qkv(P, C, 0)


def emit_p6_qkv1(P, C):
    emit_qkv(P, C, 1)


def emit_ln1(P, C):
    nc = P.nc
    bf = C["bf"]; f32 = C["f32"]
    IN = C["in"]; SC = C["sc"]
    ld = P.sem("l1_ld"); ldg = P.sem("l1_ldg"); mm = P.sem("l1_mm")
    ac = P.sem("act"); dv = P.sem("dve"); st = P.sem("l1_st")
    su = P.sem("setup"); dsy = P.sem("dma_sync")
    psf = C["ps_f32"]
    bank_ev = C.setdefault("bank_ev", [[] for _ in range(6)])
    idf = C["ident_f32"]

    ones_col = bf.alloc(1, "ones_col")
    P.op("gpsimd", lambda e: e.memset(ones_col[:, 0:1], 1.0), (), ((su, 1),),
         "l1:ones")
    g_pp = f32.alloc(8, "g_pp"); b_pp = f32.alloc(8, "b_pp")
    per_partition_bias(P, C, IN["L1_ffn_ln_g"], 1024, g_pp, "lng")
    per_partition_bias(P, C, IN["L1_ffn_ln_b"], 1024, b_pp, "lnb")
    gw_sb = f32.alloc(48, "gw")
    for c in range(DCH):
        P.dma("sync", gw_sb[:, c * 6:(c + 1) * 6],
              IN["L1_ffn_gate_w"][c * 128:(c + 1) * 128, :], (), f"l1:gw{c}")
    gbB = f32.alloc(6, "gbB")
    broadcast_row(P, C, IN["L1_ffn_gate_b"], 6, gbB, "gbB")
    w_ev = [(dsy, dsy.n), (su, su.n)]

    x1c = f32.alloc(4096, "x1c")
    xb = bf.alloc(4096, "xb")
    sq = bf.alloc(512, "sq")
    rows = f32.alloc(4 * 512, "rows")
    z2c = f32.alloc(4096, "z2c")
    bcA = f32.alloc(512, "bcA")
    bcB = f32.alloc(512, "bcB")
    rt_scr = f32.alloc(24, "g_scr")
    rt_cols = f32.alloc(8, "g_cols")
    cwT_sb = f32.alloc(512 * 2, "cw1T")
    lnrows = C["sc"]["lnrows"]

    def rowslice(i):
        return rows[0:1, i * 512:(i + 1) * 512]

    for tc in range(8):
        tsl = slice(tc * 512, (tc + 1) * 512)
        for c in range(DCH):
            P.dma("sync", x1c[:, c * 512:(c + 1) * 512],
                  SC["x1T"][c][:, tsl],
                  [(mm, mm.n), (ac, ac.n), (dv, dv.n), (st, st.n)] if c == 0 else (),
                  f"l1:x{tc}.{c}", ld)
        lev = [(ld, ld.n)]
        for phase in range(2):
            for c in range(DCH):
                P.op("vector", (lambda cc=c: (lambda e: e.tensor_copy(
                    xb[:, cc * 512:(cc + 1) * 512],
                    x1c[:, cc * 512:(cc + 1) * 512])))(),
                    (lev + [(mm, mm.n), (st, st.n)]) if c == 0 else (), ((dv, 1),),
                    f"l1:cb{phase}.{tc}.{c}")
            cb_ev = [(dv, dv.n)]
            pb = psf[4]
            for c in range(DCH):
                P.op("tensor", (lambda cc=c: (lambda e: e.matmul(
                    pb[0:1, 0:512], ones_col[:, 0:1],
                    xb[:, cc * 512:(cc + 1) * 512], start=(cc == 0),
                    stop=(cc == DCH - 1), skip_group_check=True)))(),
                    (cb_ev + w_ev + bank_ev[4]) if c == 0 else (),
                    ((mm, 1),) if c == DCH - 1 else (), f"l1:sx{phase}.{tc}")
            sx_mm = (mm, mm.n)
            pb2 = psf[5]
            for c in range(DCH):
                eq = P.op("vector", (lambda cc=c: (lambda e: e.tensor_tensor(
                    sq[:, :], xb[:, cc * 512:(cc + 1) * 512],
                    xb[:, cc * 512:(cc + 1) * 512], ALU.mult)))(),
                    [(mm, mm.n), (dv, dv.n)], ((dv, 1),),
                    f"l1:sq{phase}.{tc}.{c}")
                P.op("tensor", (lambda cc=c: (lambda e: e.matmul(
                    pb2[0:1, 0:512], ones_col[:, 0:1], sq[:, :],
                    start=(cc == 0), stop=(cc == DCH - 1),
                    skip_group_check=True)))(),
                    ([eq[0]] + bank_ev[5]) if c == 0 else [eq[0]],
                    ((mm, 1),), f"l1:s2{phase}.{tc}.{c}")
            s2_mm = (mm, mm.n)
            e1 = P.op("scalar", lambda e: e.activation(rowslice(0),
                pb[0:1, 0:512], AF.Copy, bias=0.0, scale=1.0 / D),
                [sx_mm, (dv, dv.n)], ((ac, 1),), f"l1:mu{phase}.{tc}")
            bank_ev[4] = [e1[0]]
            e2 = P.op("scalar", lambda e: e.activation(rowslice(1),
                pb2[0:1, 0:512], AF.Copy, bias=0.0, scale=1.0 / D),
                [s2_mm, (ac, ac.n)], ((ac, 1),), f"l1:m2{phase}.{tc}")
            bank_ev[5] = [e2[0]]
            P.op("vector", lambda e: e.scalar_tensor_tensor(rowslice(2),
                rowslice(0), 1.0, rowslice(0), ALU.mult, ALU.mult),
                [e2[0], (st, st.n)], ((dv, 1),), f"l1:v0{phase}.{tc}")
            P.op("vector", lambda e: e.tensor_tensor(rowslice(2), rowslice(1),
                rowslice(2), ALU.subtract), [(dv, dv.n)], ((dv, 1),),
                f"l1:v1{phase}.{tc}")
            P.op("vector", lambda e: e.tensor_scalar(rowslice(2), rowslice(2),
                float(LN_EPS), None, ALU.add), [(dv, dv.n)], ((dv, 1),),
                f"l1:v2{phase}.{tc}")
            e3 = P.op("scalar", lambda e: e.activation(rowslice(2), rowslice(2),
                AF.Ln, bias=0.0, scale=1.0), [(dv, dv.n)], ((ac, 1),),
                f"l1:v3{phase}.{tc}")
            e4 = P.op("scalar", lambda e: e.activation(rowslice(2), rowslice(2),
                AF.Exp, bias=0.0, scale=-0.5), [e3[0]], ((ac, 1),),
                f"l1:v4{phase}.{tc}")
            e5 = P.op("vector", lambda e: e.scalar_tensor_tensor(rowslice(3),
                rowslice(0), -1.0, rowslice(2), ALU.mult, ALU.mult),
                [e4[0]], ((dv, 1),), f"l1:v5{phase}.{tc}")
            d1 = P.dma("scalar", lnrows[0:1, tsl], rowslice(2), [e5[0]],
                       f"l1:r2{phase}.{tc}", st)
            d2 = P.dma("scalar", lnrows[1:2, tsl], rowslice(3), [e5[0]],
                       f"l1:r3{phase}.{tc}", st)
            b1 = P.dma("gpsimd", bcA[:, :],
                       lnrows[0:1, tsl].partition_broadcast(128),
                       [d2[0], (dv, dv.n)], f"l1:bA{phase}.{tc}", ldg)
            b2 = P.dma("gpsimd", bcB[:, :],
                       lnrows[1:2, tsl].partition_broadcast(128),
                       [d2[0], (dv, dv.n)], f"l1:bB{phase}.{tc}", ldg)
            bev = [(ldg, ldg.n)]
            zdst = x1c if phase == 0 else z2c
            for c in range(DCH):
                P.op("vector", (lambda cc=c, zz=zdst: (lambda e:
                    e.tensor_tensor(zz[:, cc * 512:(cc + 1) * 512],
                    x1c[:, cc * 512:(cc + 1) * 512], bcA[:, :], ALU.mult)))(),
                    bev if c == 0 else (), ((dv, 1),), f"l1:n0{phase}.{tc}.{c}")
                P.op("vector", (lambda cc=c, zz=zdst: (lambda e:
                    e.tensor_tensor(zz[:, cc * 512:(cc + 1) * 512],
                    zz[:, cc * 512:(cc + 1) * 512], bcB[:, :], ALU.add)))(),
                    [(dv, dv.n)], ((dv, 1),), f"l1:n1{phase}.{tc}.{c}")
            nev = [(dv, dv.n)]
            if phase == 0:
                for c in range(DCH):
                    P.dma("scalar", SC["z1f"][c][:, tsl],
                          x1c[:, c * 512:(c + 1) * 512], nev, f"l1:zf{tc}.{c}",
                          st)
                for c in range(DCH):
                    eb = P.op("scalar", (lambda cc=c: (lambda e: e.activation(
                        xb[:, cc * 512:(cc + 1) * 512],
                        x1c[:, cc * 512:(cc + 1) * 512], AF.Copy, bias=0.0,
                        scale=1.0)))(), nev + [(mm, mm.n)], ((ac, 1),),
                        f"l1:zb{tc}.{c}")
                    P.dma("scalar", SC["zn1"][c][:, tsl],
                          xb[:, c * 512:(c + 1) * 512], [eb[0]],
                          f"l1:zn{tc}.{c}", st)
            else:
                for c in range(DCH):
                    ea2 = P.op("vector", (lambda cc=c: (lambda e:
                        e.tensor_scalar(z2c[:, cc * 512:(cc + 1) * 512],
                        z2c[:, cc * 512:(cc + 1) * 512], g_pp[:, cc:cc + 1],
                        b_pp[:, cc:cc + 1], ALU.mult, ALU.add)))(),
                        [(dv, dv.n)], ((dv, 1),), f"l1:af{tc}.{c}")
                    eb2 = P.op("scalar", (lambda cc=c: (lambda e: e.activation(
                        xb[:, cc * 512:(cc + 1) * 512],
                        z2c[:, cc * 512:(cc + 1) * 512], AF.Copy, bias=0.0,
                        scale=1.0)))(), [ea2[0]], ((ac, 1),), f"l1:ab{tc}.{c}")
                    P.dma("scalar", SC["zn2"][c][:, tsl],
                          xb[:, c * 512:(c + 1) * 512], [eb2[0]],
                          f"l1:z2{tc}.{c}", st)
        for q in range(4):
            pb = psf[4]
            for c in range(DCH):
                P.op("tensor", (lambda cc=c, qq=q: (lambda e: e.matmul(
                    pb[0:128, 0:6], z2c[:, cc * 512 + qq * 128:
                                        cc * 512 + qq * 128 + 128],
                    gw_sb[:, cc * 6:(cc + 1) * 6], start=(cc == 0),
                    stop=(cc == DCH - 1), skip_group_check=True)))(),
                    ([(dv, dv.n), (ac, ac.n)] + w_ev + bank_ev[4]) if c == 0 else (),
                    ((mm, 1),) if c == DCH - 1 else (), f"l1:gt{tc}.{q}")
            gt_mm = (mm, mm.n)
            cw_ap, cw_ev = emit_cw_gate_biased(P, C, pb, gt_mm, rt_scr, rt_cols,
                                               gbB, f"l1g{tc}.{q}")
            bank_ev[4] = [cw_ev]
            P.op("tensor", (lambda: (lambda e: e.transpose(psf[5][0:6, 0:128],
                cw_ap, idf[:])))(), [cw_ev] + bank_ev[5], ((mm, 1),),
                f"l1:ct{tc}.{q}")
            e9 = P.op("vector", (lambda qq=q: (lambda e: e.tensor_copy(
                cwT_sb[0:6, qq * 128:(qq + 1) * 128], psf[5][0:6, 0:128])))(),
                [(mm, mm.n), (st, st.n)], ((dv, 1),), f"l1:ce{tc}.{q}")
            bank_ev[5] = [e9[0]]
        P.dma("scalar", C["sc"]["cw1"][:, tsl], cwT_sb[0:6, 0:512],
              [(P.sems["dve"], P.sems["dve"].n)], f"l1:cs{tc}", st)
    P.op("sync", None, [(st, st.n), (P.sem("dma_scalar"), P.sem("dma_scalar").n)],
         (), "l1:fin")
    P.barrier("ln1")


def emit_cw_gate_biased(P, C, pb, mm_ev, scr, cols, gbB, tag):
    ac = P.sem("act"); dv = P.sem("dve")
    ne = 6
    ll = scr[:, 0:ne]; eqt = scr[:, 8:8 + ne]; cw = scr[:, 16:16 + ne]
    c_m1 = cols[:, 1:2]; c_m2 = cols[:, 2:3]
    c_d = cols[:, 3:4]; c_r = cols[:, 4:5]; c_nm = cols[:, 5:6]
    e0 = P.op("vector", (lambda p=pb: (lambda e: e.tensor_tensor(ll,
        p[0:128, 0:ne], gbB[:, 0:ne], ALU.add)))(),
        [mm_ev, (dv, dv.n), (ac, ac.n)], ((dv, 1),), tag + ":b")
    P.op("vector", lambda e: e.reduce_max(c_m1, ll, axis=AX.X),
         [e0[0]], ((dv, 1),), tag + ":m1")
    P.op("vector", lambda e: e.tensor_scalar(eqt, ll, c_m1, None, ALU.is_ge),
         [(dv, dv.n)], ((dv, 1),), tag + ":eq")
    P.op("vector", lambda e: e.tensor_scalar(eqt, eqt, -1e9, None, ALU.mult),
         [(dv, dv.n)], ((dv, 1),), tag + ":sc")
    P.op("vector", lambda e: e.tensor_tensor(eqt, eqt, ll, ALU.add),
         [(dv, dv.n)], ((dv, 1),), tag + ":lm")
    P.op("vector", lambda e: e.reduce_max(c_m2, eqt, axis=AX.X),
         [(dv, dv.n)], ((dv, 1),), tag + ":m2")
    P.op("vector", lambda e: e.tensor_scalar(c_nm, c_m1, -1.0, None, ALU.mult),
         [(dv, dv.n)], ((dv, 1),), tag + ":nm")
    P.op("vector", lambda e: e.tensor_tensor(c_d, c_m2, c_nm, ALU.add),
         [(dv, dv.n)], ((dv, 1),), tag + ":dm")
    e2 = P.op("scalar", lambda e: e.activation(c_d, c_d, AF.Exp, bias=0.0,
         scale=1.0), [(dv, dv.n)], ((ac, 1),), tag + ":ex")
    P.op("vector", lambda e: e.tensor_scalar(c_d, c_d, 1.0, None, ALU.add),
         [e2[0]], ((dv, 1),), tag + ":p1")
    P.op("vector", lambda e: e.reciprocal(c_r, c_d), [(dv, dv.n)],
         ((dv, 1),), tag + ":rc")
    P.op("vector", lambda e: e.tensor_scalar(eqt, ll, c_m2, None, ALU.is_ge),
         [(dv, dv.n)], ((dv, 1),), tag + ":sel")
    e3 = P.op("scalar", (lambda: (lambda e: e.activation(ll, ll, AF.Exp,
        bias=c_nm, scale=1.0)))(), [(dv, dv.n)], ((ac, 1),), tag + ":z")
    P.op("vector", lambda e: e.tensor_tensor(cw, eqt, ll, ALU.mult),
         [e3[0]], ((dv, 1),), tag + ":cwz")
    ecw = P.op("vector", lambda e: e.tensor_scalar(cw, cw, c_r, None, ALU.mult),
         [(dv, dv.n)], ((dv, 1),), tag + ":cw")
    return cw, ecw[0]


def _ffn1(P, C, n_exp, hid, w1_of, w2_of, b1_vecs, b2_vecs, cw_name, out_name,
          out_f32, tag):
    nc = P.nc
    bf = C["bf"]; f32 = C["f32"]
    IN = C["in"]; SC = C["sc"]
    ld = P.sem("f1_lds"); mm = P.sem("f1_mm")
    cw_sem = P.sem(tag + "cws")
    w1_sems = [P.sem(tag + "w1s0"), P.sem(tag + "w1s1")]
    w2_sems = [P.sem(tag + "w2s0"), P.sem(tag + "w2s1")]
    st_sems = [P.sem(tag + "ss0"), P.sem(tag + "ss1")]
    ac = P.sem("act"); dv = P.sem("dve"); st = P.sem("f1_st")
    su = P.sem("setup")
    psf = C["ps_f32"]
    bank_ev = C.setdefault("bank_ev", [[] for _ in range(6)])
    nhb = hid // 1024
    nhc = hid // 128

    b1_pp = f32.alloc(n_exp * nhc, "b1pp") if b1_vecs else None
    b2_pp = f32.alloc(n_exp * 8, "b2pp") if b2_vecs else None
    for e in range(n_exp):
        if b1_vecs:
            per_partition_bias(P, C, b1_vecs(e), hid,
                               b1_pp[:, e * nhc:(e + 1) * nhc], f"{tag}b1{e}")
        if b2_vecs:
            per_partition_bias(P, C, b2_vecs(e), 1024,
                               b2_pp[:, e * 8:(e + 1) * 8], f"{tag}b2{e}")
    dsy = P.sem("dma_sync")
    b_ev = [(dsy, dsy.n)]

    xn_sb = Ring([bf.alloc(8192, tag + "xn")])
    w1r = Ring([bf.alloc(8192, f"{tag}w1{i}") for i in range(2)])
    w2r = Ring([bf.alloc(4096, f"{tag}w2{i}") for i in range(2)])
    h_sb = bf.alloc(nhc * 1024, tag + "h")
    if n_exp > 1:
        y_acc = f32.alloc(8192, tag + "ya")
        cwB = Ring([f32.alloc(1024, tag + "cwB")])
        ytmp = f32.alloc(512, tag + "yt")
    ost = Ring([(bf if not out_f32 else f32).alloc(1024, f"{tag}os{i}")
                for i in range(2)])
    out_sc = SC[out_name]
    zn2 = SC["zn2"]

    for ts in range(4):
        tsl = slice(ts * 1024, (ts + 1) * 1024)
        ixn, xn_ts, relx = xn_sb.take()
        for c in range(DCH):
            P.dma("sync", xn_ts[:, c * 1024:(c + 1) * 1024],
                  zn2[c][:, tsl], relx if c == 0 else (), f"{tag}xn{ts}.{c}", ld)
        xn_ev = [(ld, ld.n)]
        if n_exp > 1:
            ez = P.op("gpsimd", lambda e: e.memset(y_acc[:, :], 0.0),
                      [(dv, dv.n), (ac, ac.n)], ((su, 1),), f"{tag}z{ts}")
            acc_ev = [(su, su.n)]
        xn_last = []
        for ex in range(n_exp):
            if n_exp > 1:
                icb, cwb, relc = cwB.take()
                P.dma("gpsimd", cwb[:, :],
                      SC[cw_name][ex:ex + 1, tsl].partition_broadcast(128),
                      relc, f"{tag}cwb{ts}.{ex}", cw_sem)
                cw_ev = [(cw_sem, cw_sem.n)]
            for hb in range(nhb):
                iw, w1b, rel1 = w1r.take()
                srcw = w1_of(ex)[:, hb * 1024:(hb + 1) * 1024].rearrange(
                    "(ic p) c -> p ic c", p=128)
                w1s = w1_sems[iw % 2]
                P.dma("gpsimd", w1b[:, :].rearrange("p (ic c) -> p ic c", ic=8),
                      srcw, rel1, f"{tag}w1{ts}.{ex}.{hb}", w1s)
                w1_ev = (w1s, w1s.n)
                h_done = []
                for ht in range(8):
                    for tcl in range(2):
                        bank = tcl
                        pb = psf[bank]
                        for ic in range(DCH):
                            P.op("tensor", (lambda l=w1b[:, ic * 1024 + ht * 128:
                                ic * 1024 + ht * 128 + 128],
                                r=xn_ts[:, ic * 1024 + tcl * 512:
                                        ic * 1024 + tcl * 512 + 512],
                                p=pb, st_=(ic == 0), sp=(ic == DCH - 1):
                                (lambda e: e.matmul(p[:], l, r, start=st_,
                                 stop=sp, skip_group_check=True)))(),
                                ([w1_ev] + xn_ev + bank_ev[bank] + b_ev)
                                if ic == 0 else (),
                                ((mm, 1),) if ic == DCH - 1 else (),
                                f"{tag}h{ts}.{ex}.{hb}.{ht}.{tcl}")
                        h_mm = (mm, mm.n)
                        xn_last.append(h_mm)
                        dst = h_sb[:, (hb * 8 + ht) * 1024 + tcl * 512:
                                   (hb * 8 + ht) * 1024 + tcl * 512 + 512]
                        hc_g = hb * 8 + ht
                        if b1_vecs:
                            fng = (lambda p=pb, d=dst, col=b1_pp[:,
                                ex * nhc + hc_g:ex * nhc + hc_g + 1]:
                                (lambda e: e.activation(d, p[:], AF.Gelu,
                                bias=col, scale=1.0)))()
                        else:
                            fng = (lambda p=pb, d=dst: (lambda e: e.activation(
                                d, p[:], AF.Gelu, bias=0.0, scale=1.0)))()
                        eg = P.op("scalar", fng, [h_mm], ((ac, 1),),
                                  f"{tag}g{ts}.{ex}.{hb}.{ht}.{tcl}")
                        bank_ev[bank] = [eg[0]]
                        h_done.append(eg[0])
                w1r.release(iw, [h_done[-1]])
            h_ev = [(ac, ac.n)]
            for dt in range(8):
                iw2, w2b, rel2 = w2r.take()
                srcw2 = w2_of(ex)[:, dt * 128:(dt + 1) * 128].rearrange(
                    "(hc p) c -> p hc c", p=128)
                w2s = w2_sems[iw2 % 2]
                P.dma("gpsimd",
                      w2b[:, 0:nhc * 128].rearrange("p (hc c) -> p hc c",
                                                    hc=nhc),
                      srcw2, rel2, f"{tag}w2{ts}.{ex}.{dt}", w2s)
                w2_ev = (w2s, w2s.n)
                y_mms = []
                for tcl in range(2):
                    bank = 2 + tcl
                    pb = psf[bank]
                    for hc in range(nhc):
                        P.op("tensor", (lambda l=w2b[:, hc * 128:(hc + 1) * 128],
                            r=h_sb[:, hc * 1024 + tcl * 512:
                                   hc * 1024 + tcl * 512 + 512],
                            p=pb, st_=(hc == 0), sp=(hc == nhc - 1): (lambda e:
                            e.matmul(p[:], l, r, start=st_, stop=sp,
                             skip_group_check=True)))(),
                            ([w2_ev] + h_ev + bank_ev[bank]) if hc == 0 else (),
                            ((mm, 1),) if hc == nhc - 1 else (),
                            f"{tag}y{ts}.{ex}.{dt}.{tcl}")
                    y_mm = (mm, mm.n)
                    y_mms.append(y_mm)
                    if n_exp > 1:
                        et = P.op("vector", (lambda p=pb,
                            c_=cwb[:, tcl * 512:(tcl + 1) * 512],
                            col=b2_pp[:, ex * 8 + dt:ex * 8 + dt + 1]:
                            (lambda e: e.scalar_tensor_tensor(ytmp[:, :], p[:],
                            col, c_, ALU.add, ALU.mult)))(),
                            [y_mm] + cw_ev + acc_ev + [(dv, dv.n)], ((dv, 1),),
                            f"{tag}ym{ts}.{ex}.{dt}.{tcl}")
                        ea = P.op("vector", (lambda d=y_acc[:,
                            dt * 1024 + tcl * 512:dt * 1024 + tcl * 512 + 512]:
                            (lambda e: e.tensor_tensor(d, d, ytmp[:, :],
                            ALU.add)))(), [(dv, dv.n)], ((dv, 1),),
                            f"{tag}ya{ts}.{ex}.{dt}.{tcl}")
                        bank_ev[bank] = [ea[0]]
                    else:
                        io, obuf, relo = ost.take()
                        col = b2_pp[:, dt:dt + 1] if b2_vecs else None
                        if col is not None:
                            fne = (lambda p=pb, d=obuf, cc=col, t2=tcl:
                                (lambda e: e.activation(
                                d[:, t2 * 512:(t2 + 1) * 512], p[:],
                                AF.Identity, bias=cc, scale=1.0)))()
                        else:
                            fne = (lambda p=pb, d=obuf, t2=tcl: (lambda e:
                                e.activation(d[:, t2 * 512:(t2 + 1) * 512],
                                p[:], AF.Copy, bias=0.0, scale=1.0)))()
                        ee = P.op("scalar", fne, [y_mm] + relo, ((ac, 1),),
                                  f"{tag}oe{ts}.{ex}.{dt}.{tcl}")
                        bank_ev[bank] = [ee[0]]
                        if tcl == 1:
                            ssem = st_sems[io % 2]
                            dd = P.dma("scalar", out_sc[dt][:, tsl], obuf[:, :],
                                       [ee[0], (ac, ac.n)], f"{tag}st{ts}.{dt}",
                                       ssem)
                            ost.release(io, [(ssem, ssem.n)])
                        else:
                            ost.release(io, [])
                            ost.i -= 1
                w2r.release(iw2, y_mms)
            if n_exp > 1:
                cwB.release(icb, [(P.sems["dve"], P.sems["dve"].n)])
        xn_sb.release(ixn, xn_last[-8:])
        if n_exp > 1:
            for dt in range(8):
                io, obuf, relo = ost.take()
                ec = P.op("scalar", (lambda s_=y_acc[:, dt * 1024:(dt + 1) * 1024],
                    d=obuf: (lambda e: e.activation(d[:, :], s_, AF.Copy,
                    bias=0.0, scale=1.0)))(), [(dv, dv.n)] + relo, ((ac, 1),),
                    f"{tag}oc{ts}.{dt}")
                ssem = st_sems[io % 2]
                dd = P.dma("scalar", out_sc[dt][:, tsl], obuf[:, :], [ec[0]],
                           f"{tag}st{ts}.{dt}", ssem)
                ost.release(io, [(ssem, ssem.n)])
    P.op("sync", None, [(st_sems[0], st_sems[0].n),
         (st_sems[1], st_sems[1].n)], (), tag + "fin")
    P.barrier(tag)


def emit_abel(P, C):
    IN = C["in"]
    _ffn1(P, C, 1, A4, lambda e: IN["L1_ffn_aw1"], lambda e: IN["L1_ffn_aw2"],
          lambda e: IN["L1_ffn_ab1"], lambda e: IN["L1_ffn_ab2"], None, "abel",
          True, "ab:")


def emit_moe1(P, C):
    IN = C["in"]
    _ffn1(P, C, E1, H1FF, lambda e: IN["L1_ffn_ew1"][e],
          lambda e: IN["L1_ffn_ew2"][e], lambda e: IN["L1_ffn_eb1"][e],
          lambda e: IN["L1_ffn_eb2"][e], "cw1", "moe1", False, "m1:")


def emit_expm(P, C):
    nc = P.nc
    bf = C["bf"]; f32 = C["f32"]
    IN = C["in"]; SC = C["sc"]
    ld = P.sem("ex_ld"); mm = P.sem("ex_mm")
    ac = P.sem("act"); dv = P.sem("dve"); st = P.sem("ex_st")
    su = P.sem("setup")
    psf = C["ps_f32"]; psb = C["ps_bf"]
    bank_ev = C.setdefault("bank_ev", [[] for _ in range(6)])
    idb = C["ident_bf"]

    A = bf.alloc(8192, "exA")
    A2 = bf.alloc(8192, "exA2")
    A3 = bf.alloc(8192, "exA3")
    A4_ = bf.alloc(8192, "exA4")
    H = bf.alloc(8192, "exH")
    H2 = bf.alloc(8192, "exH2")
    Pm = bf.alloc(8192, "exP")
    Qm = bf.alloc(8192, "exQ")

    def sqmm(lhsT_sb, rhs_sb, out_sb, tagm, post=None, post_arg=None):
        ops_ev = []
        for dt in range(8):
            for nh2 in range(2):
                bank = (dt * 2 + nh2) % 2
                pb = psf[bank]
                for ic in range(DCH):
                    P.op("tensor", (lambda l=lhsT_sb[:, ic * 1024 + dt * 128:
                        ic * 1024 + dt * 128 + 128],
                        r=rhs_sb[:, ic * 1024 + nh2 * 512:
                                 ic * 1024 + nh2 * 512 + 512],
                        p=pb, st_=(ic == 0), sp=(ic == DCH - 1): (lambda e:
                        e.matmul(p[:], l, r, start=st_, stop=sp,
                         skip_group_check=True)))(),
                        ([(dv, dv.n), (ac, ac.n)] + bank_ev[bank])
                        if ic == 0 else (),
                        ((mm, 1),) if ic == DCH - 1 else (),
                        f"{tagm}.{dt}.{nh2}")
                mev = (mm, mm.n)
                dst = out_sb[:, dt * 1024 + nh2 * 512:
                             dt * 1024 + nh2 * 512 + 512]
                if post is None:
                    epv = P.op("scalar", (lambda p=pb, d=dst: (lambda e:
                        e.activation(d, p[:], AF.Copy, bias=0.0,
                        scale=1.0)))(), [mev], ((ac, 1),), f"{tagm}e{dt}.{nh2}")
                else:
                    xs = post_arg[:, dt * 1024 + nh2 * 512:
                                  dt * 1024 + nh2 * 512 + 512]
                    epv = P.op("vector", (lambda p=pb, d=dst, x_=xs: (lambda e:
                        e.tensor_tensor(d, p[:], x_, ALU.add)))(), [mev],
                        ((dv, 1),), f"{tagm}e{dt}.{nh2}")
                bank_ev[bank] = [epv[0]]
                ops_ev.append(epv[0])
        return [(P.sems["act"], P.sems["act"].n), (P.sems["dve"], P.sems["dve"].n)]

    def lincomb(dst, terms, tagl, idc=None):
        first = True
        for cf, tl in terms:
            if first:
                P.op("vector", (lambda d=dst, t_=tl, c_=cf: (lambda e:
                    e.tensor_scalar(d[:, :], t_[:, :], c_, None, ALU.mult)))(),
                    [(dv, dv.n), (ac, ac.n), (mm, mm.n)], ((dv, 1),),
                    tagl + "0")
                first = False
            else:
                P.op("vector", (lambda d=dst, t_=tl, c_=cf: (lambda e:
                    e.scalar_tensor_tensor(d[:, :], t_[:, :], c_, d[:, :],
                    ALU.mult, ALU.add)))(), [(dv, dv.n)], ((dv, 1),),
                    tagl + "x")
        if idc is not None:
            for cc2 in range(DCH):
                dsl = dst[:, cc2 * 1024 + cc2 * 128: cc2 * 1024 + (cc2 + 1) * 128]
                P.op("vector", (lambda d=dsl, c_=idc: (lambda e:
                    e.scalar_tensor_tensor(d, idb[:, :], c_, d, ALU.mult,
                    ALU.add)))(), [(dv, dv.n), (su, su.n)], ((dv, 1),),
                    tagl + f"i{cc2}")
        return [(P.sems["dve"], P.sems["dve"].n)]

    def transpose_full(src_sb, dst_sb, tagt):
        for ic in range(DCH):
            for jc in range(DCH):
                kk = ic * DCH + jc
                pbz = psb[kk % 2]
                P.op("tensor", (lambda s_=src_sb[:, ic * 1024 + jc * 128:
                    ic * 1024 + jc * 128 + 128], p=pbz: (lambda e:
                    e.transpose(p[:, 0:128], s_, idb[:])))(),
                    [(dv, dv.n), (su, su.n)] + bank_ev[4 + kk % 2], ((mm, 1),),
                    f"{tagt}t{ic}.{jc}")
                mev = (mm, mm.n)
                ee = P.op("vector", (lambda p=pbz, d=dst_sb[:,
                    jc * 1024 + ic * 128: jc * 1024 + ic * 128 + 128]:
                    (lambda e: e.tensor_copy(d, p[:, 0:128])))(), [mev],
                    ((dv, 1),), f"{tagt}e{ic}.{jc}")
                bank_ev[4 + kk % 2] = [ee[0]]
        return [(P.sems["dve"], P.sems["dve"].n)]

    c = [1.0]
    for i in range(1, 13):
        c.append(c[-1] / i)
    for gi in range(4):
        src = IN["L1_ffn_gens"][gi].rearrange("(ic p) c -> p ic c", p=128)
        P.dma("gpsimd", A[:, :].rearrange("p (ic c) -> p ic c", ic=8), src,
              [(mm, mm.n), (dv, dv.n), (ac, ac.n)], f"ex:g{gi}", ld)
        gev = [(ld, ld.n)]
        P.op("vector", lambda e: e.tensor_scalar(A[:, :], A[:, :], 0.25, None,
             ALU.mult), gev + [(mm, mm.n)], ((dv, 1),), f"ex:a{gi}")
        sqmm(A, A, A2, f"ex:A2.{gi}")
        P.op("vector", lambda e: e.tensor_scalar(A2[:, :], A2[:, :], -1.0, None,
             ALU.mult), [(dv, dv.n), (ac, ac.n)], ((dv, 1),), f"ex:a2n{gi}")
        sqmm(A2, A, A3, f"ex:A3.{gi}")
        sqmm(A2, A2, A4_, f"ex:A4.{gi}")
        lincomb(H, [(0.0, A)], f"ex:q3.{gi}", idc=c[12])
        sqmm(A4_, H, H2, f"ex:h1.{gi}")
        lincomb(H, [(c[9], A), (c[10], A2), (c[11], A3),
                    (1.0, H2)], f"ex:q2.{gi}", idc=c[8])
        sqmm(A4_, H, H2, f"ex:h2.{gi}")
        lincomb(H, [(c[5], A), (c[6], A2), (c[7], A3),
                    (1.0, H2)], f"ex:q1.{gi}", idc=c[4])
        sqmm(A4_, H, H2, f"ex:h3.{gi}")
        lincomb(H, [(c[1], A), (c[2], A2), (c[3], A3),
                    (1.0, H2)], f"ex:q0.{gi}", idc=c[0])
        transpose_full(H, H2, f"ex:tr.{gi}")
        sqmm(H2, H, A2, f"ex:s1.{gi}")
        sqmm(H, H2, A3, f"ex:s1t.{gi}")
        sqmm(A3, A2, H, f"ex:s2.{gi}")
        if gi == 0:
            P.op("vector", lambda e: e.tensor_copy(Pm[:, :], H[:, :]),
                 [(ac, ac.n), (dv, dv.n)], ((dv, 1),), "ex:p0")
            transpose_full(H, Qm, "ex:q0t")
        else:
            sqmm(Qm, H, H2, f"ex:pu.{gi}")
            qev = sqmm(H, Qm, A2, f"ex:qu.{gi}")
            P.op("vector", lambda e: e.tensor_copy(Pm[:, :], H2[:, :]),
                 [(ac, ac.n), (dv, dv.n), (mm, mm.n)], ((dv, 1),), f"ex:pc{gi}")
            P.op("vector", lambda e: e.tensor_copy(Qm[:, :], A2[:, :]),
                 [(dv, dv.n)], ((dv, 1),), f"ex:qc{gi}")
    fev = [(P.sems["dve"], P.sems["dve"].n)]
    for ic2 in range(DCH):
        P.dma("scalar", SC["rtot"][ic2],
              Pm[:, ic2 * 1024:(ic2 + 1) * 1024], fev, f"ex:st{ic2}", st)
    P.op("sync", None, [(st, st.n)], (), "ex:fin")
    P.barrier("expm")


def emit_rotate(P, C):
    nc = P.nc
    bf = C["bf"]; f32 = C["f32"]
    SC = C["sc"]
    ld = P.sem("ro_ld"); mm = P.sem("ro_mm")
    ac = P.sem("act"); dv = P.sem("dve")
    m_sems = [P.sem("ro_m0"), P.sem("ro_m1")]
    s_sems = [P.sem("ro_s0"), P.sem("ro_s1")]
    psf = C["ps_f32"]
    bank_ev = C.setdefault("bank_ev", [[] for _ in range(6)])
    rt = bf.alloc(8192, "ro_rt")
    for ic in range(DCH):
        P.dma("sync", rt[:, ic * 1024:(ic + 1) * 1024], SC["rtot"][ic], (),
              f"ro:w{ic}", ld)
    mo = Ring([bf.alloc(4096, f"ro_m{i}") for i in range(2)])
    ev_st = Ring([f32.alloc(512, f"ro_e{i}") for i in range(2)])
    w_ev = [(ld, ld.n)]
    for tc in range(8):
        tsl = slice(tc * 512, (tc + 1) * 512)
        im, mb, relm = mo.take()
        msem = m_sems[im % 2]
        for c in range(DCH):
            P.dma("sync", mb[:, c * 512:(c + 1) * 512], SC["moe1"][c][:, tsl],
                  relm if c == 0 else (), f"ro:m{tc}.{c}", msem)
        m_ev = [(msem, msem.n)]
        mm_last = []
        for dt in range(8):
            bank = dt % 2
            pb = psf[bank]
            for ic in range(DCH):
                P.op("tensor", (lambda l=rt[:, ic * 1024 + dt * 128:
                    ic * 1024 + dt * 128 + 128],
                    r=mb[:, ic * 512:(ic + 1) * 512], p=pb, st_=(ic == 0),
                    sp=(ic == DCH - 1): (lambda e: e.matmul(p[:], l, r,
                    start=st_, stop=sp, skip_group_check=True)))(),
                    (w_ev + m_ev + bank_ev[bank]) if ic == 0 else (),
                    ((mm, 1),) if ic == DCH - 1 else (), f"ro:mm{tc}.{dt}")
            mev = (mm, mm.n)
            mm_last.append(mev)
            io, obuf, relo = ev_st.take()
            ee = P.op("scalar", (lambda p=pb, d=obuf: (lambda e: e.activation(
                d[:, :], p[:], AF.Copy, bias=0.0, scale=1.0)))(),
                [mev] + relo, ((ac, 1),), f"ro:e{tc}.{dt}")
            bank_ev[bank] = [ee[0]]
            ssem = s_sems[io % 2]
            dd = P.dma("scalar", SC["nonabel"][dt][:, tsl], obuf[:, :],
                       [ee[0]], f"ro:s{tc}.{dt}", ssem)
            ev_st.release(io, [(ssem, ssem.n)])
        mo.release(im, mm_last[-1:])
    P.op("sync", None, [(s_sems[0], s_sems[0].n), (s_sems[1], s_sems[1].n)],
         (), "ro:fin")
    P.barrier("rot")


def emit_final(P, C):
    nc = P.nc
    bf = C["bf"]; f32 = C["f32"]
    IN = C["in"]; SC = C["sc"]
    ld = P.sem("fn_ld"); ldg = P.sem("fn_ldg"); mm = P.sem("fn_mm")
    ac = P.sem("act"); dv = P.sem("dve"); st = P.sem("fn_st")
    o_sems = [P.sem("fn_o0"), P.sem("fn_o1")]
    su = P.sem("setup"); dsy = P.sem("dma_sync")
    psf = C["ps_f32"]; psb = C["ps_bf"]
    bank_ev = C.setdefault("bank_ev", [[] for _ in range(6)])
    idf = C["ident_f32"]
    out = C["out"]

    gcw = bf.alloc(16, "gcw")
    P.op("gpsimd", lambda e: e.dma_start(gcw[:, 0:16],
         IN["L1_ffn_gc_w"].rearrange("(k p) one -> p (k one)", p=128),
         allow_slow_non_contiguous=True), (), ((P.sem("dma_gpsimd"), 16),),
         "fn:gcw")
    gcb = f32.alloc(1, "gcb")
    P.op("sync", lambda e: e.dma_start(gcb[0:1, 0:1],
         IN["L1_ffn_gc_b"].unsqueeze(0)), (), ((P.sem("dma_sync"), 16),),
         "fn:gcb")
    w_ev = [(dsy, dsy.n), (P.sem("dma_gpsimd"), P.sem("dma_gpsimd").n)]

    ab_f = f32.alloc(2048, "fn_ab")
    na_f = f32.alloc(2048, "fn_na")
    z_f = f32.alloc(2048, "fn_z")
    s_f = f32.alloc(2048, "fn_s")
    x_f = f32.alloc(2048, "fn_x")
    ab_b = bf.alloc(2048, "fn_abb")
    na_b = bf.alloc(2048, "fn_nab")
    grow = f32.alloc(256 * 3, "fn_gr")
    gB = f32.alloc(256, "fn_gB")
    ot_st = Ring([f32.alloc(256, f"fn_o{i}") for i in range(2)])
    gc_sc = SC["gate1"]

    for tb in range(16):
        tsl = slice(tb * 256, (tb + 1) * 256)
        guard = [(mm, mm.n), (dv, dv.n), (ac, ac.n), (st, st.n)]
        for c in range(DCH):
            P.dma("sync", ab_f[:, c * 256:(c + 1) * 256], SC["abel"][c][:, tsl],
                  guard if c == 0 else (), f"fn:a{tb}.{c}", ld)
            P.dma("sync", na_f[:, c * 256:(c + 1) * 256],
                  SC["nonabel"][c][:, tsl], (), f"fn:n{tb}.{c}", ld)
            P.dma("sync", z_f[:, c * 256:(c + 1) * 256], SC["z1f"][c][:, tsl],
                  (), f"fn:z{tb}.{c}", ld)
            P.dma("sync", s_f[:, c * 256:(c + 1) * 256], SC["s1"][c][:, tsl],
                  (), f"fn:s{tb}.{c}", ld)
            P.dma("sync", x_f[:, c * 256:(c + 1) * 256], SC["x1T"][c][:, tsl],
                  (), f"fn:x{tb}.{c}", ld)
        lev = [(ld, ld.n)]
        cast_ev = []
        for c in range(DCH):
            e1 = P.op("vector", (lambda cc=c: (lambda e: e.tensor_copy(
                ab_b[:, cc * 256:(cc + 1) * 256],
                ab_f[:, cc * 256:(cc + 1) * 256])))(),
                (lev + [(mm, mm.n)]) if c == 0 else (), ((dv, 1),),
                f"fn:cb{tb}.{c}")
            e2 = P.op("vector", (lambda cc=c: (lambda e: e.tensor_copy(
                na_b[:, cc * 256:(cc + 1) * 256],
                na_f[:, cc * 256:(cc + 1) * 256])))(), (), ((dv, 1),),
                f"fn:cn{tb}.{c}")
        cast_ev = [(dv, dv.n)]
        pb = psf[4]
        for c in range(DCH):
            P.op("tensor", (lambda cc=c: (lambda e: e.matmul(pb[0:1, 0:256],
                gcw[:, cc:cc + 1], ab_b[:, cc * 256:(cc + 1) * 256],
                start=(cc == 0), stop=False, skip_group_check=True)))(),
                (cast_ev + w_ev + bank_ev[4]) if c == 0 else (),
                ((mm, 1),), f"fn:g1{tb}.{c}")
        for c in range(DCH):
            P.op("tensor", (lambda cc=c: (lambda e: e.matmul(pb[0:1, 0:256],
                gcw[:, 8 + cc:8 + cc + 1], na_b[:, cc * 256:(cc + 1) * 256],
                start=False, stop=(cc == DCH - 1), skip_group_check=True)))(),
                (), ((mm, 1),), f"fn:g2{tb}.{c}")
        g_mm = (mm, mm.n)
        e3 = P.op("scalar", lambda e: e.activation(grow[0:1, 0:256],
            pb[0:1, 0:256], AF.Sigmoid, bias=gcb[0:1, 0:1], scale=1.0),
            [g_mm, (st, st.n)], ((ac, 1),), f"fn:sg{tb}")
        bank_ev[4] = [e3[0]]
        d1 = P.dma("scalar", gc_sc[0:1, tsl], grow[0:1, 0:256], [e3[0]],
                   f"fn:gs{tb}", st)
        b1 = P.dma("gpsimd", gB[:, :], gc_sc[0:1, tsl].partition_broadcast(128),
                   [d1[0], (dv, dv.n)], f"fn:gb{tb}", ldg)
        gbev = [(ldg, ldg.n)]
        for c in range(DCH):
            sl2 = slice(c * 256, (c + 1) * 256)
            P.op("vector", (lambda s_=sl2: (lambda e: e.tensor_tensor(
                na_f[:, s_], na_f[:, s_], ab_f[:, s_], ALU.subtract)))(),
                gbev if c == 0 else [(dv, dv.n)], ((dv, 1),), f"fn:d{tb}.{c}")
            P.op("vector", (lambda s_=sl2: (lambda e: e.tensor_tensor(
                na_f[:, s_], na_f[:, s_], gB[:, :], ALU.mult)))(),
                [(dv, dv.n)], ((dv, 1),), f"fn:m{tb}.{c}")
            P.op("vector", (lambda s_=sl2: (lambda e: e.tensor_tensor(
                na_f[:, s_], na_f[:, s_], ab_f[:, s_], ALU.add)))(),
                [(dv, dv.n)], ((dv, 1),), f"fn:mx{tb}.{c}")
            P.op("vector", (lambda s_=sl2: (lambda e: e.tensor_tensor(
                na_f[:, s_], na_f[:, s_], z_f[:, s_], ALU.add)))(),
                [(dv, dv.n)], ((dv, 1),), f"fn:f{tb}.{c}")
            e5 = P.op("scalar", (lambda s_=sl2: (lambda e: e.activation(
                na_f[:, s_], na_f[:, s_], AF.Silu, bias=0.0, scale=1.0)))(),
                [(dv, dv.n)], ((ac, 1),), f"fn:si{tb}.{c}")
            P.op("vector", (lambda s_=sl2: (lambda e: e.tensor_tensor(
                na_f[:, s_], na_f[:, s_], s_f[:, s_], ALU.mult)))(),
                [e5[0]], ((dv, 1),), f"fn:sm{tb}.{c}")
            P.op("vector", (lambda s_=sl2: (lambda e: e.tensor_tensor(
                na_f[:, s_], na_f[:, s_], x_f[:, s_], ALU.add)))(),
                [(dv, dv.n)], ((dv, 1),), f"fn:x2{tb}.{c}")
        x2_ev = [(dv, dv.n)]
        for q in range(2):
            for c in range(DCH):
                kk = q * DCH + c
                pbf = psf[kk % 2]
                P.op("tensor", (lambda s_=na_f[:, c * 256 + q * 128:
                    c * 256 + q * 128 + 128], p=pbf: (lambda e: e.transpose(
                    p[:, 0:128], s_, idf[:])))(),
                    x2_ev + bank_ev[kk % 2] + [(su, su.n)], ((mm, 1),),
                    f"fn:t{tb}.{q}.{c}")
                mev = (mm, mm.n)
                io, obuf, relo = ot_st.take()
                ee = P.op("vector", (lambda p=pbf, d=obuf: (lambda e:
                    e.tensor_copy(d[:, 0:128], p[:, 0:128])))(), [mev] + relo,
                    ((dv, 1),), f"fn:te{tb}.{q}.{c}")
                bank_ev[kk % 2] = [ee[0]]
                osem = o_sems[io % 2]
                dd = P.dma("scalar", out[tb * 256 + q * 128:
                           tb * 256 + (q + 1) * 128, c * 128:(c + 1) * 128],
                           obuf[:, 0:128], [ee[0]], f"fn:o{tb}.{q}.{c}", osem)
                ot_st.release(io, [(osem, osem.n)])
    P.op("sync", None, [(o_sems[0], o_sems[0].n), (o_sems[1], o_sems[1].n),
         (st, st.n)], (), "fn:fin")
    P.barrier("final")


SCRATCH_SPECS = {
    "x0T": ([DCH, 128, T], F32),
    "xn0": ([DCH, 128, T], BF16),
    "q0": ([NSEQ, H0, 64, SEQ], BF16),
    "k0": ([NSEQ, H0, 64, SEQ], BF16),
    "v0": ([NT, 128, 1040], BF16),
    "q1": ([NSEQ, H0, 68, SEQ], BF16),
    "k1": ([NSEQ, H0, 68, SEQ], BF16),
    "v1": ([NT, 128, 1040], BF16),
    "s0": ([DCH, 128, T], F32),
    "s1": ([DCH, 128, T], F32),
    "x1T": ([DCH, 128, T], F32),
    "z1f": ([DCH, 128, T], F32),
    "zn1": ([DCH, 128, T], BF16),
    "zn2": ([DCH, 128, T], BF16),
    "abel": ([DCH, 128, T], F32),
    "nonabel": ([DCH, 128, T], F32),
    "moe0": ([DCH, 128, T], F32),
    "moe1": ([DCH, 128, T], BF16),
    "rtot": ([DCH, 128, D], BF16),
    "cw0": ([E0, T], F32),
    "cw1": ([E1, T], F32),
    "gate1": ([1, T], F32),
    "dbg": ([32, 128, 1024], BF16),
    "lnrows": ([2, T], F32),
}


def build_program(phases, mode="fused", scratch_in=(), scratch_out=()):
    nc = bass.Bass("TRN2", target_bir_lowering=False)
    C = {"in": {}, "sc": {}}
    C["in"]["x"] = nc.dram_tensor("x", [T, D], F32, kind="ExternalInput").ap()
    for nm, shape in INPUT_WEIGHT_SPECS:
        C["in"][nm] = nc.dram_tensor(nm, list(shape), F32, kind="ExternalInput").ap()
    for nm, arr_shape, dt in CONST_SPECS:
        C["in"][nm] = nc.dram_tensor(nm, list(arr_shape), dt, kind="ExternalInput").ap()
    C["out"] = nc.dram_tensor("out", [T, D], F32, kind="ExternalOutput").ap()
    for nm, (shape, dt) in SCRATCH_SPECS.items():
        kind = "Internal"
        if nm in scratch_in:
            kind = "ExternalInput"
        elif nm in scratch_out:
            kind = "ExternalOutput"
        C["sc"][nm] = nc.dram_tensor(nm, list(shape), dt, kind=kind).ap()

    import contextlib
    stack = contextlib.ExitStack()
    bf_t = stack.enter_context(nc.sbuf_tensor("pool_bf", [128, 73300], BF16))
    f32_t = stack.enter_context(nc.sbuf_tensor("pool_f32", [128, 12400], F32))
    C["bf"] = Pool(bf_t, 73300)
    C["f32"] = Pool(f32_t, 12400)
    C["ident_bf"] = stack.enter_context(nc.sbuf_tensor("ident_bf", [128, 128], BF16))
    C["ident_f32"] = stack.enter_context(nc.sbuf_tensor("ident_f32", [128, 128], F32))
    C["qpos_row"] = stack.enter_context(nc.sbuf_tensor("qpos_row", [1, SEQ], F32))
    C["wt_stage"] = [stack.enter_context(nc.sbuf_tensor(f"wts{i}", [128, 1024], BF16))
                     for i in range(2)]
    C["ps_f32"] = [stack.enter_context(nc.psum_tensor(f"psf{i}", [128, 512], F32))
                   for i in range(6)]
    C["ps_bf"] = [stack.enter_context(nc.psum_tensor(f"psb{i}", [128, 1024], BF16))
                  for i in range(2)]

    P = Prog(nc)
    emit_setup(P, C)
    P.barrier("setup")
    for ph in phases:
        C["bf"].reset()
        C["f32"].reset()
        ph(P, C)
    with nc.Block() as block:
        P.emit(block)
    stack.close()
    return nc, P


INPUT_WEIGHT_SPECS = [
    ("L0_attn_wq", (1024, 1024)), ("L0_attn_bq", (1024,)),
    ("L0_attn_wk", (1024, 1024)), ("L0_attn_bk", (1024,)),
    ("L0_attn_wv", (1024, 1024)), ("L0_attn_bv", (1024,)),
    ("L0_attn_wo", (1024, 1024)), ("L0_attn_bo", (1024,)),
    ("L0_ffn_router_w", (8, 1024)),
    ("L0_ffn_w1", (8, 1024, 4096)), ("L0_ffn_w2", (8, 4096, 1024)),
    ("L1_attn_w_qkv", (3072, 1024)), ("L1_attn_w_out", (1024, 1024)),
    ("L1_attn_alibi_slopes", (16,)),
    ("L1_ffn_ln_g", (1024,)), ("L1_ffn_ln_b", (1024,)),
    ("L1_ffn_aw1", (1024, 4096)), ("L1_ffn_ab1", (4096,)),
    ("L1_ffn_aw2", (4096, 1024)), ("L1_ffn_ab2", (1024,)),
    ("L1_ffn_gate_w", (1024, 6)), ("L1_ffn_gate_b", (6,)),
    ("L1_ffn_ew1", (6, 1024, 2048)), ("L1_ffn_eb1", (6, 2048)),
    ("L1_ffn_ew2", (6, 2048, 1024)), ("L1_ffn_eb2", (6, 1024)),
    ("L1_ffn_gens", (4, 1024, 1024)),
    ("L1_ffn_gc_w", (2048, 1)), ("L1_ffn_gc_b", (1,)),
]

CONST_SPECS = [
    ("rope_cos", (128, SEQ), BF16),
    ("rope_sin", (128, SEQ), BF16),
    ("rope_perm", (128, 128), BF16),
    ("ones2_64", (2, 64), BF16),
]


ALL_PHASES = [emit_p1_ln0, emit_p2_qkv0, emit_p3_attn0, emit_moe0,
              emit_ln1, emit_p6_qkv1, emit_p7_attn1, emit_abel,
              emit_moe1, emit_expm, emit_rotate, emit_final]


def kernel(**inputs):
    consts = host_constants()
    x_full = np.ascontiguousarray(np.asarray(inputs["L0_attn_x"], dtype=np.float32))
    shards = x_full.reshape(NCORES, T, D)
    base_map = {nm: np.ascontiguousarray(np.asarray(inputs[nm], np.float32))
                for nm, _ in INPUT_WEIGHT_SPECS}
    for nm, _, _ in CONST_SPECS:
        base_map[nm] = consts[nm]
    nc, P = build_program(ALL_PHASES)
    in_maps = []
    for c in range(NCORES):
        m = dict(base_map)
        m["x"] = shards[c]
        in_maps.append(m)
    res = run_bass_kernel_spmd(nc, in_maps, core_ids=list(range(NCORES)))
    out = np.concatenate([r["out"][None] for r in res.results], axis=0)
    return out.reshape(16, 2048, 1024)


# version 04
# speedup vs baseline: 1.0034x; 1.0034x over previous
import math
import numpy as np
import ml_dtypes
import concourse.bass as bass
import concourse.mybir as mybir
from concourse.bass_utils import run_bass_kernel_spmd

F32 = mybir.dt.float32
BF16 = mybir.dt.bfloat16
AF = mybir.ActivationFunctionType
ALU = mybir.AluOpType
AX = mybir.AxisListType
bfd = ml_dtypes.bfloat16

NCORES = 8
SEQ = 2048
NSEQ = 2
T = SEQ * NSEQ
D = 1024
NT = T // 128
DCH = D // 128
H0 = 16
DK = 64
E0, H0FF = 8, 4096
E1, H1FF = 6, 2048
A4 = 4096
LN_EPS = 1e-5
DEBUG_MOE = False
ENGS = ("tensor", "scalar", "vector", "gpsimd", "sync")


class Sem:
    __slots__ = ("h", "n", "name")
    def __init__(self, h, name):
        self.h = h
        self.n = 0
        self.name = name


class Op:
    __slots__ = ("fn", "waits", "incs", "tag")
    def __init__(self, fn, waits, incs, tag):
        self.fn = fn
        self.waits = waits
        self.incs = incs
        self.tag = tag


class Prog:

    def __init__(self, nc):
        self.nc = nc
        self.ops = {e: [] for e in ENGS}
        self.sems = {}
        self._exit = []

    def sem(self, name):
        if name not in self.sems:
            h = self.nc.semaphore(name).__enter__()
            self.sems[name] = Sem(h, name)
        return self.sems[name]

    def op(self, engine, fn, waits=(), incs=(), tag=""):
        ws = []
        for w in waits:
            if isinstance(w, Sem):
                ws.append((w, w.n))
            else:
                ws.append(w)
        evs = []
        rincs = []
        for s, amt in incs:
            s.n += amt
            rincs.append((s, amt))
            evs.append((s, s.n))
        self.ops[engine].append(Op(fn, ws, rincs, tag))
        return evs

    def barrier(self, tag="bar"):
        snap = [(s, s.n) for s in self.sems.values() if s.n > 0]
        bar = self.sem("bar")
        target = bar.n + len(ENGS)
        for e in ENGS:
            self.ops[e].append(Op(None, list(snap), [], tag + ":snap"))
            self.ops[e].append(Op(lambda eng: eng.nop(), [], [(bar, 1)], tag + ":inc"))
        bar.n = target
        for e in ENGS:
            self.ops[e].append(Op(None, [(bar, target)], [], tag + ":wait"))

    def dma(self, qeng, dst, src, waits=(), tag="dma", sem=None):
        sem = sem or self.sem("dma_" + qeng)
        return self.op(qeng, lambda eng: eng.dma_start(dst, src), waits,
                       [(sem, 16)], tag)

    def emit(self, block):
        def mk(engine):
            def run(eng):
                for o in self.ops[engine]:
                    for s, v in o.waits:
                        eng.wait_ge(s.h, v)
                    if o.fn is None:
                        continue
                    inst = o.fn(eng)
                    for s, amt in o.incs:
                        inst.then_inc(s.h, amt)
            return run
        block.tensor(mk("tensor"))
        block.scalar(mk("scalar"))
        block.vector(mk("vector"))
        block.gpsimd(mk("gpsimd"))
        block.sync(mk("sync"))

    def counts(self):
        return {e: len(v) for e, v in self.ops.items()}


class Pool:

    def __init__(self, t, n):
        self.t = t
        self.n = n
        self.off = 0
        self.hi = n

    def alloc(self, cols, name=""):
        off = self.off
        self.off += cols
        assert self.off <= self.hi, f"pool overflow {name}: {self.off} > {self.hi}"
        return self.t[:, off:off + cols]

    def persist(self, cols, name=""):
        self.hi -= cols
        assert self.off <= self.hi, f"pool overflow (persist) {name}"
        return self.t[:, self.hi:self.hi + cols]

    def reset(self):
        self.off = 0


def host_constants():
    d = np.arange(64)
    j = np.where(d < 32, d, d - 32)
    theta = 10000.0 ** (2.0 * j / 64.0)
    pos = np.arange(SEQ, dtype=np.float64)
    ang = pos[None, :] / theta[:, None]
    cos = np.cos(ang)
    sin = np.sin(ang)
    cos_fm = np.tile(cos, (2, 1)).astype(bfd)
    sin_fm = np.tile(sin, (2, 1)).astype(bfd)
    P = np.zeros((128, 128), np.float32)
    for b in range(2):
        for i in range(0, 64, 2):
            P[b * 64 + i + 1, b * 64 + i] = -1.0
            P[b * 64 + i, b * 64 + i + 1] = +1.0
    rope_perm = P.astype(bfd)
    ones2_64 = np.ones((2, 64), np.float32).astype(bfd)
    return {"rope_cos": cos_fm, "rope_sin": sin_fm, "rope_perm": rope_perm,
            "ones2_64": ones2_64}


def emit_setup(P, C):
    nc = P.nc
    s1 = P.sem("setup")
    g = P.op
    idb, idf = C["ident_bf"], C["ident_f32"]
    g("gpsimd", lambda e: e.memset(idb[:], 1.0), (), ((s1, 1),), "idb0")
    g("gpsimd", lambda e: e.affine_select(idb[:], idb[:], pattern=[[1, 128]],
      compare_op=ALU.is_equal, fill=0.0, base=0, channel_multiplier=-1),
      [(s1, s1.n)], ((s1, 1),), "idb1")
    g("gpsimd", lambda e: e.memset(idf[:], 1.0), [(s1, s1.n)], ((s1, 1),), "idf0")
    g("gpsimd", lambda e: e.affine_select(idf[:], idf[:], pattern=[[1, 128]],
      compare_op=ALU.is_equal, fill=0.0, base=0, channel_multiplier=-1),
      [(s1, s1.n)], ((s1, 1),), "idf1")
    def mkregs(eng):
        r0 = eng.alloc_register("fz")
        eng.reg_mov(r0, 0.0)
        rn = eng.alloc_register("fneg")
        inst = eng.reg_mov(rn, -10000.0)
        C["fill0"] = r0
        C["filln"] = rn
        return inst
    g("gpsimd", mkregs, (), ((s1, 1),), "fregs")
    qpos = C["qpos_row"]
    g("gpsimd", lambda e: e.iota(qpos[:], pattern=[[1, SEQ]], base=0,
      channel_multiplier=0, allow_small_or_imprecise_dtypes=True),
      (), ((s1, 1),), "iota")
    C["ev_setup"] = [(s1, s1.n)]


def load_weight_T(P, C, w_ap, n_out, n_in, dst_sb, tag):
    ld_sems = [P.sem("wt_ld0"), P.sem("wt_ld1")]
    mm = P.sem("wt_mm")
    ev = P.sem("wt_ev")
    ring = C.setdefault("wt_ring", Ring(list(C["wt_stage"])))
    psr = C.setdefault("wt_psr", Ring(list(C["ps_bf"])))
    idb = C["ident_bf"]
    su = P.sem("setup")
    id_ev = [(su, su.n)]
    n_oc = n_out // 128
    n_ic = n_in // 128
    for oc in range(n_oc):
        i, buf, rel = ring.take()
        ldi = ld_sems[i % 2]
        P.dma("gpsimd", buf[:, 0:n_in],
              w_ap[oc * 128:(oc + 1) * 128, :], rel, f"wT:{tag}:ld{oc}", ldi)
        lev = (ldi, ldi.n)
        last_mm = None
        for ic in range(n_ic):
            j, pb, relp = psr.take()
            P.op("tensor", (lambda b=buf, icc=ic, p=pb: (lambda e: e.transpose(
                p[:, 0:128], b[:, icc * 128:(icc + 1) * 128], idb[:])))(),
                [lev] + relp + id_ev, ((mm, 1),), f"wT:{tag}:tr{oc}.{ic}")
            last_mm = (mm, mm.n)
            dst = dst_sb[:, ic * n_out + oc * 128: ic * n_out + (oc + 1) * 128]
            e1 = P.op("vector", (lambda d=dst, p=pb: (lambda e: e.tensor_copy(
                d, p[:, 0:128])))(), [last_mm], ((ev, 1),),
                f"wT:{tag}:ev{oc}.{ic}")
            psr.release(j, [e1[0]])
        ring.release(i, [last_mm])


def per_partition_bias(P, C, vec_ap, n, dst_sb, tag):
    ap = vec_ap.rearrange("(c p) -> p c", p=128)
    sem = P.sem("dma_sync")
    P.op("sync", lambda e: e.dma_start(dst_sb[:, 0:n // 128], ap,
         allow_slow_non_contiguous=True), (), ((sem, 16),), "ppb:" + tag)
    return [(sem, sem.n)]


def broadcast_row(P, C, row_ap, n, dst_sb, tag, waits=()):
    src = row_ap if row_ap.ndim == 2 else row_ap.unsqueeze(0)
    sem = P.sem("dma_sync")
    P.op("sync", lambda e: e.dma_start(dst_sb[:, 0:n],
         src.partition_broadcast(128)), waits, ((sem, 16),), "bc:" + tag)
    return [(sem, sem.n)]


def emit_rstd(P, sums_col, sumsq_col, rstd_col, nmurstd_col, n, eps, waits, tag):
    g = P.op
    dv = P.sem("dve")
    sc = P.sem("act")
    inv = 1.0 / n
    g("vector", lambda e: e.tensor_scalar(nmurstd_col, sums_col, inv, None,
      ALU.mult), waits, ((dv, 1),), tag + ":mu")
    g("vector", lambda e: e.scalar_tensor_tensor(rstd_col, nmurstd_col, 1.0,
      nmurstd_col, ALU.mult, ALU.mult), [(dv, dv.n)], ((dv, 1),), tag + ":mu2")
    g("vector", lambda e: e.scalar_tensor_tensor(rstd_col, sumsq_col, inv,
      rstd_col, ALU.mult, ALU.subtract), [(dv, dv.n)], ((dv, 1),), tag + ":var")
    g("vector", lambda e: e.tensor_scalar(rstd_col, rstd_col, float(eps), None,
      ALU.add), [(dv, dv.n)], ((dv, 1),), tag + ":eps")
    g("scalar", lambda e: e.activation(rstd_col, rstd_col, AF.Ln, bias=0.0,
      scale=1.0), [(dv, dv.n)], ((sc, 1),), tag + ":ln")
    g("scalar", lambda e: e.activation(rstd_col, rstd_col, AF.Exp, bias=0.0,
      scale=-0.5), [(sc, sc.n)], ((sc, 1),), tag + ":exp")
    g("vector", lambda e: e.scalar_tensor_tensor(nmurstd_col, nmurstd_col, -1.0,
      rstd_col, ALU.mult, ALU.mult), [(sc, sc.n)], ((dv, 1),), tag + ":nmr")
    return [(P.sems["dve"], P.sems["dve"].n)]


def emit_cw_router(P, C, pb, mm_ev, scr, cols, tag):
    ac = P.sem("act"); dv = P.sem("dve")
    ne = 8
    pp = scr[:, 0:ne]; eqt = scr[:, ne:2 * ne]; cw = scr[:, 2 * ne:3 * ne]
    c_s = cols[:, 0:1]; c_m1 = cols[:, 1:2]; c_m2 = cols[:, 2:3]
    c_d = cols[:, 3:4]; c_r = cols[:, 4:5]
    e1 = P.op("scalar", (lambda p=pb: (lambda e: e.activation(pp, p[0:128, 0:ne],
        AF.Exp, bias=0.0, scale=1.0, accum_out=c_s)))(),
        [mm_ev, (dv, dv.n)], ((ac, 1),), tag + ":exp")
    P.op("vector", lambda e: e.reduce_max(c_m1, pp, axis=AX.X),
         [e1[0]], ((dv, 1),), tag + ":m1")
    P.op("vector", lambda e: e.tensor_scalar(eqt, pp, c_m1, None, ALU.is_ge),
         [(dv, dv.n)], ((dv, 1),), tag + ":eq")
    P.op("vector", lambda e: e.tensor_scalar(eqt, eqt, -1.0, 1.0, ALU.mult,
         ALU.add), [(dv, dv.n)], ((dv, 1),), tag + ":om")
    P.op("vector", lambda e: e.tensor_tensor(eqt, pp, eqt, ALU.mult),
         [(dv, dv.n)], ((dv, 1),), tag + ":pm")
    P.op("vector", lambda e: e.reduce_max(c_m2, eqt, axis=AX.X),
         [(dv, dv.n)], ((dv, 1),), tag + ":m2")
    P.op("vector", lambda e: e.tensor_scalar(c_d, c_s, 1e-9, None, ALU.mult),
         [(dv, dv.n)], ((dv, 1),), tag + ":ds")
    P.op("vector", lambda e: e.tensor_tensor(c_d, c_d, c_m1, ALU.add),
         [(dv, dv.n)], ((dv, 1),), tag + ":d1")
    P.op("vector", lambda e: e.tensor_tensor(c_d, c_d, c_m2, ALU.add),
         [(dv, dv.n)], ((dv, 1),), tag + ":d2")
    P.op("vector", lambda e: e.reciprocal(c_r, c_d), [(dv, dv.n)],
         ((dv, 1),), tag + ":rc")
    P.op("vector", lambda e: e.tensor_scalar(cw, pp, c_m2, None, ALU.is_ge),
         [(dv, dv.n)], ((dv, 1),), tag + ":sel")
    P.op("vector", lambda e: e.tensor_tensor(cw, cw, pp, ALU.mult),
         [(dv, dv.n)], ((dv, 1),), tag + ":cwp")
    ecw = P.op("vector", lambda e: e.tensor_scalar(cw, cw, c_r, None,
         ALU.mult), [(dv, dv.n)], ((dv, 1),), tag + ":cw")
    return cw, ecw[0]


def emit_cw_gate(P, C, pb, mm_ev, scr, cols, tag):
    ac = P.sem("act"); dv = P.sem("dve")
    ne = 6
    ll = scr[:, 0:ne]; eqt = scr[:, 8:8 + ne]; cw = scr[:, 16:16 + ne]
    c_m1 = cols[:, 1:2]; c_m2 = cols[:, 2:3]
    c_d = cols[:, 3:4]; c_r = cols[:, 4:5]; c_nm = cols[:, 5:6]
    e0 = P.op("scalar", (lambda p=pb: (lambda e: e.activation(ll,
        p[0:128, 0:ne], AF.Copy, bias=0.0, scale=1.0)))(),
        [mm_ev, (dv, dv.n)], ((ac, 1),), tag + ":cp")
    P.op("vector", lambda e: e.reduce_max(c_m1, ll, axis=AX.X),
         [e0[0]], ((dv, 1),), tag + ":m1")
    P.op("vector", lambda e: e.tensor_scalar(eqt, ll, c_m1, None, ALU.is_ge),
         [(dv, dv.n)], ((dv, 1),), tag + ":eq")
    P.op("vector", lambda e: e.tensor_scalar(eqt, eqt, -1e9, None, ALU.mult),
         [(dv, dv.n)], ((dv, 1),), tag + ":sc")
    P.op("vector", lambda e: e.tensor_tensor(eqt, eqt, ll, ALU.add),
         [(dv, dv.n)], ((dv, 1),), tag + ":lm")
    P.op("vector", lambda e: e.reduce_max(c_m2, eqt, axis=AX.X),
         [(dv, dv.n)], ((dv, 1),), tag + ":m2")
    P.op("vector", lambda e: e.tensor_scalar(c_nm, c_m1, -1.0, None, ALU.mult),
         [(dv, dv.n)], ((dv, 1),), tag + ":nm")
    P.op("vector", lambda e: e.tensor_tensor(c_d, c_m2, c_nm, ALU.add),
         [(dv, dv.n)], ((dv, 1),), tag + ":dm")
    e2 = P.op("scalar", lambda e: e.activation(c_d, c_d, AF.Exp, bias=0.0,
         scale=1.0), [(dv, dv.n)], ((ac, 1),), tag + ":ex")
    P.op("vector", lambda e: e.tensor_scalar(c_d, c_d, 1.0, None, ALU.add),
         [e2[0]], ((dv, 1),), tag + ":p1")
    P.op("vector", lambda e: e.reciprocal(c_r, c_d), [(dv, dv.n)],
         ((dv, 1),), tag + ":rc")
    e3 = P.op("scalar", (lambda: (lambda e: e.activation(ll, ll, AF.Exp,
        bias=c_nm, scale=1.0)))(), [(dv, dv.n)], ((ac, 1),), tag + ":z")
    P.op("vector", lambda e: e.tensor_scalar(cw, ll, c_m2, None, ALU.is_ge),
         [e3[0]], ((dv, 1),), tag + ":sel") if False else None
    P.op("vector", lambda e: e.tensor_scalar(c_d, c_d, -1.0, None, ALU.add),
         [(dv, dv.n)], ((dv, 1),), tag + ":em2")
    P.op("vector", lambda e: e.tensor_scalar(cw, ll, c_d, None, ALU.is_ge),
         [(dv, dv.n)], ((dv, 1),), tag + ":sel2")
    P.op("vector", lambda e: e.tensor_tensor(cw, cw, ll, ALU.mult),
         [(dv, dv.n)], ((dv, 1),), tag + ":cwz")
    ecw = P.op("vector", lambda e: e.tensor_scalar(cw, cw, c_r, None, ALU.mult),
         [(dv, dv.n)], ((dv, 1),), tag + ":cw")
    return cw, ecw[0]


def emit_p1_ln0(P, C):
    nc = P.nc
    bf = C["bf"]; f32 = C["f32"]
    x = C["in"]["x"]
    x0T = C["sc"]["x0T"]
    xn0 = C["sc"]["xn0"]
    idf, idb = C["ident_f32"], C["ident_bf"]
    ld = P.sem("p1_ld"); mm = P.sem("p1_mm"); ev = P.sem("p1_ev")
    ac = P.sem("act"); dv = P.sem("dve"); st = P.sem("p1_st")
    su = P.sem("setup")
    xt = [f32.alloc(1024, "xt0"), f32.alloc(1024, "xt1")]
    scr = f32.alloc(1024, "p1scr")
    xnf = [f32.alloc(1024, "xnf0"), f32.alloc(1024, "xnf1")]
    xnt = [bf.alloc(1024, "xnt0"), bf.alloc(1024, "xnt1")]
    cols = f32.alloc(8, "p1cols")
    evf = [bf.alloc(512, "evb0"), bf.alloc(512, "evb1")]
    evv = [f32.alloc(512, "evf0"), f32.alloc(512, "evf1")]
    xnfm = f32.alloc(1024, "xnfm")
    rt_scr = f32.alloc(24, "rt_scr")
    rt_cols = f32.alloc(8, "rt_cols")
    cwT_sb = f32.alloc(1024, "cwT")
    rw_nat = f32.alloc(1024, "rw_nat")
    rwT = f32.alloc(64, "rwT")
    psf = C["ps_f32"]; psb = C["ps_bf"]
    bank_ev = C.setdefault("bank_ev", [[] for _ in range(6)])

    ldrw = P.sem("p1_ldrw")
    P.dma("sync", rw_nat[0:8, :], C["in"]["L0_ffn_router_w"], (), "p1:rw", ldrw)
    rw_ld = (ldrw, ldrw.n)
    for c in range(DCH):
        P.op("tensor", (lambda cc=c: (lambda e: e.transpose(
            psf[5][0:128, 0:8], rw_nat[0:8, cc * 128:(cc + 1) * 128],
            idf[0:8, 0:8])))(), [rw_ld, (su, su.n)] + bank_ev[5], ((mm, 1),),
            f"p1:rwt{c}")
        e1 = P.op("vector", (lambda cc=c: (lambda e: e.tensor_copy(
            rwT[:, cc * 8:(cc + 1) * 8], psf[5][0:128, 0:8])))(),
            [(mm, mm.n)], ((dv, 1),), f"p1:rwe{c}")
        bank_ev[5] = [e1[0]]
    rw_ev = [(dv, dv.n)]

    lds = [P.sem("p1_ld0"), P.sem("p1_ld1")]
    for it in range(NT):
        b = it % 2
        xbuf = xt[b]
        ld = lds[b]
        w = [(mm, mm.n), (ac, ac.n)] if it >= 2 else ()
        P.dma("sync", xbuf[:, :], x[it * 128:(it + 1) * 128, :], w, f"p1:ld{it}", ld)
        lev = (ld, ld.n)
        P.op("scalar", (lambda xb=xbuf, c=cols: (lambda e: e.activation(
            scr[:, 0:1024], xb[:, :], AF.Copy, bias=0.0, scale=1.0,
            accum_out=c[:, 0:1])))(), [lev, (dv, dv.n)], ((ac, 1),), f"p1:sx{it}")
        P.op("scalar", (lambda xb=xbuf, c=cols: (lambda e: e.activation(
            scr[:, 0:1024], xb[:, :], AF.Square, bias=0.0, scale=1.0,
            accum_out=c[:, 1:2])))(), [(ac, ac.n)], ((ac, 1),), f"p1:sx2{it}")
        emit_rstd(P, cols[:, 0:1], cols[:, 1:2], cols[:, 2:3], cols[:, 3:4],
                  D, LN_EPS, [(ac, ac.n)], f"p1:st{it}")
        xnb = xnt[b]; xnfb = xnf[b]
        ef = P.op("scalar", (lambda xb=xbuf, xn=xnfb, c=cols: (lambda e:
            e.activation(xn[:, :], xb[:, :], AF.Identity, bias=c[:, 3:4],
                         scale=c[:, 2:3])))(), [(dv, dv.n)], ((ac, 1),),
            f"p1:xnf{it}")
        eb = P.op("scalar", (lambda xf=xnfb, xn=xnb: (lambda e: e.activation(
            xn[:, :], xf[:, :], AF.Copy, bias=0.0, scale=1.0)))(),
            [ef[0]], ((ac, 1),), f"p1:xnb{it}")
        xnev = eb[0]
        xnfev = ef[0]
        for half in range(2):
            pf = psf[half]
            pfn = psf[2 + half]
            pbz = psb[half]
            for q in range(4):
                c = half * 4 + q
                P.op("tensor", (lambda xb=xbuf, cc=c, p=pf, qq=q: (lambda e:
                    e.transpose(p[:, qq * 128:(qq + 1) * 128],
                                xb[:, cc * 128:(cc + 1) * 128], idf[:])))(),
                    [lev, (ev, ev.n)] if it * 2 + half >= 2 else [lev, (su, su.n)],
                    ((mm, 1),), f"p1:trf{it}.{c}")
                P.op("tensor", (lambda xn=xnb, cc=c, p=pbz, qq=q: (lambda e:
                    e.transpose(p[:, qq * 128:(qq + 1) * 128],
                                xn[:, cc * 128:(cc + 1) * 128], idb[:])))(),
                    [xnev, (ev, ev.n)] if it * 2 + half >= 2 else [xnev],
                    ((mm, 1),), f"p1:trb{it}.{c}")
                P.op("tensor", (lambda xn=xnfb, cc=c, p=pfn, qq=q: (lambda e:
                    e.transpose(p[:, qq * 128:(qq + 1) * 128],
                                xn[:, cc * 128:(cc + 1) * 128], idf[:])))(),
                    [xnfev, (ev, ev.n)] if it * 2 + half >= 2 else [xnfev],
                    ((mm, 1),), f"p1:trn{it}.{c}")
            mev = (mm, mm.n)
            ef2 = evv[half]
            eb2 = evf[half]
            sth_sems = [P.sem("p1_st0"), P.sem("p1_st1")]
            P.op("scalar", (lambda p=pf, d=ef2: (lambda e: e.activation(
                d[:, :], p[:, 0:512], AF.Copy, bias=0.0, scale=1.0)))(),
                [mev, (sth_sems[half], sth_sems[half].n)] if it * 2 + half >= 2 else [mev],
                ((ev, 1),), f"p1:evf{it}.{half}")
            P.op("vector", (lambda p=pbz, d=eb2: (lambda e: e.tensor_copy(
                d[:, :], p[:, 0:512])))(),
                [mev, (sth_sems[half], sth_sems[half].n)] if it * 2 + half >= 2 else [mev],
                ((ev, 1),), f"p1:evb{it}.{half}")
            P.op("vector", (lambda p=pfn, d=xnfm, hh=half: (lambda e:
                e.tensor_copy(d[:, hh * 512:(hh + 1) * 512], p[:, 0:512])))(),
                [mev, (mm, mm.n)], ((ev, 1),), f"p1:evn{it}.{half}")
            eev = (ev, ev.n)
            sth = sth_sems[half]
            for q in range(4):
                c = half * 4 + q
                P.dma("scalar", x0T[c, :, it * 128:(it + 1) * 128],
                      ef2[:, q * 128:(q + 1) * 128], [eev], f"p1:so{it}.{c}", sth)
                P.dma("scalar", xn0[c, :, it * 128:(it + 1) * 128],
                      eb2[:, q * 128:(q + 1) * 128], [eev], f"p1:sn{it}.{c}", sth)
        fm_ev = (ev, ev.n)
        rpb = psf[4]
        for c in range(DCH):
            P.op("tensor", (lambda cc=c: (lambda e: e.matmul(
                rpb[0:128, 0:8], xnfm[:, cc * 128:(cc + 1) * 128],
                rwT[:, cc * 8:(cc + 1) * 8], start=(cc == 0),
                stop=(cc == DCH - 1), skip_group_check=True)))(),
                ([fm_ev] + rw_ev + bank_ev[4]) if c == 0 else (),
                ((mm, 1),) if c == DCH - 1 else (), f"p1:rt{it}.{c}")
        rt_mm = (mm, mm.n)
        cw_ap, cw_ev = emit_cw_router(P, C, rpb, rt_mm, rt_scr, rt_cols,
                                      f"p1rt{it}")
        bank_ev[4] = [cw_ev]
        P.op("tensor", (lambda: (lambda e: e.transpose(psf[5][0:8, 0:128],
            cw_ap, idf[:])))(), [cw_ev] + bank_ev[5], ((mm, 1),),
            f"p1:ct{it}")
        e9 = P.op("vector", (lambda itt=it: (lambda e: e.tensor_copy(
            cwT_sb[0:8, (itt % 8) * 128:(itt % 8 + 1) * 128],
            psf[5][0:8, 0:128])))(),
            [(mm, mm.n), (P.sem("p1_stc"), P.sem("p1_stc").n)], ((dv, 1),),
            f"p1:ce{it}")
        bank_ev[5] = [e9[0]]
        if it % 8 == 7:
            ts8 = it // 8
            stc = P.sem("p1_stc")
            P.dma("scalar", C["sc"]["cw0"][:, ts8 * 1024:(ts8 + 1) * 1024],
                  cwT_sb[0:8, 0:1024], [e9[0]], f"p1:cs{ts8}", stc)
    P.op("sync", None, [(P.sem("p1_st0"), P.sem("p1_st0").n),
         (P.sem("p1_st1"), P.sem("p1_st1").n),
         (P.sem("p1_stc"), P.sem("p1_stc").n),
         (P.sem("dma_scalar"), P.sem("dma_scalar").n)], (), "p1:fin")
    P.barrier("p1")


class Ring:
    def __init__(self, slices):
        self.s = slices
        self.ev = [[] for _ in slices]
        self.i = 0

    def take(self):
        i = self.i % len(self.s)
        self.i += 1
        return i, self.s[i], list(self.ev[i])

    def release(self, i, evs):
        self.ev[i] = list(evs)


def emit_qkv(P, C, layer):
    nc = P.nc
    bf = C["bf"]; f32 = C["f32"]
    IN = C["in"]; SC = C["sc"]
    xn_src = SC["xn0"] if layer == 0 else SC["zn1"]
    qsc = SC["q0"] if layer == 0 else SC["q1"]
    ksc = SC["k0"] if layer == 0 else SC["k1"]
    vsc = SC["v0"] if layer == 0 else SC["v1"]
    mm = P.sem("qkv_mm")
    ac = P.sem("act"); dv = P.sem("dve")
    su = P.sem("setup")
    xn_sems = [P.sem("qkv_xn0"), P.sem("qkv_xn1")]
    qkst_sems = [P.sem("qkv_qs0"), P.sem("qkv_qs1"), P.sem("qkv_qs2")]
    vst_sems = [P.sem("qkv_vs0"), P.sem("qkv_vs1")]
    aug_st = P.sem("qkv_aug")
    psf = C["ps_f32"]
    bank_ev = C.setdefault("bank_ev", [[] for _ in range(6)])

    if layer == 0:
        wqT = bf.alloc(8192, "wqT"); wkT = bf.alloc(8192, "wkT")
        wvT = bf.alloc(8192, "wvT")
        load_weight_T(P, C, IN["L0_attn_wq"], 1024, 1024, wqT, "wq")
        load_weight_T(P, C, IN["L0_attn_wk"], 1024, 1024, wkT, "wk")
        load_weight_T(P, C, IN["L0_attn_wv"], 1024, 1024, wvT, "wv")
        def lhsT_tile(which, ic, oc):
            t = {"q": wqT, "k": wkT, "v": wvT}[which]
            return t[:, ic * 1024 + oc * 128: ic * 1024 + (oc + 1) * 128]
        def v_rhs(ic, half):
            return wvT[:, ic * 1024 + half * 512: ic * 1024 + half * 512 + 512]
        bq_pp = f32.alloc(8, "bq"); bk_pp = f32.alloc(8, "bk")
        per_partition_bias(P, C, IN["L0_attn_bq"], 1024, bq_pp, "bq")
        per_partition_bias(P, C, IN["L0_attn_bk"], 1024, bk_pp, "bk")
        bvB = f32.alloc(1024, "bvB")
        broadcast_row(P, C, IN["L0_attn_bv"], 1024, bvB, "bvB")
        cs_sb = bf.alloc(2048, "cos"); sn_sb = bf.alloc(2048, "sin")
        perm_sb = bf.alloc(128, "perm")
        P.dma("sync", cs_sb[:, :], IN["rope_cos"], (), "cos")
        P.dma("sync", sn_sb[:, :], IN["rope_sin"], (), "sin")
        P.dma("sync", perm_sb[:, :], IN["rope_perm"], (), "perm")
    else:
        wT = bf.alloc(24576, "wqkvT")
        load_weight_T(P, C, IN["L1_attn_w_qkv"], 3072, 1024, wT, "wqkv")
        def lhsT_tile(which, ic, oc):
            base = {"q": 0, "k": 1024, "v": 2048}[which]
            return wT[:, ic * 3072 + base + oc * 128: ic * 3072 + base + (oc + 1) * 128]
        def v_rhs(ic, half):
            return wT[:, ic * 3072 + 2048 + half * 512: ic * 3072 + 2048 + half * 512 + 512]
        slopes_sb = f32.alloc(16, "slopes")
        P.op("sync", lambda e: e.dma_start(slopes_sb[0:1, 0:16],
             IN["L1_attn_alibi_slopes"].unsqueeze(0)), (),
             ((P.sem("dma_sync"), 16),), "slopes")
    wt_ev = P.sem("wt_ev")
    dsy = P.sem("dma_sync")
    w_ready = [(wt_ev, wt_ev.n), (dsy, dsy.n)]

    xnr = Ring([bf.alloc(DCH * 512, f"xnc{i}") for i in range(2)])

    stg = Ring([bf.alloc(512, f"qk_s{i}") for i in range(3)])
    rtg = Ring([bf.alloc(512, f"qk_r{i}") for i in range(2)])
    t1g = Ring([bf.alloc(512, f"qk_t{i}") for i in range(2)])
    outg = Ring([bf.alloc(512, f"qk_o{i}") for i in range(3)])
    vst = Ring([bf.alloc(1040, f"vst{i}") for i in range(2)])
    for i in range(2):
        P.op("gpsimd", (lambda t=vst.s[i]: (lambda e: e.memset(t[:, :], 1.0)))(),
             (), ((su, 1),), f"vones{i}")
    ones_ev = [(su, su.n)]
    bias_ev = [(dsy, dsy.n)]

    for tc in range(8):
        s = tc // 4
        pos0 = (tc % 4) * 512
        ix, xn_c, relx = xnr.take()
        ldx = xn_sems[ix % 2]
        for c in range(DCH):
            P.dma("sync", xn_c[:, c * 512:(c + 1) * 512],
                  xn_src[c][:, tc * 512:(tc + 1) * 512],
                  relx if c == 0 else (), f"qkv{layer}:xn{tc}.{c}", ldx)
        xn_ev = [(ldx, ldx.n)]
        xn_consumers = []

        def xn_rhs(ic):
            return xn_c[:, ic * 512:(ic + 1) * 512]

        def xn_lhsT(ic, q):
            return xn_c[:, ic * 512 + q * 128: ic * 512 + q * 128 + 128]

        for which in ("q", "k"):
            for oc in range(8):
                bank = (oc + (0 if which == "q" else 1)) % 4
                pb = psf[bank]
                for ic in range(DCH):
                    waits = []
                    if ic == 0:
                        waits = w_ready + xn_ev + bank_ev[bank]
                    P.op("tensor", (lambda l=lhsT_tile(which, ic, oc),
                        r=xn_rhs(ic), p=pb, st_=(ic == 0), sp=(ic == DCH - 1):
                        (lambda e: e.matmul(p[:], l, r, start=st_, stop=sp)))(),
                        waits, ((mm, 1),) if ic == DCH - 1 else (),
                        f"qkv{layer}:{which}mm{oc}.{tc}")
                mm_ev = (mm, mm.n)
                xn_consumers.append(mm_ev)
                i0, sbuf_o, rel = stg.take()
                if layer == 0:
                    b_pp = bq_pp if which == "q" else bk_pp
                    fn = (lambda p=pb, d=sbuf_o, b=b_pp, o=oc: (lambda e:
                        e.activation(d[:, :], p[:], AF.Identity,
                                     bias=b[:, o:o + 1], scale=1.0)))()
                else:
                    sc_ = 0.125 if which == "q" else 1.0
                    fn = (lambda p=pb, d=sbuf_o, sc2=sc_: (lambda e:
                        e.activation(d[:, :], p[:], AF.Copy, bias=0.0,
                                     scale=sc2)))()
                eva = P.op("scalar", fn, [mm_ev] + rel + bias_ev, ((ac, 1),),
                           f"qkv{layer}:{which}ev{oc}.{tc}")
                bank_ev[bank] = [eva[0]]
                if layer == 0:
                    rbank = 4 + oc % 2
                    rpb = psf[rbank]
                    P.op("tensor", (lambda d=sbuf_o, p=rpb: (lambda e:
                        e.matmul(p[:], perm_sb[:, :], d[:, :], start=True,
                                 stop=True)))(),
                        [eva[0]] + bank_ev[rbank], ((mm, 1),),
                        f"qkv0:{which}rot{oc}.{tc}")
                    rot_mm = (mm, mm.n)
                    i1, rot_o, rel1 = rtg.take()
                    ebv = P.op("scalar", (lambda p=rpb, d=rot_o: (lambda e:
                        e.activation(d[:, :], p[:], AF.Copy, bias=0.0,
                                     scale=1.0)))(), [rot_mm] + rel1,
                        ((ac, 1),), f"qkv0:{which}rev{oc}.{tc}")
                    bank_ev[rbank] = [ebv[0]]
                    i2, t1_o, rel2 = t1g.take()
                    e1 = P.op("vector", (lambda a=sbuf_o, d=t1_o,
                        p0=pos0: (lambda e: e.tensor_tensor(d[:, :], a[:, :],
                        cs_sb[:, p0:p0 + 512], ALU.mult)))(),
                        [eva[0]] + rel2, ((dv, 1),), f"qkv0:{which}c{oc}.{tc}")
                    stg.release(i0, [rot_mm, e1[0]])
                    i3, out_o, rel3 = outg.take()
                    e2 = P.op("vector", (lambda a=rot_o, d=out_o,
                        p0=pos0: (lambda e: e.tensor_tensor(d[:, :], a[:, :],
                        sn_sb[:, p0:p0 + 512], ALU.mult)))(),
                        [ebv[0]] + rel3, ((dv, 1),), f"qkv0:{which}s{oc}.{tc}")
                    rtg.release(i1, [e2[0]])
                    e3 = P.op("vector", (lambda a=t1_o, b=out_o: (lambda e:
                        e.tensor_tensor(a[:, :], a[:, :], b[:, :], ALU.add)))(),
                        [e1[0], e2[0]], ((dv, 1),), f"qkv0:{which}a{oc}.{tc}")
                    outg.release(i3, [e3[0]])
                    final = t1_o
                    fev = e3[0]
                    out_slot = i2
                    def relf(evs, _i=i2):
                        t1g.release(_i, evs)
                else:
                    final = sbuf_o
                    fev = eva[0]
                    out_slot = i0
                    def relf(evs, _i=i0):
                        stg.release(_i, evs)
                dst = qsc if which == "q" else ksc
                slot_sem = qkst_sems[out_slot % 3]
                d1 = P.dma("sync", dst[s, 2 * oc, 0:64, pos0:pos0 + 512],
                           final[0:64, :], [fev], f"qkv{layer}:{which}o1",
                           slot_sem)
                d2 = P.dma("sync", dst[s, 2 * oc + 1, 0:64, pos0:pos0 + 512],
                           final[64:128, :], [fev], f"qkv{layer}:{which}o2",
                           slot_sem)
                relf([(slot_sem, slot_sem.n)])

        for q in range(4):
            tt = tc * 4 + q
            iv, vbuf, relv = vst.take()
            evs_half = []
            for half in range(2):
                bank = (tt * 2 + half) % 4
                pb = psf[bank]
                for ic in range(DCH):
                    waits = []
                    if ic == 0:
                        waits = w_ready + xn_ev + bank_ev[bank]
                    P.op("tensor", (lambda l=xn_lhsT(ic, q), r=v_rhs(ic, half),
                        p=pb, st_=(ic == 0), sp=(ic == DCH - 1): (lambda e:
                        e.matmul(p[:], l, r, start=st_, stop=sp)))(),
                        waits, ((mm, 1),) if ic == DCH - 1 else (),
                        f"qkv{layer}:vmm{tt}.{half}")
                mm_ev = (mm, mm.n)
                xn_consumers.append(mm_ev)
                dstv = vbuf[:, half * 520:(half + 1) * 520].rearrange(
                    "p (h j) -> p h j", h=8, j=65)[:, :, 0:64]
                if layer == 0:
                    fn = (lambda p=pb, d=dstv, h=half: (lambda e:
                        e.tensor_tensor(d, p[:], bvB[:, h * 512:(h + 1) * 512],
                                        ALU.add)))()
                    eng = "vector"; sem_ = dv
                else:
                    fn = (lambda p=pb, d=dstv: (lambda e: e.activation(
                        d, p[:], AF.Copy, bias=0.0, scale=1.0)))()
                    eng = "scalar"; sem_ = ac
                w2 = [mm_ev] + relv + ones_ev + (bias_ev if layer == 0 else [])
                eh = P.op(eng, fn, w2, ((sem_, 1),), f"qkv{layer}:vev{tt}.{half}")
                bank_ev[bank] = [eh[0]]
                evs_half.append(eh[0])
                relv = []
            vs_sem = vst_sems[iv % 2]
            dd = P.dma("sync", vsc[tt], vbuf[:, :], evs_half,
                       f"qkv{layer}:vo{tt}", vs_sem)
            vst.release(iv, [dd[0]])
        xnr.release(ix, xn_consumers)

    if layer == 1:
        qpos = C["qpos_row"]
        ar = f32.alloc(4096, "augf")
        ab = bf.alloc(4096, "augb")
        ones_row = bf.alloc(2048, "ones_row")
        P.op("gpsimd", lambda e: e.memset(ones_row[0:1, :], 1.0), (),
             ((su, 1),), "aug:ones")
        one_ev = (su, su.n)
        for h in range(H0):
            for side, sgn in (("k", 1.0), ("q", -1.0)):
                rf = ar[0:1, 0:2048]
                hf = ar[0:1, 2048:4096]
                hb = ab[0:1, 0:2048]
                lb = ab[0:1, 2048:4096]
                sc_ap = slopes_sb[0:1, h:h + 1]
                prev = [(ac, ac.n), (dv, dv.n), (aug_st, aug_st.n),
                        (dsy, dsy.n)]
                P.op("scalar", (lambda r=rf, sa=sc_ap: (lambda e:
                    e.activation(r, qpos[0:1, :], AF.Identity, bias=0.0,
                                 scale=sa)))(), prev, ((ac, 1),),
                    f"aug:{side}{h}a")
                if sgn < 0:
                    P.op("scalar", (lambda r=rf: (lambda e: e.activation(
                        r, r, AF.Copy, bias=0.0, scale=-1.0)))(),
                        [(ac, ac.n)], ((ac, 1),), f"aug:{side}{h}n")
                P.op("scalar", (lambda r=rf, d=hb: (lambda e: e.activation(
                    d, r, AF.Copy, bias=0.0, scale=1.0)))(), [(ac, ac.n)],
                    ((ac, 1),), f"aug:{side}{h}h")
                P.op("scalar", (lambda r2=hb, d=hf: (lambda e: e.activation(
                    d, r2, AF.Copy, bias=0.0, scale=1.0)))(), [(ac, ac.n)],
                    ((ac, 1),), f"aug:{side}{h}hf")
                P.op("vector", (lambda a=rf, b=hf, d=lb: (lambda e:
                    e.tensor_tensor(d, a, b, ALU.subtract)))(),
                    [(ac, ac.n)], ((dv, 1),), f"aug:{side}{h}lo")
                dvev = (dv, dv.n)
                dstt = ksc if side == "k" else qsc
                r0 = 64 if side == "k" else 66
                for s in range(NSEQ):
                    P.dma("sync", dstt[s, h, r0:r0 + 1, :], hb, [dvev],
                          f"aug:{side}{h}s{s}h", aug_st)
                    P.dma("sync", dstt[s, h, r0 + 1:r0 + 2, :], lb, [dvev],
                          f"aug:{side}{h}s{s}l", aug_st)
        for s in range(NSEQ):
            for h in range(H0):
                for r in (66, 67):
                    P.dma("sync", ksc[s, h, r:r + 1, :], ones_row[0:1, :],
                          [one_ev], f"aug:ok{s}.{h}.{r}", aug_st)
                for r in (64, 65):
                    P.dma("sync", qsc[s, h, r:r + 1, :], ones_row[0:1, :],
                          [one_ev], f"aug:oq{s}.{h}.{r}", aug_st)
    fin_waits = [(sx, sx.n) for sx in qkst_sems + vst_sems + [aug_st]]
    P.op("sync", None, fin_waits, (), f"qkv{layer}:fin")
    P.barrier(f"qkv{layer}")


def emit_attn(P, C, layer):
    nc = P.nc
    bf = C["bf"]; f32 = C["f32"]
    IN = C["in"]; SC = C["sc"]
    qsc = SC["q0" if layer == 0 else "q1"]
    ksc = SC["k0" if layer == 0 else "k1"]
    vsc = SC["v0" if layer == 0 else "v1"]
    ssc = SC["s0" if layer == 0 else "s1"]
    KR = 64 if layer == 0 else 68
    mm = P.sem("at_mm")
    ac = P.sem("act"); dv = P.sem("dve")
    ldv = P.sem("at_v")
    q_sems = [P.sem("at_q0"), P.sem("at_q1")]
    k_sems = [P.sem("at_k0"), P.sem("at_k1")]
    s_sems = [P.sem("at_s0"), P.sem("at_s1")]
    su = P.sem("setup"); dsy = P.sem("dma_sync")
    psf = C["ps_f32"]
    bank_ev = C.setdefault("bank_ev", [[] for _ in range(6)])

    woT = bf.alloc(8192, "woT")
    if layer == 0:
        load_weight_T(P, C, IN["L0_attn_wo"], 1024, 1024, woT, "wo")
        bo_pp = f32.alloc(8, "bo")
        per_partition_bias(P, C, IN["L0_attn_bo"], 1024, bo_pp, "bo")
    else:
        load_weight_T(P, C, IN["L1_attn_w_out"], 1024, 1024, woT, "wout")
        bo_pp = None
    wt_ev = P.sem("wt_ev")
    w_ready = [(wt_ev, wt_ev.n), (dsy, dsy.n)]

    v_sb = bf.alloc(NT * 1040, "v_sb")
    for tt in range(NT):
        P.dma("sync", v_sb[:, tt * 1040:(tt + 1) * 1040],
              vsc[tt], (), f"at{layer}:v{tt}", ldv)
    v_ev = [(ldv, ldv.n)]

    def v_tile(s, kt, h):
        col = (s * 16 + kt) * 1040 + h * 65
        return v_sb[:, col:col + 65]

    ones_bb = bf.alloc(64, "ones_bb")
    P.op("gpsimd", lambda e: e.memset(ones_bb[0:2, :], 1.0), [(su, su.n)],
         ((su, 1),), "at:ones")

    den = f32.alloc(1536, "den")
    rec2 = bf.alloc(1024, "rec2")
    recS = f32.alloc(512, "recS")
    attn_fm = bf.alloc(16384, "attn_fm")
    sev = Ring([f32.alloc(512, "soev0")])

    qg = Ring([bf.alloc(2048, f"qg{i}") for i in range(2)])
    kg = Ring([bf.alloc(2048, f"kg{i}") for i in range(2)])

    def load_qk(s, h):
        iq, qb, relq = qg.take()
        ik, kb, relk = kg.take()
        if not relq:
            relq = [(mm, mm.n)]
        if not relk:
            relk = [(mm, mm.n)]
        P.dma("sync", qb[0:KR, :], qsc[s, h, 0:KR, :], relq, f"at:q{s}.{h}",
              q_sems[iq % 2])
        P.dma("sync", kb[0:KR, :], ksc[s, h, 0:KR, :], relk, f"at:k{s}.{h}",
              k_sems[ik % 2])
        return iq, qb, ik, kb, [(q_sems[iq % 2], q_sems[iq % 2].n),
                                (k_sems[ik % 2], k_sems[ik % 2].n)]

    pend = load_qk(0, 0)
    s4r = Ring([f32.alloc(2048, f"s4_{i}") for i in range(2)])
    u4r = Ring([f32.alloc(2048, f"u4_{i}") for i in range(2)])
    a4r = Ring([bf.alloc(2048, f"a4_{i}") for i in range(2)])
    mask_ev = [(su, su.n)]

    if layer == 1:
        negt = f32.alloc(512, "negt")
        nt_u = s4r.s[0][:, 0:512]
        nt_r = s4r.s[0][:, 512:1024]
        nt_a = u4r.s[0][:, 0:512]
        nt_b = u4r.s[0][:, 512:1024]
        atn_neg = bf.alloc(512, "atn_neg")
        P.op("gpsimd", lambda e: e.memset(negt[:, :], -10000.0), [(su, su.n)],
             ((su, 1),), "at:neg0")
        nev = [(su, su.n)]
        P.op("scalar", lambda e: e.activation(nt_u, negt[:, :],
             AF.Square, bias=0.0, scale=1.0), nev, ((ac, 1),), "at:neg1")
        P.op("scalar", lambda e: e.activation(nt_r, nt_u,
             AF.Ln, bias=1.0, scale=1.0), [(ac, ac.n)], ((ac, 1),), "at:neg2")
        P.op("scalar", lambda e: e.activation(nt_r, nt_r,
             AF.Exp, bias=0.0, scale=-0.5), [(ac, ac.n)], ((ac, 1),), "at:neg3")
        P.op("vector", lambda e: e.tensor_tensor(nt_a, nt_r, nt_r, ALU.mult),
             [(ac, ac.n)], ((dv, 1),), "at:nw1")
        P.op("vector", lambda e: e.tensor_tensor(nt_b, nt_a, nt_u, ALU.mult),
             [(dv, dv.n)], ((dv, 1),), "at:nw2")
        P.op("vector", lambda e: e.tensor_tensor(nt_b, nt_b, nt_a, ALU.add),
             [(dv, dv.n)], ((dv, 1),), "at:nw3")
        P.op("vector", lambda e: e.tensor_scalar(nt_b, nt_b, -0.5,
             1.5, ALU.mult, ALU.add), [(dv, dv.n)], ((dv, 1),), "at:nw4")
        P.op("vector", lambda e: e.tensor_tensor(nt_r, nt_r, nt_b, ALU.mult),
             [(dv, dv.n)], ((dv, 1),), "at:nw5")
        P.op("vector", lambda e: e.tensor_tensor(negt[:, :], negt[:, :],
             nt_r, ALU.mult), [(dv, dv.n)], ((dv, 1),), "at:neg4")
        ex9 = P.op("scalar", lambda e: e.activation(atn_neg[:, :], negt[:, :],
             AF.Relu, bias=1.0, scale=1.0), [(dv, dv.n)], ((ac, 1),), "at:neg5")
        neg_ready = [(ac, ac.n)]
        s4r.ev[0] = [(dv, dv.n)]
        u4r.ev[0] = [(dv, dv.n)]

    for s in range(NSEQ):
        fm_guard = [(mm, mm.n)] if s else []
        for h in range(H0):
            iq, qb, ik, kb, qk_ev = pend
            if not (s == NSEQ - 1 and h == H0 - 1):
                nh = h + 1 if h < H0 - 1 else 0
                ns = s if h < H0 - 1 else s + 1
                pend = load_qk(ns, nh)
            q_consumers = []
            for qc in range(4):
                kmax = 4 * (qc + 1)
                nb = qc + 1
                ot_bank = 4
                otp = psf[ot_bank]
                batch_state = []
                rela_by_slot = {}
                for b in range(nb + 1):
                    if b < nb:
                        for j in range(4):
                            kt = 4 * b + j
                            pb = psf[j]
                            P.op("tensor", (lambda
                                l=kb[0:KR, kt * 128:(kt + 1) * 128],
                                r=qb[0:KR, qc * 512:(qc + 1) * 512], p=pb:
                                (lambda e: e.matmul(p[:], l, r, start=True,
                                 stop=True, skip_group_check=True)))(),
                                qk_ev + bank_ev[j] + v_ev + mask_ev,
                                ((mm, 1),), f"at{layer}:s{s}.{h}.{qc}.{kt}")
                            q_consumers.append((mm, mm.n))
                        sc_mm = (mm, mm.n)
                        diag = (b == nb - 1)
                        isf, s4, rels = s4r.take()
                        iu, u4, relu = u4r.take()
                        ia, a4, rela = a4r.take()
                        if layer == 1:
                            rela_by_slot[ia] = rela
                        if layer == 0:
                            ee = None
                            for j in range(4):
                                ee = P.op("scalar", (lambda p=psf[j],
                                    d=a4[:, j * 512:(j + 1) * 512]: (lambda e:
                                    e.activation(d, p[:], AF.Exp, bias=0.0,
                                    scale=0.125)))(),
                                    ([sc_mm] + rela + rels + relu) if j == 0
                                    else [sc_mm], ((ac, 1),),
                                    f"at0:e{s}.{h}.{qc}.{b}.{j}")
                                bank_ev[j] = [ee[0]]
                            aev = ee[0]
                            if diag:
                                em = P.op("gpsimd", (lambda t=a4: (lambda e:
                                    e.affine_select(t[:, :], t[:, :],
                                    pattern=[[-128, 4], [1, 512]],
                                    compare_op=ALU.is_ge, fill=C["fill0"],
                                    base=0, channel_multiplier=-1)))(),
                                    [aev], ((su, 1),),
                                    f"at0:m{s}.{h}.{qc}.{b}")
                                aev = em[0]
                            s4r.release(isf, [aev])
                            u4r.release(iu, [aev])
                            batch_state.append((a4, ia, aev, b))
                        else:
                            eev = None
                            for j in range(4):
                                eev = P.op("vector", (lambda p=psf[j],
                                    d=s4[:, j * 512:(j + 1) * 512]:
                                    (lambda e: e.tensor_copy(d, p[:])))(),
                                    ([sc_mm] + rels) if j == 0 else [sc_mm],
                                    ((dv, 1),),
                                    f"at1:cp{s}.{h}.{qc}.{b}.{j}")
                                bank_ev[j] = [eev[0]]
                            s_ev = eev[0]
                            if diag:
                                ems = P.op("gpsimd", (lambda t=s4: (lambda e:
                                    e.affine_select(t[:, :], t[:, :],
                                    pattern=[[-128, 4], [1, 512]],
                                    compare_op=ALU.is_ge, fill=C["filln"],
                                    base=0, channel_multiplier=-1)))(),
                                    [s_ev], ((su, 1),),
                                    f"at1:mn{s}.{h}.{qc}.{b}")
                                s_ev = ems[0]
                            eu = P.op("scalar", (lambda a=s4, d=u4: (lambda e:
                                e.activation(d[:, :], a[:, :], AF.Square,
                                bias=0.0, scale=1.0)))(), [s_ev] + relu,
                                ((ac, 1),), f"at1:sq{s}.{h}.{qc}.{b}")
                            el = P.op("scalar", (lambda d=u4: (lambda e:
                                e.activation(d[:, :], d[:, :], AF.Ln, bias=1.0,
                                scale=1.0)))(), [eu[0]], ((ac, 1),),
                                f"at1:ln{s}.{h}.{qc}.{b}")
                            er = P.op("scalar", (lambda d=u4: (lambda e:
                                e.activation(d[:, :], d[:, :], AF.Exp,
                                bias=0.0, scale=-0.5)))(), [el[0]], ((ac, 1),),
                                f"at1:ex{s}.{h}.{qc}.{b}")
                            et = P.op("vector", (lambda a=s4, b_=u4: (lambda e:
                                e.tensor_tensor(b_[:, :], a[:, :], b_[:, :],
                                ALU.mult)))(), [er[0]], ((dv, 1),),
                                f"at1:t{s}.{h}.{qc}.{b}")
                            s4r.release(isf, [et[0]])
                            batch_state.append((a4, ia, u4, iu, et[0], b, diag))

                    if batch_state and (b == nb or len(batch_state) > 1):
                        stt = batch_state.pop(0)
                        if layer == 0:
                            a4p, iap, evp, bp = stt
                        else:
                            a4p, iap, u4p, iup, etp, bp, diagp = stt
                            ep = P.op("scalar", (lambda a=u4p, d=a4p:
                                (lambda e: e.activation(d[:, :], a[:, :],
                                AF.Relu, bias=1.0, scale=1.0)))(),
                                [etp] + rela_by_slot.get(iap, []), ((ac, 1),),
                                f"at1:p{s}.{h}.{qc}.{bp}")
                            u4r.release(iup, [ep[0]])
                            evp = ep[0]
                            if diagp:
                                em = P.op("gpsimd", (lambda t=a4p: (lambda e:
                                    e.affine_select(t[:, :], t[:, :],
                                    pattern=[[-128, 4], [1, 512]],
                                    compare_op=ALU.is_ge, fill=C["fill0"],
                                    base=0, channel_multiplier=-1)))(),
                                    [evp], ((su, 1),),
                                    f"at1:z{s}.{h}.{qc}.{bp}")
                                evp = em[0]
                        emv = None
                        last_b = (bp == nb - 1) and (layer == 0 or kmax == 16)
                        for j in range(4):
                            kt = 4 * bp + j
                            emv = P.op("tensor", (lambda
                                l=v_tile(s, kt, h),
                                r=a4p[:, j * 512:(j + 1) * 512], p=otp,
                                st_=(kt == 0), sp=(last_b and j == 3):
                                (lambda e: e.matmul(
                                p[0:65, :], l, r, start=st_, stop=sp,
                                skip_group_check=True)))(),
                                ([evp] + (bank_ev[ot_bank] if kt == 0 else [])
                                 + v_ev), ((mm, 1),),
                                f"at{layer}:av{s}.{h}.{qc}.{kt}")
                        a4r.release(iap, [emv[0]])
                if layer == 1:
                    for kt in range(kmax, 16):
                        P.op("tensor", (lambda l=v_tile(s, kt, h), p=otp,
                            sp=(kt == 15): (lambda e: e.matmul(
                            p[0:65, :], l, atn_neg[:, :],
                            start=False, stop=sp,
                            skip_group_check=True)))(),
                            neg_ready + v_ev, ((mm, 1),),
                            f"at1:avm{s}.{h}.{qc}.{kt}")
                ot_mm = (mm, mm.n)
                e1 = P.op("scalar", (lambda p=otp: (lambda e: e.activation(
                    den[0:1, 0:512], p[64:65, 0:512], AF.Copy, bias=0.0,
                    scale=1.0)))(), [ot_mm, (dv, dv.n)], ((ac, 1),),
                    f"at{layer}:dn{s}.{h}.{qc}")
                if layer == 1:
                    e1 = P.op("vector", lambda e: e.tensor_scalar(
                        den[0:1, 0:512], den[0:1, 0:512], 2e-6, None, ALU.add),
                        [e1[0]], ((dv, 1),), f"at1:eps{s}.{h}.{qc}")
                e2 = P.op("vector", lambda e: e.reciprocal(den[0:1, 512:1024],
                    den[0:1, 0:512]), [e1[0]], ((dv, 1),),
                    f"at{layer}:rc{s}.{h}.{qc}")
                e3 = P.op("scalar", lambda e: e.activation(rec2[0:1, 0:512],
                    den[0:1, 512:1024], AF.Copy, bias=0.0, scale=1.0),
                    [e2[0]] + bank_ev[2], ((ac, 1),), f"at{layer}:rh{s}.{h}.{qc}")
                e4 = P.op("scalar", lambda e: e.activation(den[0:1, 1024:1536],
                    rec2[0:1, 0:512], AF.Copy, bias=0.0, scale=1.0),
                    [e3[0]], ((ac, 1),), f"at{layer}:rhf{s}.{h}.{qc}")
                e5 = P.op("vector", lambda e: e.tensor_tensor(rec2[0:1,
                    512:1024], den[0:1, 512:1024], den[0:1, 1024:1536],
                    ALU.subtract), [e4[0]], ((dv, 1),),
                    f"at{layer}:rl{s}.{h}.{qc}")
                rb = psf[2]
                P.op("tensor", lambda e: e.matmul(rb[0:64, 0:512],
                    ones_bb[0:1, 0:64], rec2[0:1, 0:512], start=True,
                    stop=False, skip_group_check=True),
                    [e3[0]] + bank_ev[2], ((mm, 1),), f"at{layer}:b1{s}.{h}.{qc}")
                P.op("tensor", lambda e: e.matmul(rb[0:64, 0:512],
                    ones_bb[0:1, 0:64], rec2[0:1, 512:1024], start=False,
                    stop=True, skip_group_check=True),
                    [e5[0]], ((mm, 1),), f"at{layer}:b2{s}.{h}.{qc}")
                bmv = (mm, mm.n)
                e6 = P.op("scalar", lambda e: e.activation(recS[0:64, 0:512],
                    rb[0:64, 0:512], AF.Copy, bias=0.0, scale=1.0),
                    [bmv, (dv, dv.n)], ((ac, 1),), f"at{layer}:re{s}.{h}.{qc}")
                bank_ev[2] = [e6[0]]
                dstp = attn_fm[64 * (h % 2):64 * (h % 2) + 64,
                               (h // 2) * 2048 + qc * 512:
                               (h // 2) * 2048 + qc * 512 + 512]
                e7 = P.op("vector", (lambda p=otp, d=dstp: (lambda e:
                    e.tensor_tensor(d, p[0:64, 0:512], recS[0:64, 0:512],
                    ALU.mult)))(), [e6[0]] + fm_guard, ((dv, 1),),
                    f"at{layer}:o{s}.{h}.{qc}")
                fm_guard = []
                bank_ev[ot_bank] = [e7[0]]
            qg.release(iq, [q_consumers[-1]])
            kg.release(ik, [q_consumers[-1]])
        fm_ev = [(dv, dv.n)]
        for oc in range(8):
            for tc2 in range(4):
                bank = 4 + (oc * 4 + tc2) % 2
                pb = psf[bank]
                for ic in range(DCH):
                    waits = []
                    if ic == 0:
                        waits = w_ready + fm_ev + bank_ev[bank]
                    P.op("tensor", (lambda l=woT[:, ic * 1024 + oc * 128:
                        ic * 1024 + (oc + 1) * 128],
                        r=attn_fm[:, ic * 2048 + tc2 * 512:
                                  ic * 2048 + tc2 * 512 + 512],
                        p=pb, st_=(ic == 0), sp=(ic == DCH - 1):
                        (lambda e: e.matmul(p[:], l, r, start=st_, stop=sp,
                         skip_group_check=True)))(), waits,
                        ((mm, 1),) if ic == DCH - 1 else (),
                        f"at{layer}:wo{s}.{oc}.{tc2}")
                wo_mm = (mm, mm.n)
                iw, wbuf, relw = sev.take()
                if layer == 0:
                    fn = (lambda p=pb, d=wbuf, o=oc: (lambda e: e.activation(
                        d[:, :], p[:], AF.Identity, bias=bo_pp[:, o:o + 1],
                        scale=1.0)))()
                else:
                    fn = (lambda p=pb, d=wbuf: (lambda e: e.activation(
                        d[:, :], p[:], AF.Copy, bias=0.0, scale=1.0)))()
                ew = P.op("scalar", fn, [wo_mm] + relw, ((ac, 1),),
                          f"at{layer}:we{s}.{oc}.{tc2}")
                bank_ev[bank] = [ew[0]]
                ssem = s_sems[iw % 2]
                dd = P.dma("scalar", ssc[oc][:, s * 2048 + tc2 * 512:
                           s * 2048 + tc2 * 512 + 512], wbuf[:, :], [ew[0]],
                           f"at{layer}:ws{s}.{oc}.{tc2}", ssem)
                sev.release(iw, [(ssem, ssem.n)])
    P.op("sync", None, [(s_sems[0], s_sems[0].n), (s_sems[1], s_sems[1].n),
         (P.sem("dma_scalar"), P.sem("dma_scalar").n)], (), f"at{layer}:fin")
    P.barrier(f"at{layer}")


def emit_moe0(P, C):
    nc = P.nc
    bf = C["bf"]; f32 = C["f32"]
    IN = C["in"]; SC = C["sc"]
    w1 = IN["L0_ffn_w1"]; w2 = IN["L0_ffn_w2"]
    ld = P.sem("moe_lds"); mm = P.sem("moe_mm")
    cw_sem = P.sem("moe_cw")
    w1_sems = [P.sem("moe_w10"), P.sem("moe_w11")]
    w2_sems = [P.sem("moe_w20"), P.sem("moe_w21")]
    ac = P.sem("act"); dv = P.sem("dve"); st = P.sem("moe_st")
    su = P.sem("setup"); dsy = P.sem("dma_sync"); dgp = P.sem("dma_gpsimd")
    dsc = P.sem("dma_scalar")
    psf = C["ps_f32"]
    bank_ev = C.setdefault("bank_ev", [[] for _ in range(6)])

    xn_sb = Ring([bf.alloc(8192, "mxn0")])
    w1r = Ring([bf.alloc(8192, f"mw1{i}") for i in range(2)])
    w2r = Ring([bf.alloc(4096, f"mw2{i}") for i in range(2)])
    h_sb = bf.alloc(32768, "h_sb")
    y_acc = f32.alloc(8192, "y_acc")
    cwB = Ring([f32.alloc(1024, "cwB0")])
    gtmp = f32.alloc(1024, "gtmp")
    ytmp = gtmp[:, 0:512]
    stmp = f32.alloc(1024, "stmp")
    xtmp = f32.alloc(1024, "xtmp")

    xn0 = SC["xn0"]
    for ts in range(4):
        tsl = slice(ts * 1024, (ts + 1) * 1024)
        ixn, xn_ts, relx = xn_sb.take()
        for c in range(DCH):
            P.dma("sync", xn_ts[:, c * 1024:(c + 1) * 1024],
                  xn0[c][:, tsl], relx if c == 0 else (), f"m0:xn{ts}.{c}", ld)
        xn_ev = [(ld, ld.n)]
        cw_done = []
        ez = P.op("gpsimd", lambda e: e.memset(y_acc[:, :], 0.0),
                  [(dv, dv.n), (ac, ac.n)], ((su, 1),), f"m0:z{ts}")
        acc_ev = [(su, su.n)]
        xn_consumers = []
        for ex in range(E0):
            icb, cwb, relc = cwB.take()
            P.dma("gpsimd", cwb[:, :],
                  SC["cw0"][ex:ex + 1, tsl].partition_broadcast(128),
                  cw_done + relc, f"m0:cwb{ts}.{ex}", cw_sem)
            cw_ev = [(cw_sem, cw_sem.n)]
            h_done = []
            for hb in range(4):
                iw, w1b, rel1 = w1r.take()
                src = w1[ex][:, hb * 1024:(hb + 1) * 1024].rearrange(
                    "(ic p) c -> p ic c", p=128)
                w1s = w1_sems[iw % 2]
                P.dma("gpsimd", w1b[:, :].rearrange("p (ic c) -> p ic c", ic=8),
                      src, rel1, f"m0:w1{ts}.{ex}.{hb}", w1s)
                w1_ev = (w1s, w1s.n)
                for ht in range(8):
                    for tcl in range(2):
                        bank = (ht * 2 + tcl) % 2
                        pb = psf[bank]
                        for ic in range(DCH):
                            P.op("tensor", (lambda l=w1b[:, ic * 1024 + ht * 128:
                                ic * 1024 + ht * 128 + 128],
                                r=xn_ts[:, ic * 1024 + tcl * 512:
                                        ic * 1024 + tcl * 512 + 512],
                                p=pb, st_=(ic == 0), sp=(ic == DCH - 1):
                                (lambda e: e.matmul(p[:], l, r, start=st_,
                                 stop=sp, skip_group_check=True)))(),
                                ([w1_ev] + xn_ev + bank_ev[bank]) if ic == 0 else (),
                                ((mm, 1),) if ic == DCH - 1 else (),
                                f"m0:h{ts}.{ex}.{hb}.{ht}.{tcl}")
                        h_mm = (mm, mm.n)
                        if ic == DCH - 1:
                            xn_consumers.append(h_mm)
                        dst = h_sb[:, (hb * 8 + ht) * 1024 + tcl * 512:
                                   (hb * 8 + ht) * 1024 + tcl * 512 + 512]
                        eg = P.op("scalar", (lambda p=pb, d=dst: (lambda e:
                            e.activation(d, p[:], AF.Gelu, bias=0.0,
                            scale=1.0)))(), [h_mm], ((ac, 1),),
                            f"m0:g{ts}.{ex}.{hb}.{ht}.{tcl}")
                        bank_ev[bank] = [eg[0]]
                        h_done.append(eg[0])
                w1r.release(iw, [h_done[-1]])
            h_ev = [(ac, ac.n)]
            if DEBUG_MOE == 'h' and ts == 0 and ex == 0:
                for hcd in range(32):
                    P.dma("scalar", SC["dbg"][hcd],
                          h_sb[:, hcd * 1024:(hcd + 1) * 1024], h_ev,
                          f"dbg{hcd}", st)
            for dt in range(8):
                iw2, w2b, rel2 = w2r.take()
                src = w2[ex][:, dt * 128:(dt + 1) * 128].rearrange(
                    "(hc p) c -> p hc c", p=128)
                w2s = w2_sems[iw2 % 2]
                P.dma("gpsimd", w2b[:, :].rearrange("p (hc c) -> p hc c", hc=32),
                      src, rel2, f"m0:w2{ts}.{ex}.{dt}", w2s)
                w2_ev = (w2s, w2s.n)
                y_mms = []
                for tcl in range(2):
                    bank = 2 + (dt * 2 + tcl) % 2
                    pb = psf[bank]
                    for hc in range(32):
                        P.op("tensor", (lambda l=w2b[:, hc * 128:(hc + 1) * 128],
                            r=h_sb[:, hc * 1024 + tcl * 512:
                                   hc * 1024 + tcl * 512 + 512],
                            p=pb, st_=(hc == 0), sp=(hc == 31): (lambda e:
                            e.matmul(p[:], l, r, start=st_, stop=sp,
                             skip_group_check=True)))(),
                            ([w2_ev] + h_ev + bank_ev[bank]) if hc == 0 else (),
                            ((mm, 1),) if hc == 31 else (),
                            f"m0:y{ts}.{ex}.{dt}.{tcl}")
                    y_mm = (mm, mm.n)
                    y_mms.append(y_mm)
                    et = P.op("vector", (lambda p=pb,
                        c_=cwb[:, tcl * 512:(tcl + 1) * 512]: (lambda e:
                        e.tensor_tensor(ytmp, p[:], c_, ALU.mult)))(),
                        [y_mm, (st, st.n)] + cw_ev + acc_ev + [(dv, dv.n)],
                        ((dv, 1),), f"m0:ym{ts}.{ex}.{dt}.{tcl}")
                    ea = P.op("vector", (lambda d=y_acc[:, dt * 1024 + tcl * 512:
                        dt * 1024 + tcl * 512 + 512]: (lambda e:
                        e.tensor_tensor(d, d, ytmp, ALU.add)))(),
                        [(dv, dv.n)], ((dv, 1),), f"m0:ya{ts}.{ex}.{dt}.{tcl}")
                    bank_ev[bank] = [ea[0]]
                w2r.release(iw2, y_mms)
            cwB.release(icb, [(P.sems["dve"], P.sems["dve"].n)])
            if DEBUG_MOE and ts == 0 and ex < 4:
                for dtd in range(8):
                    P.dma("scalar", SC["dbg"][ex * 8 + dtd],
                          y_acc[:, dtd * 1024:(dtd + 1) * 1024],
                          [(dv, dv.n)], f"dbgy{ex}.{dtd}", st)
        xn_sb.release(ixn, xn_consumers[-8:])
        for dt in range(8):
            esl = P.op("scalar", (lambda s_=y_acc[:, dt * 1024:(dt + 1) * 1024]:
                (lambda e: e.activation(gtmp[:, :], s_, AF.Silu, bias=0.0,
                scale=1.0)))(), [(dv, dv.n), (st, st.n)], ((ac, 1),),
                f"m0:si{ts}.{dt}")
            cmb = P.sem("moe_cmb")
            P.dma("sync", stmp[:, :], SC["s0"][dt][:, tsl], [(dv, dv.n)],
                  f"m0:ls{ts}.{dt}", cmb)
            P.dma("sync", xtmp[:, :], SC["x0T"][dt][:, tsl], [(dv, dv.n)],
                  f"m0:lx{ts}.{dt}", cmb)
            lev = (cmb, cmb.n)
            em = P.op("vector", lambda e: e.tensor_tensor(gtmp[:, :], gtmp[:, :],
                stmp[:, :], ALU.mult), [esl[0], lev], ((dv, 1),),
                f"m0:sg{ts}.{dt}")
            ex2 = P.op("vector", lambda e: e.tensor_tensor(gtmp[:, :], gtmp[:, :],
                xtmp[:, :], ALU.add), [(dv, dv.n)], ((dv, 1),),
                f"m0:x1{ts}.{dt}")
            P.dma("scalar", SC["x1T"][dt][:, tsl], gtmp[:, :], [ex2[0]],
                  f"m0:so{ts}.{dt}", st)
    P.op("sync", None, [(st, st.n), (dsc, dsc.n)], (), "m0:fin")
    P.barrier("moe0")


def emit_p3_attn0(P, C):
    emit_attn(P, C, 0)


def emit_p7_attn1(P, C):
    emit_attn(P, C, 1)


def emit_p2_qkv0(P, C):
    emit_qkv(P, C, 0)


def emit_p6_qkv1(P, C):
    emit_qkv(P, C, 1)


def emit_ln1(P, C):
    nc = P.nc
    bf = C["bf"]; f32 = C["f32"]
    IN = C["in"]; SC = C["sc"]
    ld = P.sem("l1_ld"); ldg = P.sem("l1_ldg"); mm = P.sem("l1_mm")
    ac = P.sem("act"); dv = P.sem("dve"); st = P.sem("l1_st")
    su = P.sem("setup"); dsy = P.sem("dma_sync")
    psf = C["ps_f32"]
    bank_ev = C.setdefault("bank_ev", [[] for _ in range(6)])
    idf = C["ident_f32"]

    ones_col = bf.alloc(1, "ones_col")
    P.op("gpsimd", lambda e: e.memset(ones_col[:, 0:1], 1.0), (), ((su, 1),),
         "l1:ones")
    g_pp = f32.alloc(8, "g_pp"); b_pp = f32.alloc(8, "b_pp")
    per_partition_bias(P, C, IN["L1_ffn_ln_g"], 1024, g_pp, "lng")
    per_partition_bias(P, C, IN["L1_ffn_ln_b"], 1024, b_pp, "lnb")
    gw_sb = f32.alloc(48, "gw")
    for c in range(DCH):
        P.dma("sync", gw_sb[:, c * 6:(c + 1) * 6],
              IN["L1_ffn_gate_w"][c * 128:(c + 1) * 128, :], (), f"l1:gw{c}")
    gbB = f32.alloc(6, "gbB")
    broadcast_row(P, C, IN["L1_ffn_gate_b"], 6, gbB, "gbB")
    w_ev = [(dsy, dsy.n), (su, su.n)]

    x1c = f32.alloc(4096, "x1c")
    xb = bf.alloc(4096, "xb")
    sq = bf.alloc(512, "sq")
    rows = f32.alloc(4 * 512, "rows")
    z2c = f32.alloc(4096, "z2c")
    bcA = f32.alloc(512, "bcA")
    bcB = f32.alloc(512, "bcB")
    rt_scr = f32.alloc(24, "g_scr")
    rt_cols = f32.alloc(8, "g_cols")
    cwT_sb = f32.alloc(512 * 2, "cw1T")
    lnrows = C["sc"]["lnrows"]

    def rowslice(i):
        return rows[0:1, i * 512:(i + 1) * 512]

    for tc in range(8):
        tsl = slice(tc * 512, (tc + 1) * 512)
        for c in range(DCH):
            P.dma("sync", x1c[:, c * 512:(c + 1) * 512],
                  SC["x1T"][c][:, tsl],
                  [(mm, mm.n), (ac, ac.n), (dv, dv.n), (st, st.n)] if c == 0 else (),
                  f"l1:x{tc}.{c}", ld)
        lev = [(ld, ld.n)]
        for phase in range(2):
            for c in range(DCH):
                P.op("vector", (lambda cc=c: (lambda e: e.tensor_copy(
                    xb[:, cc * 512:(cc + 1) * 512],
                    x1c[:, cc * 512:(cc + 1) * 512])))(),
                    (lev + [(mm, mm.n), (st, st.n)]) if c == 0 else (), ((dv, 1),),
                    f"l1:cb{phase}.{tc}.{c}")
            cb_ev = [(dv, dv.n)]
            pb = psf[4]
            for c in range(DCH):
                P.op("tensor", (lambda cc=c: (lambda e: e.matmul(
                    pb[0:1, 0:512], ones_col[:, 0:1],
                    xb[:, cc * 512:(cc + 1) * 512], start=(cc == 0),
                    stop=(cc == DCH - 1), skip_group_check=True)))(),
                    (cb_ev + w_ev + bank_ev[4]) if c == 0 else (),
                    ((mm, 1),) if c == DCH - 1 else (), f"l1:sx{phase}.{tc}")
            sx_mm = (mm, mm.n)
            pb2 = psf[5]
            for c in range(DCH):
                eq = P.op("vector", (lambda cc=c: (lambda e: e.tensor_tensor(
                    sq[:, :], xb[:, cc * 512:(cc + 1) * 512],
                    xb[:, cc * 512:(cc + 1) * 512], ALU.mult)))(),
                    [(mm, mm.n), (dv, dv.n)], ((dv, 1),),
                    f"l1:sq{phase}.{tc}.{c}")
                P.op("tensor", (lambda cc=c: (lambda e: e.matmul(
                    pb2[0:1, 0:512], ones_col[:, 0:1], sq[:, :],
                    start=(cc == 0), stop=(cc == DCH - 1),
                    skip_group_check=True)))(),
                    ([eq[0]] + bank_ev[5]) if c == 0 else [eq[0]],
                    ((mm, 1),), f"l1:s2{phase}.{tc}.{c}")
            s2_mm = (mm, mm.n)
            e1 = P.op("scalar", lambda e: e.activation(rowslice(0),
                pb[0:1, 0:512], AF.Copy, bias=0.0, scale=1.0 / D),
                [sx_mm, (dv, dv.n)], ((ac, 1),), f"l1:mu{phase}.{tc}")
            bank_ev[4] = [e1[0]]
            e2 = P.op("scalar", lambda e: e.activation(rowslice(1),
                pb2[0:1, 0:512], AF.Copy, bias=0.0, scale=1.0 / D),
                [s2_mm, (ac, ac.n)], ((ac, 1),), f"l1:m2{phase}.{tc}")
            bank_ev[5] = [e2[0]]
            P.op("vector", lambda e: e.scalar_tensor_tensor(rowslice(2),
                rowslice(0), 1.0, rowslice(0), ALU.mult, ALU.mult),
                [e2[0], (st, st.n)], ((dv, 1),), f"l1:v0{phase}.{tc}")
            P.op("vector", lambda e: e.tensor_tensor(rowslice(2), rowslice(1),
                rowslice(2), ALU.subtract), [(dv, dv.n)], ((dv, 1),),
                f"l1:v1{phase}.{tc}")
            P.op("vector", lambda e: e.tensor_scalar(rowslice(2), rowslice(2),
                float(LN_EPS), None, ALU.add), [(dv, dv.n)], ((dv, 1),),
                f"l1:v2{phase}.{tc}")
            e3 = P.op("scalar", lambda e: e.activation(rowslice(2), rowslice(2),
                AF.Ln, bias=0.0, scale=1.0), [(dv, dv.n)], ((ac, 1),),
                f"l1:v3{phase}.{tc}")
            e4 = P.op("scalar", lambda e: e.activation(rowslice(2), rowslice(2),
                AF.Exp, bias=0.0, scale=-0.5), [e3[0]], ((ac, 1),),
                f"l1:v4{phase}.{tc}")
            e5 = P.op("vector", lambda e: e.scalar_tensor_tensor(rowslice(3),
                rowslice(0), -1.0, rowslice(2), ALU.mult, ALU.mult),
                [e4[0]], ((dv, 1),), f"l1:v5{phase}.{tc}")
            d1 = P.dma("scalar", lnrows[0:1, tsl], rowslice(2), [e5[0]],
                       f"l1:r2{phase}.{tc}", st)
            d2 = P.dma("scalar", lnrows[1:2, tsl], rowslice(3), [e5[0]],
                       f"l1:r3{phase}.{tc}", st)
            b1 = P.dma("gpsimd", bcA[:, :],
                       lnrows[0:1, tsl].partition_broadcast(128),
                       [d2[0], (dv, dv.n)], f"l1:bA{phase}.{tc}", ldg)
            b2 = P.dma("gpsimd", bcB[:, :],
                       lnrows[1:2, tsl].partition_broadcast(128),
                       [d2[0], (dv, dv.n)], f"l1:bB{phase}.{tc}", ldg)
            bev = [(ldg, ldg.n)]
            zdst = x1c if phase == 0 else z2c
            for c in range(DCH):
                P.op("vector", (lambda cc=c, zz=zdst: (lambda e:
                    e.tensor_tensor(zz[:, cc * 512:(cc + 1) * 512],
                    x1c[:, cc * 512:(cc + 1) * 512], bcA[:, :], ALU.mult)))(),
                    bev if c == 0 else (), ((dv, 1),), f"l1:n0{phase}.{tc}.{c}")
                P.op("vector", (lambda cc=c, zz=zdst: (lambda e:
                    e.tensor_tensor(zz[:, cc * 512:(cc + 1) * 512],
                    zz[:, cc * 512:(cc + 1) * 512], bcB[:, :], ALU.add)))(),
                    [(dv, dv.n)], ((dv, 1),), f"l1:n1{phase}.{tc}.{c}")
            nev = [(dv, dv.n)]
            if phase == 0:
                for c in range(DCH):
                    P.dma("scalar", SC["z1f"][c][:, tsl],
                          x1c[:, c * 512:(c + 1) * 512], nev, f"l1:zf{tc}.{c}",
                          st)
                for c in range(DCH):
                    eb = P.op("scalar", (lambda cc=c: (lambda e: e.activation(
                        xb[:, cc * 512:(cc + 1) * 512],
                        x1c[:, cc * 512:(cc + 1) * 512], AF.Copy, bias=0.0,
                        scale=1.0)))(), nev + [(mm, mm.n)], ((ac, 1),),
                        f"l1:zb{tc}.{c}")
                    P.dma("scalar", SC["zn1"][c][:, tsl],
                          xb[:, c * 512:(c + 1) * 512], [eb[0]],
                          f"l1:zn{tc}.{c}", st)
            else:
                for c in range(DCH):
                    ea2 = P.op("vector", (lambda cc=c: (lambda e:
                        e.tensor_scalar(z2c[:, cc * 512:(cc + 1) * 512],
                        z2c[:, cc * 512:(cc + 1) * 512], g_pp[:, cc:cc + 1],
                        b_pp[:, cc:cc + 1], ALU.mult, ALU.add)))(),
                        [(dv, dv.n)], ((dv, 1),), f"l1:af{tc}.{c}")
                    eb2 = P.op("scalar", (lambda cc=c: (lambda e: e.activation(
                        xb[:, cc * 512:(cc + 1) * 512],
                        z2c[:, cc * 512:(cc + 1) * 512], AF.Copy, bias=0.0,
                        scale=1.0)))(), [ea2[0]], ((ac, 1),), f"l1:ab{tc}.{c}")
                    P.dma("scalar", SC["zn2"][c][:, tsl],
                          xb[:, c * 512:(c + 1) * 512], [eb2[0]],
                          f"l1:z2{tc}.{c}", st)
        for q in range(4):
            pb = psf[4]
            for c in range(DCH):
                P.op("tensor", (lambda cc=c, qq=q: (lambda e: e.matmul(
                    pb[0:128, 0:6], z2c[:, cc * 512 + qq * 128:
                                        cc * 512 + qq * 128 + 128],
                    gw_sb[:, cc * 6:(cc + 1) * 6], start=(cc == 0),
                    stop=(cc == DCH - 1), skip_group_check=True)))(),
                    ([(dv, dv.n), (ac, ac.n)] + w_ev + bank_ev[4]) if c == 0 else (),
                    ((mm, 1),) if c == DCH - 1 else (), f"l1:gt{tc}.{q}")
            gt_mm = (mm, mm.n)
            cw_ap, cw_ev = emit_cw_gate_biased(P, C, pb, gt_mm, rt_scr, rt_cols,
                                               gbB, f"l1g{tc}.{q}")
            bank_ev[4] = [cw_ev]
            P.op("tensor", (lambda: (lambda e: e.transpose(psf[5][0:6, 0:128],
                cw_ap, idf[:])))(), [cw_ev] + bank_ev[5], ((mm, 1),),
                f"l1:ct{tc}.{q}")
            e9 = P.op("vector", (lambda qq=q: (lambda e: e.tensor_copy(
                cwT_sb[0:6, qq * 128:(qq + 1) * 128], psf[5][0:6, 0:128])))(),
                [(mm, mm.n), (st, st.n)], ((dv, 1),), f"l1:ce{tc}.{q}")
            bank_ev[5] = [e9[0]]
        P.dma("scalar", C["sc"]["cw1"][:, tsl], cwT_sb[0:6, 0:512],
              [(P.sems["dve"], P.sems["dve"].n)], f"l1:cs{tc}", st)
    P.op("sync", None, [(st, st.n), (P.sem("dma_scalar"), P.sem("dma_scalar").n)],
         (), "l1:fin")
    P.barrier("ln1")


def emit_cw_gate_biased(P, C, pb, mm_ev, scr, cols, gbB, tag):
    ac = P.sem("act"); dv = P.sem("dve")
    ne = 6
    ll = scr[:, 0:ne]; eqt = scr[:, 8:8 + ne]; cw = scr[:, 16:16 + ne]
    c_m1 = cols[:, 1:2]; c_m2 = cols[:, 2:3]
    c_d = cols[:, 3:4]; c_r = cols[:, 4:5]; c_nm = cols[:, 5:6]
    e0 = P.op("vector", (lambda p=pb: (lambda e: e.tensor_tensor(ll,
        p[0:128, 0:ne], gbB[:, 0:ne], ALU.add)))(),
        [mm_ev, (dv, dv.n), (ac, ac.n)], ((dv, 1),), tag + ":b")
    P.op("vector", lambda e: e.reduce_max(c_m1, ll, axis=AX.X),
         [e0[0]], ((dv, 1),), tag + ":m1")
    P.op("vector", lambda e: e.tensor_scalar(eqt, ll, c_m1, None, ALU.is_ge),
         [(dv, dv.n)], ((dv, 1),), tag + ":eq")
    P.op("vector", lambda e: e.tensor_scalar(eqt, eqt, -1e9, None, ALU.mult),
         [(dv, dv.n)], ((dv, 1),), tag + ":sc")
    P.op("vector", lambda e: e.tensor_tensor(eqt, eqt, ll, ALU.add),
         [(dv, dv.n)], ((dv, 1),), tag + ":lm")
    P.op("vector", lambda e: e.reduce_max(c_m2, eqt, axis=AX.X),
         [(dv, dv.n)], ((dv, 1),), tag + ":m2")
    P.op("vector", lambda e: e.tensor_scalar(c_nm, c_m1, -1.0, None, ALU.mult),
         [(dv, dv.n)], ((dv, 1),), tag + ":nm")
    P.op("vector", lambda e: e.tensor_tensor(c_d, c_m2, c_nm, ALU.add),
         [(dv, dv.n)], ((dv, 1),), tag + ":dm")
    e2 = P.op("scalar", lambda e: e.activation(c_d, c_d, AF.Exp, bias=0.0,
         scale=1.0), [(dv, dv.n)], ((ac, 1),), tag + ":ex")
    P.op("vector", lambda e: e.tensor_scalar(c_d, c_d, 1.0, None, ALU.add),
         [e2[0]], ((dv, 1),), tag + ":p1")
    P.op("vector", lambda e: e.reciprocal(c_r, c_d), [(dv, dv.n)],
         ((dv, 1),), tag + ":rc")
    P.op("vector", lambda e: e.tensor_scalar(eqt, ll, c_m2, None, ALU.is_ge),
         [(dv, dv.n)], ((dv, 1),), tag + ":sel")
    e3 = P.op("scalar", (lambda: (lambda e: e.activation(ll, ll, AF.Exp,
        bias=c_nm, scale=1.0)))(), [(dv, dv.n)], ((ac, 1),), tag + ":z")
    P.op("vector", lambda e: e.tensor_tensor(cw, eqt, ll, ALU.mult),
         [e3[0]], ((dv, 1),), tag + ":cwz")
    ecw = P.op("vector", lambda e: e.tensor_scalar(cw, cw, c_r, None, ALU.mult),
         [(dv, dv.n)], ((dv, 1),), tag + ":cw")
    return cw, ecw[0]


def _ffn1(P, C, n_exp, hid, w1_of, w2_of, b1_vecs, b2_vecs, cw_name, out_name,
          out_f32, tag):
    nc = P.nc
    bf = C["bf"]; f32 = C["f32"]
    IN = C["in"]; SC = C["sc"]
    ld = P.sem("f1_lds"); mm = P.sem("f1_mm")
    cw_sem = P.sem(tag + "cws")
    w1_sems = [P.sem(tag + "w1s0"), P.sem(tag + "w1s1")]
    w2_sems = [P.sem(tag + "w2s0"), P.sem(tag + "w2s1")]
    st_sems = [P.sem(tag + "ss0"), P.sem(tag + "ss1")]
    ac = P.sem("act"); dv = P.sem("dve"); st = P.sem("f1_st")
    su = P.sem("setup")
    psf = C["ps_f32"]
    bank_ev = C.setdefault("bank_ev", [[] for _ in range(6)])
    nhb = hid // 1024
    nhc = hid // 128

    b1_pp = f32.alloc(n_exp * nhc, "b1pp") if b1_vecs else None
    b2_pp = f32.alloc(n_exp * 8, "b2pp") if b2_vecs else None
    for e in range(n_exp):
        if b1_vecs:
            per_partition_bias(P, C, b1_vecs(e), hid,
                               b1_pp[:, e * nhc:(e + 1) * nhc], f"{tag}b1{e}")
        if b2_vecs:
            per_partition_bias(P, C, b2_vecs(e), 1024,
                               b2_pp[:, e * 8:(e + 1) * 8], f"{tag}b2{e}")
    dsy = P.sem("dma_sync")
    b_ev = [(dsy, dsy.n)]

    xn_sb = Ring([bf.alloc(8192, tag + "xn")])
    w1r = Ring([bf.alloc(8192, f"{tag}w1{i}") for i in range(2)])
    w2r = Ring([bf.alloc(4096, f"{tag}w2{i}") for i in range(2)])
    h_sb = bf.alloc(nhc * 1024, tag + "h")
    if n_exp > 1:
        y_acc = f32.alloc(8192, tag + "ya")
        cwB = Ring([f32.alloc(1024, tag + "cwB")])
        ytmp = f32.alloc(512, tag + "yt")
    ost = Ring([(bf if not out_f32 else f32).alloc(1024, f"{tag}os{i}")
                for i in range(2)])
    out_sc = SC[out_name]
    zn2 = SC["zn2"]

    for ts in range(4):
        tsl = slice(ts * 1024, (ts + 1) * 1024)
        ixn, xn_ts, relx = xn_sb.take()
        for c in range(DCH):
            P.dma("sync", xn_ts[:, c * 1024:(c + 1) * 1024],
                  zn2[c][:, tsl], relx if c == 0 else (), f"{tag}xn{ts}.{c}", ld)
        xn_ev = [(ld, ld.n)]
        if n_exp > 1:
            ez = P.op("gpsimd", lambda e: e.memset(y_acc[:, :], 0.0),
                      [(dv, dv.n), (ac, ac.n)], ((su, 1),), f"{tag}z{ts}")
            acc_ev = [(su, su.n)]
        xn_last = []
        for ex in range(n_exp):
            if n_exp > 1:
                icb, cwb, relc = cwB.take()
                P.dma("gpsimd", cwb[:, :],
                      SC[cw_name][ex:ex + 1, tsl].partition_broadcast(128),
                      relc, f"{tag}cwb{ts}.{ex}", cw_sem)
                cw_ev = [(cw_sem, cw_sem.n)]
            for hb in range(nhb):
                iw, w1b, rel1 = w1r.take()
                srcw = w1_of(ex)[:, hb * 1024:(hb + 1) * 1024].rearrange(
                    "(ic p) c -> p ic c", p=128)
                w1s = w1_sems[iw % 2]
                P.dma("gpsimd", w1b[:, :].rearrange("p (ic c) -> p ic c", ic=8),
                      srcw, rel1, f"{tag}w1{ts}.{ex}.{hb}", w1s)
                w1_ev = (w1s, w1s.n)
                h_done = []
                for ht in range(8):
                    for tcl in range(2):
                        bank = tcl
                        pb = psf[bank]
                        for ic in range(DCH):
                            P.op("tensor", (lambda l=w1b[:, ic * 1024 + ht * 128:
                                ic * 1024 + ht * 128 + 128],
                                r=xn_ts[:, ic * 1024 + tcl * 512:
                                        ic * 1024 + tcl * 512 + 512],
                                p=pb, st_=(ic == 0), sp=(ic == DCH - 1):
                                (lambda e: e.matmul(p[:], l, r, start=st_,
                                 stop=sp, skip_group_check=True)))(),
                                ([w1_ev] + xn_ev + bank_ev[bank] + b_ev)
                                if ic == 0 else (),
                                ((mm, 1),) if ic == DCH - 1 else (),
                                f"{tag}h{ts}.{ex}.{hb}.{ht}.{tcl}")
                        h_mm = (mm, mm.n)
                        xn_last.append(h_mm)
                        dst = h_sb[:, (hb * 8 + ht) * 1024 + tcl * 512:
                                   (hb * 8 + ht) * 1024 + tcl * 512 + 512]
                        hc_g = hb * 8 + ht
                        if b1_vecs:
                            fng = (lambda p=pb, d=dst, col=b1_pp[:,
                                ex * nhc + hc_g:ex * nhc + hc_g + 1]:
                                (lambda e: e.activation(d, p[:], AF.Gelu,
                                bias=col, scale=1.0)))()
                        else:
                            fng = (lambda p=pb, d=dst: (lambda e: e.activation(
                                d, p[:], AF.Gelu, bias=0.0, scale=1.0)))()
                        eg = P.op("scalar", fng, [h_mm], ((ac, 1),),
                                  f"{tag}g{ts}.{ex}.{hb}.{ht}.{tcl}")
                        bank_ev[bank] = [eg[0]]
                        h_done.append(eg[0])
                w1r.release(iw, [h_done[-1]])
            h_ev = [(ac, ac.n)]
            for dt in range(8):
                iw2, w2b, rel2 = w2r.take()
                srcw2 = w2_of(ex)[:, dt * 128:(dt + 1) * 128].rearrange(
                    "(hc p) c -> p hc c", p=128)
                w2s = w2_sems[iw2 % 2]
                P.dma("gpsimd",
                      w2b[:, 0:nhc * 128].rearrange("p (hc c) -> p hc c",
                                                    hc=nhc),
                      srcw2, rel2, f"{tag}w2{ts}.{ex}.{dt}", w2s)
                w2_ev = (w2s, w2s.n)
                y_mms = []
                for tcl in range(2):
                    bank = 2 + tcl
                    pb = psf[bank]
                    for hc in range(nhc):
                        P.op("tensor", (lambda l=w2b[:, hc * 128:(hc + 1) * 128],
                            r=h_sb[:, hc * 1024 + tcl * 512:
                                   hc * 1024 + tcl * 512 + 512],
                            p=pb, st_=(hc == 0), sp=(hc == nhc - 1): (lambda e:
                            e.matmul(p[:], l, r, start=st_, stop=sp,
                             skip_group_check=True)))(),
                            ([w2_ev] + h_ev + bank_ev[bank]) if hc == 0 else (),
                            ((mm, 1),) if hc == nhc - 1 else (),
                            f"{tag}y{ts}.{ex}.{dt}.{tcl}")
                    y_mm = (mm, mm.n)
                    y_mms.append(y_mm)
                    if n_exp > 1:
                        et = P.op("vector", (lambda p=pb,
                            c_=cwb[:, tcl * 512:(tcl + 1) * 512],
                            col=b2_pp[:, ex * 8 + dt:ex * 8 + dt + 1]:
                            (lambda e: e.scalar_tensor_tensor(ytmp[:, :], p[:],
                            col, c_, ALU.add, ALU.mult)))(),
                            [y_mm] + cw_ev + acc_ev + [(dv, dv.n)], ((dv, 1),),
                            f"{tag}ym{ts}.{ex}.{dt}.{tcl}")
                        ea = P.op("vector", (lambda d=y_acc[:,
                            dt * 1024 + tcl * 512:dt * 1024 + tcl * 512 + 512]:
                            (lambda e: e.tensor_tensor(d, d, ytmp[:, :],
                            ALU.add)))(), [(dv, dv.n)], ((dv, 1),),
                            f"{tag}ya{ts}.{ex}.{dt}.{tcl}")
                        bank_ev[bank] = [ea[0]]
                    else:
                        io, obuf, relo = ost.take()
                        col = b2_pp[:, dt:dt + 1] if b2_vecs else None
                        if col is not None:
                            fne = (lambda p=pb, d=obuf, cc=col, t2=tcl:
                                (lambda e: e.activation(
                                d[:, t2 * 512:(t2 + 1) * 512], p[:],
                                AF.Identity, bias=cc, scale=1.0)))()
                        else:
                            fne = (lambda p=pb, d=obuf, t2=tcl: (lambda e:
                                e.activation(d[:, t2 * 512:(t2 + 1) * 512],
                                p[:], AF.Copy, bias=0.0, scale=1.0)))()
                        ee = P.op("scalar", fne, [y_mm] + relo, ((ac, 1),),
                                  f"{tag}oe{ts}.{ex}.{dt}.{tcl}")
                        bank_ev[bank] = [ee[0]]
                        if tcl == 1:
                            ssem = st_sems[io % 2]
                            dd = P.dma("scalar", out_sc[dt][:, tsl], obuf[:, :],
                                       [ee[0], (ac, ac.n)], f"{tag}st{ts}.{dt}",
                                       ssem)
                            ost.release(io, [(ssem, ssem.n)])
                        else:
                            ost.release(io, [])
                            ost.i -= 1
                w2r.release(iw2, y_mms)
            if n_exp > 1:
                cwB.release(icb, [(P.sems["dve"], P.sems["dve"].n)])
        xn_sb.release(ixn, xn_last[-8:])
        if n_exp > 1:
            for dt in range(8):
                io, obuf, relo = ost.take()
                ec = P.op("scalar", (lambda s_=y_acc[:, dt * 1024:(dt + 1) * 1024],
                    d=obuf: (lambda e: e.activation(d[:, :], s_, AF.Copy,
                    bias=0.0, scale=1.0)))(), [(dv, dv.n)] + relo, ((ac, 1),),
                    f"{tag}oc{ts}.{dt}")
                ssem = st_sems[io % 2]
                dd = P.dma("scalar", out_sc[dt][:, tsl], obuf[:, :], [ec[0]],
                           f"{tag}st{ts}.{dt}", ssem)
                ost.release(io, [(ssem, ssem.n)])
    P.op("sync", None, [(st_sems[0], st_sems[0].n),
         (st_sems[1], st_sems[1].n)], (), tag + "fin")
    P.barrier(tag)


def emit_abel(P, C):
    IN = C["in"]
    _ffn1(P, C, 1, A4, lambda e: IN["L1_ffn_aw1"], lambda e: IN["L1_ffn_aw2"],
          lambda e: IN["L1_ffn_ab1"], lambda e: IN["L1_ffn_ab2"], None, "abel",
          True, "ab:")


def emit_moe1(P, C):
    IN = C["in"]
    _ffn1(P, C, E1, H1FF, lambda e: IN["L1_ffn_ew1"][e],
          lambda e: IN["L1_ffn_ew2"][e], lambda e: IN["L1_ffn_eb1"][e],
          lambda e: IN["L1_ffn_eb2"][e], "cw1", "moe1", False, "m1:")


def emit_expm(P, C):
    nc = P.nc
    bf = C["bf"]; f32 = C["f32"]
    IN = C["in"]; SC = C["sc"]
    ld = P.sem("ex_ld"); mm = P.sem("ex_mm")
    ac = P.sem("act"); dv = P.sem("dve"); st = P.sem("ex_st")
    su = P.sem("setup")
    psf = C["ps_f32"]; psb = C["ps_bf"]
    bank_ev = C.setdefault("bank_ev", [[] for _ in range(6)])
    idb = C["ident_bf"]

    A = bf.alloc(8192, "exA")
    A2 = bf.alloc(8192, "exA2")
    A3 = bf.alloc(8192, "exA3")
    A4_ = bf.alloc(8192, "exA4")
    H = bf.alloc(8192, "exH")
    H2 = bf.alloc(8192, "exH2")
    Pm = bf.alloc(8192, "exP")
    Qm = bf.alloc(8192, "exQ")

    def sqmm(lhsT_sb, rhs_sb, out_sb, tagm, post=None, post_arg=None):
        ops_ev = []
        for dt in range(8):
            for nh2 in range(2):
                bank = (dt * 2 + nh2) % 2
                pb = psf[bank]
                for ic in range(DCH):
                    P.op("tensor", (lambda l=lhsT_sb[:, ic * 1024 + dt * 128:
                        ic * 1024 + dt * 128 + 128],
                        r=rhs_sb[:, ic * 1024 + nh2 * 512:
                                 ic * 1024 + nh2 * 512 + 512],
                        p=pb, st_=(ic == 0), sp=(ic == DCH - 1): (lambda e:
                        e.matmul(p[:], l, r, start=st_, stop=sp,
                         skip_group_check=True)))(),
                        ([(dv, dv.n), (ac, ac.n)] + bank_ev[bank])
                        if ic == 0 else (),
                        ((mm, 1),) if ic == DCH - 1 else (),
                        f"{tagm}.{dt}.{nh2}")
                mev = (mm, mm.n)
                dst = out_sb[:, dt * 1024 + nh2 * 512:
                             dt * 1024 + nh2 * 512 + 512]
                if post is None:
                    epv = P.op("scalar", (lambda p=pb, d=dst: (lambda e:
                        e.activation(d, p[:], AF.Copy, bias=0.0,
                        scale=1.0)))(), [mev], ((ac, 1),), f"{tagm}e{dt}.{nh2}")
                else:
                    xs = post_arg[:, dt * 1024 + nh2 * 512:
                                  dt * 1024 + nh2 * 512 + 512]
                    epv = P.op("vector", (lambda p=pb, d=dst, x_=xs: (lambda e:
                        e.tensor_tensor(d, p[:], x_, ALU.add)))(), [mev],
                        ((dv, 1),), f"{tagm}e{dt}.{nh2}")
                bank_ev[bank] = [epv[0]]
                ops_ev.append(epv[0])
        return [(P.sems["act"], P.sems["act"].n), (P.sems["dve"], P.sems["dve"].n)]

    def lincomb(dst, terms, tagl, idc=None):
        first = True
        for cf, tl in terms:
            if first:
                P.op("vector", (lambda d=dst, t_=tl, c_=cf: (lambda e:
                    e.tensor_scalar(d[:, :], t_[:, :], c_, None, ALU.mult)))(),
                    [(dv, dv.n), (ac, ac.n), (mm, mm.n)], ((dv, 1),),
                    tagl + "0")
                first = False
            else:
                P.op("vector", (lambda d=dst, t_=tl, c_=cf: (lambda e:
                    e.scalar_tensor_tensor(d[:, :], t_[:, :], c_, d[:, :],
                    ALU.mult, ALU.add)))(), [(dv, dv.n)], ((dv, 1),),
                    tagl + "x")
        if idc is not None:
            for cc2 in range(DCH):
                dsl = dst[:, cc2 * 1024 + cc2 * 128: cc2 * 1024 + (cc2 + 1) * 128]
                P.op("vector", (lambda d=dsl, c_=idc: (lambda e:
                    e.scalar_tensor_tensor(d, idb[:, :], c_, d, ALU.mult,
                    ALU.add)))(), [(dv, dv.n), (su, su.n)], ((dv, 1),),
                    tagl + f"i{cc2}")
        return [(P.sems["dve"], P.sems["dve"].n)]

    def transpose_full(src_sb, dst_sb, tagt):
        for ic in range(DCH):
            for jc in range(DCH):
                kk = ic * DCH + jc
                pbz = psb[kk % 2]
                P.op("tensor", (lambda s_=src_sb[:, ic * 1024 + jc * 128:
                    ic * 1024 + jc * 128 + 128], p=pbz: (lambda e:
                    e.transpose(p[:, 0:128], s_, idb[:])))(),
                    [(dv, dv.n), (su, su.n)] + bank_ev[4 + kk % 2], ((mm, 1),),
                    f"{tagt}t{ic}.{jc}")
                mev = (mm, mm.n)
                ee = P.op("vector", (lambda p=pbz, d=dst_sb[:,
                    jc * 1024 + ic * 128: jc * 1024 + ic * 128 + 128]:
                    (lambda e: e.tensor_copy(d, p[:, 0:128])))(), [mev],
                    ((dv, 1),), f"{tagt}e{ic}.{jc}")
                bank_ev[4 + kk % 2] = [ee[0]]
        return [(P.sems["dve"], P.sems["dve"].n)]

    c = [1.0]
    for i in range(1, 13):
        c.append(c[-1] / i)
    for gi in range(4):
        src = IN["L1_ffn_gens"][gi].rearrange("(ic p) c -> p ic c", p=128)
        P.dma("gpsimd", A[:, :].rearrange("p (ic c) -> p ic c", ic=8), src,
              [(mm, mm.n), (dv, dv.n), (ac, ac.n)], f"ex:g{gi}", ld)
        gev = [(ld, ld.n)]
        P.op("vector", lambda e: e.tensor_scalar(A[:, :], A[:, :], 0.25, None,
             ALU.mult), gev + [(mm, mm.n)], ((dv, 1),), f"ex:a{gi}")
        sqmm(A, A, A2, f"ex:A2.{gi}")
        P.op("vector", lambda e: e.tensor_scalar(A2[:, :], A2[:, :], -1.0, None,
             ALU.mult), [(dv, dv.n), (ac, ac.n)], ((dv, 1),), f"ex:a2n{gi}")
        sqmm(A2, A, A3, f"ex:A3.{gi}")
        sqmm(A2, A2, A4_, f"ex:A4.{gi}")
        lincomb(H, [(0.0, A)], f"ex:q3.{gi}", idc=c[12])
        sqmm(A4_, H, H2, f"ex:h1.{gi}")
        lincomb(H, [(c[9], A), (c[10], A2), (c[11], A3),
                    (1.0, H2)], f"ex:q2.{gi}", idc=c[8])
        sqmm(A4_, H, H2, f"ex:h2.{gi}")
        lincomb(H, [(c[5], A), (c[6], A2), (c[7], A3),
                    (1.0, H2)], f"ex:q1.{gi}", idc=c[4])
        sqmm(A4_, H, H2, f"ex:h3.{gi}")
        lincomb(H, [(c[1], A), (c[2], A2), (c[3], A3),
                    (1.0, H2)], f"ex:q0.{gi}", idc=c[0])
        transpose_full(H, H2, f"ex:tr.{gi}")
        sqmm(H2, H, A2, f"ex:s1.{gi}")
        sqmm(H, H2, A3, f"ex:s1t.{gi}")
        sqmm(A3, A2, H, f"ex:s2.{gi}")
        if gi == 0:
            P.op("vector", lambda e: e.tensor_copy(Pm[:, :], H[:, :]),
                 [(ac, ac.n), (dv, dv.n)], ((dv, 1),), "ex:p0")
            transpose_full(H, Qm, "ex:q0t")
        else:
            sqmm(Qm, H, H2, f"ex:pu.{gi}")
            qev = sqmm(H, Qm, A2, f"ex:qu.{gi}")
            P.op("vector", lambda e: e.tensor_copy(Pm[:, :], H2[:, :]),
                 [(ac, ac.n), (dv, dv.n), (mm, mm.n)], ((dv, 1),), f"ex:pc{gi}")
            P.op("vector", lambda e: e.tensor_copy(Qm[:, :], A2[:, :]),
                 [(dv, dv.n)], ((dv, 1),), f"ex:qc{gi}")
    fev = [(P.sems["dve"], P.sems["dve"].n)]
    for ic2 in range(DCH):
        P.dma("scalar", SC["rtot"][ic2],
              Pm[:, ic2 * 1024:(ic2 + 1) * 1024], fev, f"ex:st{ic2}", st)
    P.op("sync", None, [(st, st.n)], (), "ex:fin")
    P.barrier("expm")


def emit_rotate(P, C):
    nc = P.nc
    bf = C["bf"]; f32 = C["f32"]
    SC = C["sc"]
    ld = P.sem("ro_ld"); mm = P.sem("ro_mm")
    ac = P.sem("act"); dv = P.sem("dve")
    m_sems = [P.sem("ro_m0"), P.sem("ro_m1")]
    s_sems = [P.sem("ro_s0"), P.sem("ro_s1")]
    psf = C["ps_f32"]
    bank_ev = C.setdefault("bank_ev", [[] for _ in range(6)])
    rt = bf.alloc(8192, "ro_rt")
    for ic in range(DCH):
        P.dma("sync", rt[:, ic * 1024:(ic + 1) * 1024], SC["rtot"][ic], (),
              f"ro:w{ic}", ld)
    mo = Ring([bf.alloc(4096, f"ro_m{i}") for i in range(2)])
    ev_st = Ring([f32.alloc(512, f"ro_e{i}") for i in range(2)])
    w_ev = [(ld, ld.n)]
    for tc in range(8):
        tsl = slice(tc * 512, (tc + 1) * 512)
        im, mb, relm = mo.take()
        msem = m_sems[im % 2]
        for c in range(DCH):
            P.dma("sync", mb[:, c * 512:(c + 1) * 512], SC["moe1"][c][:, tsl],
                  relm if c == 0 else (), f"ro:m{tc}.{c}", msem)
        m_ev = [(msem, msem.n)]
        mm_last = []
        for dt in range(8):
            bank = dt % 2
            pb = psf[bank]
            for ic in range(DCH):
                P.op("tensor", (lambda l=rt[:, ic * 1024 + dt * 128:
                    ic * 1024 + dt * 128 + 128],
                    r=mb[:, ic * 512:(ic + 1) * 512], p=pb, st_=(ic == 0),
                    sp=(ic == DCH - 1): (lambda e: e.matmul(p[:], l, r,
                    start=st_, stop=sp, skip_group_check=True)))(),
                    (w_ev + m_ev + bank_ev[bank]) if ic == 0 else (),
                    ((mm, 1),) if ic == DCH - 1 else (), f"ro:mm{tc}.{dt}")
            mev = (mm, mm.n)
            mm_last.append(mev)
            io, obuf, relo = ev_st.take()
            ee = P.op("scalar", (lambda p=pb, d=obuf: (lambda e: e.activation(
                d[:, :], p[:], AF.Copy, bias=0.0, scale=1.0)))(),
                [mev] + relo, ((ac, 1),), f"ro:e{tc}.{dt}")
            bank_ev[bank] = [ee[0]]
            ssem = s_sems[io % 2]
            dd = P.dma("scalar", SC["nonabel"][dt][:, tsl], obuf[:, :],
                       [ee[0]], f"ro:s{tc}.{dt}", ssem)
            ev_st.release(io, [(ssem, ssem.n)])
        mo.release(im, mm_last[-1:])
    P.op("sync", None, [(s_sems[0], s_sems[0].n), (s_sems[1], s_sems[1].n)],
         (), "ro:fin")
    P.barrier("rot")


def emit_final(P, C):
    nc = P.nc
    bf = C["bf"]; f32 = C["f32"]
    IN = C["in"]; SC = C["sc"]
    ld = P.sem("fn_ld"); ldg = P.sem("fn_ldg"); mm = P.sem("fn_mm")
    ac = P.sem("act"); dv = P.sem("dve"); st = P.sem("fn_st")
    o_sems = [P.sem("fn_o0"), P.sem("fn_o1")]
    su = P.sem("setup"); dsy = P.sem("dma_sync")
    psf = C["ps_f32"]; psb = C["ps_bf"]
    bank_ev = C.setdefault("bank_ev", [[] for _ in range(6)])
    idf = C["ident_f32"]
    out = C["out"]

    gcw = bf.alloc(16, "gcw")
    P.op("gpsimd", lambda e: e.dma_start(gcw[:, 0:16],
         IN["L1_ffn_gc_w"].rearrange("(k p) one -> p (k one)", p=128),
         allow_slow_non_contiguous=True), (), ((P.sem("dma_gpsimd"), 16),),
         "fn:gcw")
    gcb = f32.alloc(1, "gcb")
    P.op("sync", lambda e: e.dma_start(gcb[0:1, 0:1],
         IN["L1_ffn_gc_b"].unsqueeze(0)), (), ((P.sem("dma_sync"), 16),),
         "fn:gcb")
    w_ev = [(dsy, dsy.n), (P.sem("dma_gpsimd"), P.sem("dma_gpsimd").n)]

    ab_f = f32.alloc(2048, "fn_ab")
    na_f = f32.alloc(2048, "fn_na")
    z_f = f32.alloc(2048, "fn_z")
    s_f = f32.alloc(2048, "fn_s")
    x_f = f32.alloc(2048, "fn_x")
    ab_b = bf.alloc(2048, "fn_abb")
    na_b = bf.alloc(2048, "fn_nab")
    grow = f32.alloc(256 * 3, "fn_gr")
    gB = f32.alloc(256, "fn_gB")
    ot_st = Ring([f32.alloc(256, f"fn_o{i}") for i in range(2)])
    gc_sc = SC["gate1"]

    for tb in range(16):
        tsl = slice(tb * 256, (tb + 1) * 256)
        guard = [(mm, mm.n), (dv, dv.n), (ac, ac.n), (st, st.n)]
        for c in range(DCH):
            P.dma("sync", ab_f[:, c * 256:(c + 1) * 256], SC["abel"][c][:, tsl],
                  guard if c == 0 else (), f"fn:a{tb}.{c}", ld)
            P.dma("sync", na_f[:, c * 256:(c + 1) * 256],
                  SC["nonabel"][c][:, tsl], (), f"fn:n{tb}.{c}", ld)
            P.dma("sync", z_f[:, c * 256:(c + 1) * 256], SC["z1f"][c][:, tsl],
                  (), f"fn:z{tb}.{c}", ld)
            P.dma("sync", s_f[:, c * 256:(c + 1) * 256], SC["s1"][c][:, tsl],
                  (), f"fn:s{tb}.{c}", ld)
            P.dma("sync", x_f[:, c * 256:(c + 1) * 256], SC["x1T"][c][:, tsl],
                  (), f"fn:x{tb}.{c}", ld)
        lev = [(ld, ld.n)]
        cast_ev = []
        for c in range(DCH):
            e1 = P.op("vector", (lambda cc=c: (lambda e: e.tensor_copy(
                ab_b[:, cc * 256:(cc + 1) * 256],
                ab_f[:, cc * 256:(cc + 1) * 256])))(),
                (lev + [(mm, mm.n)]) if c == 0 else (), ((dv, 1),),
                f"fn:cb{tb}.{c}")
            e2 = P.op("vector", (lambda cc=c: (lambda e: e.tensor_copy(
                na_b[:, cc * 256:(cc + 1) * 256],
                na_f[:, cc * 256:(cc + 1) * 256])))(), (), ((dv, 1),),
                f"fn:cn{tb}.{c}")
        cast_ev = [(dv, dv.n)]
        pb = psf[4]
        for c in range(DCH):
            P.op("tensor", (lambda cc=c: (lambda e: e.matmul(pb[0:1, 0:256],
                gcw[:, cc:cc + 1], ab_b[:, cc * 256:(cc + 1) * 256],
                start=(cc == 0), stop=False, skip_group_check=True)))(),
                (cast_ev + w_ev + bank_ev[4]) if c == 0 else (),
                ((mm, 1),), f"fn:g1{tb}.{c}")
        for c in range(DCH):
            P.op("tensor", (lambda cc=c: (lambda e: e.matmul(pb[0:1, 0:256],
                gcw[:, 8 + cc:8 + cc + 1], na_b[:, cc * 256:(cc + 1) * 256],
                start=False, stop=(cc == DCH - 1), skip_group_check=True)))(),
                (), ((mm, 1),), f"fn:g2{tb}.{c}")
        g_mm = (mm, mm.n)
        e3 = P.op("scalar", lambda e: e.activation(grow[0:1, 0:256],
            pb[0:1, 0:256], AF.Sigmoid, bias=gcb[0:1, 0:1], scale=1.0),
            [g_mm, (st, st.n)], ((ac, 1),), f"fn:sg{tb}")
        bank_ev[4] = [e3[0]]
        d1 = P.dma("scalar", gc_sc[0:1, tsl], grow[0:1, 0:256], [e3[0]],
                   f"fn:gs{tb}", st)
        b1 = P.dma("gpsimd", gB[:, :], gc_sc[0:1, tsl].partition_broadcast(128),
                   [d1[0], (dv, dv.n)], f"fn:gb{tb}", ldg)
        gbev = [(ldg, ldg.n)]
        for c in range(DCH):
            sl2 = slice(c * 256, (c + 1) * 256)
            P.op("vector", (lambda s_=sl2: (lambda e: e.tensor_tensor(
                na_f[:, s_], na_f[:, s_], ab_f[:, s_], ALU.subtract)))(),
                gbev if c == 0 else [(dv, dv.n)], ((dv, 1),), f"fn:d{tb}.{c}")
            P.op("vector", (lambda s_=sl2: (lambda e: e.tensor_tensor(
                na_f[:, s_], na_f[:, s_], gB[:, :], ALU.mult)))(),
                [(dv, dv.n)], ((dv, 1),), f"fn:m{tb}.{c}")
            P.op("vector", (lambda s_=sl2: (lambda e: e.tensor_tensor(
                na_f[:, s_], na_f[:, s_], ab_f[:, s_], ALU.add)))(),
                [(dv, dv.n)], ((dv, 1),), f"fn:mx{tb}.{c}")
            P.op("vector", (lambda s_=sl2: (lambda e: e.tensor_tensor(
                na_f[:, s_], na_f[:, s_], z_f[:, s_], ALU.add)))(),
                [(dv, dv.n)], ((dv, 1),), f"fn:f{tb}.{c}")
            e5 = P.op("scalar", (lambda s_=sl2: (lambda e: e.activation(
                na_f[:, s_], na_f[:, s_], AF.Silu, bias=0.0, scale=1.0)))(),
                [(dv, dv.n)], ((ac, 1),), f"fn:si{tb}.{c}")
            P.op("vector", (lambda s_=sl2: (lambda e: e.tensor_tensor(
                na_f[:, s_], na_f[:, s_], s_f[:, s_], ALU.mult)))(),
                [e5[0]], ((dv, 1),), f"fn:sm{tb}.{c}")
            P.op("vector", (lambda s_=sl2: (lambda e: e.tensor_tensor(
                na_f[:, s_], na_f[:, s_], x_f[:, s_], ALU.add)))(),
                [(dv, dv.n)], ((dv, 1),), f"fn:x2{tb}.{c}")
        x2_ev = [(dv, dv.n)]
        for q in range(2):
            for c in range(DCH):
                kk = q * DCH + c
                pbf = psf[kk % 2]
                P.op("tensor", (lambda s_=na_f[:, c * 256 + q * 128:
                    c * 256 + q * 128 + 128], p=pbf: (lambda e: e.transpose(
                    p[:, 0:128], s_, idf[:])))(),
                    x2_ev + bank_ev[kk % 2] + [(su, su.n)], ((mm, 1),),
                    f"fn:t{tb}.{q}.{c}")
                mev = (mm, mm.n)
                io, obuf, relo = ot_st.take()
                ee = P.op("vector", (lambda p=pbf, d=obuf: (lambda e:
                    e.tensor_copy(d[:, 0:128], p[:, 0:128])))(), [mev] + relo,
                    ((dv, 1),), f"fn:te{tb}.{q}.{c}")
                bank_ev[kk % 2] = [ee[0]]
                osem = o_sems[io % 2]
                dd = P.dma("scalar", out[tb * 256 + q * 128:
                           tb * 256 + (q + 1) * 128, c * 128:(c + 1) * 128],
                           obuf[:, 0:128], [ee[0]], f"fn:o{tb}.{q}.{c}", osem)
                ot_st.release(io, [(osem, osem.n)])
    P.op("sync", None, [(o_sems[0], o_sems[0].n), (o_sems[1], o_sems[1].n),
         (st, st.n)], (), "fn:fin")
    P.barrier("final")


SCRATCH_SPECS = {
    "x0T": ([DCH, 128, T], F32),
    "xn0": ([DCH, 128, T], BF16),
    "q0": ([NSEQ, H0, 64, SEQ], BF16),
    "k0": ([NSEQ, H0, 64, SEQ], BF16),
    "v0": ([NT, 128, 1040], BF16),
    "q1": ([NSEQ, H0, 68, SEQ], BF16),
    "k1": ([NSEQ, H0, 68, SEQ], BF16),
    "v1": ([NT, 128, 1040], BF16),
    "s0": ([DCH, 128, T], F32),
    "s1": ([DCH, 128, T], F32),
    "x1T": ([DCH, 128, T], F32),
    "z1f": ([DCH, 128, T], F32),
    "zn1": ([DCH, 128, T], BF16),
    "zn2": ([DCH, 128, T], BF16),
    "abel": ([DCH, 128, T], F32),
    "nonabel": ([DCH, 128, T], F32),
    "moe0": ([DCH, 128, T], F32),
    "moe1": ([DCH, 128, T], BF16),
    "rtot": ([DCH, 128, D], BF16),
    "cw0": ([E0, T], F32),
    "cw1": ([E1, T], F32),
    "gate1": ([1, T], F32),
    "dbg": ([32, 128, 1024], BF16),
    "lnrows": ([2, T], F32),
}


def build_program(phases, mode="fused", scratch_in=(), scratch_out=()):
    nc = bass.Bass("TRN2", target_bir_lowering=False)
    C = {"in": {}, "sc": {}}
    C["in"]["x"] = nc.dram_tensor("x", [T, D], F32, kind="ExternalInput").ap()
    for nm, shape in INPUT_WEIGHT_SPECS:
        C["in"][nm] = nc.dram_tensor(nm, list(shape), F32, kind="ExternalInput").ap()
    for nm, arr_shape, dt in CONST_SPECS:
        C["in"][nm] = nc.dram_tensor(nm, list(arr_shape), dt, kind="ExternalInput").ap()
    C["out"] = nc.dram_tensor("out", [T, D], F32, kind="ExternalOutput").ap()
    for nm, (shape, dt) in SCRATCH_SPECS.items():
        kind = "Internal"
        if nm in scratch_in:
            kind = "ExternalInput"
        elif nm in scratch_out:
            kind = "ExternalOutput"
        C["sc"][nm] = nc.dram_tensor(nm, list(shape), dt, kind=kind).ap()

    import contextlib
    stack = contextlib.ExitStack()
    bf_t = stack.enter_context(nc.sbuf_tensor("pool_bf", [128, 73300], BF16))
    f32_t = stack.enter_context(nc.sbuf_tensor("pool_f32", [128, 12400], F32))
    C["bf"] = Pool(bf_t, 73300)
    C["f32"] = Pool(f32_t, 12400)
    C["ident_bf"] = stack.enter_context(nc.sbuf_tensor("ident_bf", [128, 128], BF16))
    C["ident_f32"] = stack.enter_context(nc.sbuf_tensor("ident_f32", [128, 128], F32))
    C["qpos_row"] = stack.enter_context(nc.sbuf_tensor("qpos_row", [1, SEQ], F32))
    C["wt_stage"] = [stack.enter_context(nc.sbuf_tensor(f"wts{i}", [128, 1024], BF16))
                     for i in range(2)]
    C["ps_f32"] = [stack.enter_context(nc.psum_tensor(f"psf{i}", [128, 512], F32))
                   for i in range(6)]
    C["ps_bf"] = [stack.enter_context(nc.psum_tensor(f"psb{i}", [128, 1024], BF16))
                  for i in range(2)]

    P = Prog(nc)
    emit_setup(P, C)
    P.barrier("setup")
    for ph in phases:
        C["bf"].reset()
        C["f32"].reset()
        ph(P, C)
    with nc.Block() as block:
        P.emit(block)
    stack.close()
    return nc, P


INPUT_WEIGHT_SPECS = [
    ("L0_attn_wq", (1024, 1024)), ("L0_attn_bq", (1024,)),
    ("L0_attn_wk", (1024, 1024)), ("L0_attn_bk", (1024,)),
    ("L0_attn_wv", (1024, 1024)), ("L0_attn_bv", (1024,)),
    ("L0_attn_wo", (1024, 1024)), ("L0_attn_bo", (1024,)),
    ("L0_ffn_router_w", (8, 1024)),
    ("L0_ffn_w1", (8, 1024, 4096)), ("L0_ffn_w2", (8, 4096, 1024)),
    ("L1_attn_w_qkv", (3072, 1024)), ("L1_attn_w_out", (1024, 1024)),
    ("L1_attn_alibi_slopes", (16,)),
    ("L1_ffn_ln_g", (1024,)), ("L1_ffn_ln_b", (1024,)),
    ("L1_ffn_aw1", (1024, 4096)), ("L1_ffn_ab1", (4096,)),
    ("L1_ffn_aw2", (4096, 1024)), ("L1_ffn_ab2", (1024,)),
    ("L1_ffn_gate_w", (1024, 6)), ("L1_ffn_gate_b", (6,)),
    ("L1_ffn_ew1", (6, 1024, 2048)), ("L1_ffn_eb1", (6, 2048)),
    ("L1_ffn_ew2", (6, 2048, 1024)), ("L1_ffn_eb2", (6, 1024)),
    ("L1_ffn_gens", (4, 1024, 1024)),
    ("L1_ffn_gc_w", (2048, 1)), ("L1_ffn_gc_b", (1,)),
]

CONST_SPECS = [
    ("rope_cos", (128, SEQ), BF16),
    ("rope_sin", (128, SEQ), BF16),
    ("rope_perm", (128, 128), BF16),
    ("ones2_64", (2, 64), BF16),
]


ALL_PHASES = [emit_p1_ln0, emit_p2_qkv0, emit_p3_attn0, emit_moe0,
              emit_ln1, emit_p6_qkv1, emit_p7_attn1, emit_abel,
              emit_moe1, emit_expm, emit_rotate, emit_final]


def kernel(**inputs):
    consts = host_constants()
    x_full = np.ascontiguousarray(np.asarray(inputs["L0_attn_x"], dtype=np.float32))
    shards = x_full.reshape(NCORES, T, D)
    base_map = {nm: np.ascontiguousarray(np.asarray(inputs[nm], np.float32))
                for nm, _ in INPUT_WEIGHT_SPECS}
    for nm, _, _ in CONST_SPECS:
        base_map[nm] = consts[nm]
    nc, P = build_program(ALL_PHASES)
    in_maps = []
    for c in range(NCORES):
        m = dict(base_map)
        m["x"] = shards[c]
        in_maps.append(m)
    res = run_bass_kernel_spmd(nc, in_maps, core_ids=list(range(NCORES)))
    out = np.concatenate([r["out"][None] for r in res.results], axis=0)
    return out.reshape(16, 2048, 1024)


# version 05 (final)
# speedup vs baseline: 1.0515x; 1.0479x over previous
import math
import numpy as np
import ml_dtypes
import concourse.bass as bass
import concourse.mybir as mybir
from concourse.bass_utils import run_bass_kernel_spmd

F32 = mybir.dt.float32
BF16 = mybir.dt.bfloat16
AF = mybir.ActivationFunctionType
ALU = mybir.AluOpType
AX = mybir.AxisListType
bfd = ml_dtypes.bfloat16

NCORES = 8
SEQ = 2048
NSEQ = 2
T = SEQ * NSEQ
D = 1024
NT = T // 128
DCH = D // 128
H0 = 16
DK = 64
E0, H0FF = 8, 4096
E1, H1FF = 6, 2048
A4 = 4096
LN_EPS = 1e-5
DEBUG_MOE = False
ENGS = ("tensor", "scalar", "vector", "gpsimd", "sync")


class Sem:
    __slots__ = ("h", "n", "name")
    def __init__(self, h, name):
        self.h = h
        self.n = 0
        self.name = name


class Op:
    __slots__ = ("fn", "waits", "incs", "tag")
    def __init__(self, fn, waits, incs, tag):
        self.fn = fn
        self.waits = waits
        self.incs = incs
        self.tag = tag


class Prog:

    def __init__(self, nc):
        self.nc = nc
        self.ops = {e: [] for e in ENGS}
        self.sems = {}
        self._exit = []

    def sem(self, name):
        if name not in self.sems:
            h = self.nc.semaphore(name).__enter__()
            self.sems[name] = Sem(h, name)
        return self.sems[name]

    def op(self, engine, fn, waits=(), incs=(), tag=""):
        ws = []
        for w in waits:
            if isinstance(w, Sem):
                ws.append((w, w.n))
            else:
                ws.append(w)
        evs = []
        rincs = []
        for s, amt in incs:
            s.n += amt
            rincs.append((s, amt))
            evs.append((s, s.n))
        self.ops[engine].append(Op(fn, ws, rincs, tag))
        return evs

    def barrier(self, tag="bar"):
        snap = [(s, s.n) for s in self.sems.values() if s.n > 0]
        bar = self.sem("bar")
        target = bar.n + len(ENGS)
        for e in ENGS:
            self.ops[e].append(Op(None, list(snap), [], tag + ":snap"))
            self.ops[e].append(Op(lambda eng: eng.nop(), [], [(bar, 1)], tag + ":inc"))
        bar.n = target
        for e in ENGS:
            self.ops[e].append(Op(None, [(bar, target)], [], tag + ":wait"))

    def dma(self, qeng, dst, src, waits=(), tag="dma", sem=None):
        sem = sem or self.sem("dma_" + qeng)
        return self.op(qeng, lambda eng: eng.dma_start(dst, src), waits,
                       [(sem, 16)], tag)

    def emit(self, block):
        def mk(engine):
            def run(eng):
                for o in self.ops[engine]:
                    for s, v in o.waits:
                        eng.wait_ge(s.h, v)
                    if o.fn is None:
                        continue
                    inst = o.fn(eng)
                    for s, amt in o.incs:
                        inst.then_inc(s.h, amt)
            return run
        block.tensor(mk("tensor"))
        block.scalar(mk("scalar"))
        block.vector(mk("vector"))
        block.gpsimd(mk("gpsimd"))
        block.sync(mk("sync"))

    def counts(self):
        return {e: len(v) for e, v in self.ops.items()}


class Pool:

    def __init__(self, t, n):
        self.t = t
        self.n = n
        self.off = 0
        self.hi = n

    def alloc(self, cols, name=""):
        off = self.off
        self.off += cols
        assert self.off <= self.hi, f"pool overflow {name}: {self.off} > {self.hi}"
        return self.t[:, off:off + cols]

    def persist(self, cols, name=""):
        self.hi -= cols
        assert self.off <= self.hi, f"pool overflow (persist) {name}"
        return self.t[:, self.hi:self.hi + cols]

    def reset(self):
        self.off = 0


def host_constants():
    d = np.arange(64)
    j = np.where(d < 32, d, d - 32)
    theta = 10000.0 ** (2.0 * j / 64.0)
    pos = np.arange(SEQ, dtype=np.float64)
    ang = pos[None, :] / theta[:, None]
    cos = np.cos(ang)
    sin = np.sin(ang)
    cos_fm = np.tile(cos, (2, 1)).astype(bfd)
    sin_fm = np.tile(sin, (2, 1)).astype(bfd)
    P = np.zeros((128, 128), np.float32)
    for b in range(2):
        for i in range(0, 64, 2):
            P[b * 64 + i + 1, b * 64 + i] = -1.0
            P[b * 64 + i, b * 64 + i + 1] = +1.0
    rope_perm = P.astype(bfd)
    ones2_64 = np.ones((2, 64), np.float32).astype(bfd)
    return {"rope_cos": cos_fm, "rope_sin": sin_fm, "rope_perm": rope_perm,
            "ones2_64": ones2_64}


def emit_setup(P, C):
    nc = P.nc
    s1 = P.sem("setup")
    g = P.op
    idb, idf = C["ident_bf"], C["ident_f32"]
    g("gpsimd", lambda e: e.memset(idb[:], 1.0), (), ((s1, 1),), "idb0")
    g("gpsimd", lambda e: e.affine_select(idb[:], idb[:], pattern=[[1, 128]],
      compare_op=ALU.is_equal, fill=0.0, base=0, channel_multiplier=-1),
      [(s1, s1.n)], ((s1, 1),), "idb1")
    g("gpsimd", lambda e: e.memset(idf[:], 1.0), [(s1, s1.n)], ((s1, 1),), "idf0")
    g("gpsimd", lambda e: e.affine_select(idf[:], idf[:], pattern=[[1, 128]],
      compare_op=ALU.is_equal, fill=0.0, base=0, channel_multiplier=-1),
      [(s1, s1.n)], ((s1, 1),), "idf1")
    def mkregs(eng):
        r0 = eng.alloc_register("fz")
        eng.reg_mov(r0, 0.0)
        rn = eng.alloc_register("fneg")
        inst = eng.reg_mov(rn, -10000.0)
        C["fill0"] = r0
        C["filln"] = rn
        return inst
    g("gpsimd", mkregs, (), ((s1, 1),), "fregs")
    qpos = C["qpos_row"]
    g("gpsimd", lambda e: e.iota(qpos[:], pattern=[[1, SEQ]], base=0,
      channel_multiplier=0, allow_small_or_imprecise_dtypes=True),
      (), ((s1, 1),), "iota")
    C["ev_setup"] = [(s1, s1.n)]


def load_weight_T(P, C, w_ap, n_out, n_in, dst_sb, tag):
    ld_sems = [P.sem("wt_ld0"), P.sem("wt_ld1")]
    mm = P.sem("wt_mm")
    ev = P.sem("wt_ev")
    ring = C.setdefault("wt_ring", Ring(list(C["wt_stage"])))
    psr = C.setdefault("wt_psr", Ring(list(C["ps_bf"])))
    idb = C["ident_bf"]
    su = P.sem("setup")
    id_ev = [(su, su.n)]
    n_oc = n_out // 128
    n_ic = n_in // 128
    for oc in range(n_oc):
        i, buf, rel = ring.take()
        ldi = ld_sems[i % 2]
        P.dma("gpsimd", buf[:, 0:n_in],
              w_ap[oc * 128:(oc + 1) * 128, :], rel, f"wT:{tag}:ld{oc}", ldi)
        lev = (ldi, ldi.n)
        last_mm = None
        for ic in range(n_ic):
            j, pb, relp = psr.take()
            P.op("tensor", (lambda b=buf, icc=ic, p=pb: (lambda e: e.transpose(
                p[:, 0:128], b[:, icc * 128:(icc + 1) * 128], idb[:])))(),
                [lev] + relp + id_ev, ((mm, 1),), f"wT:{tag}:tr{oc}.{ic}")
            last_mm = (mm, mm.n)
            dst = dst_sb[:, ic * n_out + oc * 128: ic * n_out + (oc + 1) * 128]
            e1 = P.op("vector", (lambda d=dst, p=pb: (lambda e: e.tensor_copy(
                d, p[:, 0:128])))(), [last_mm], ((ev, 1),),
                f"wT:{tag}:ev{oc}.{ic}")
            psr.release(j, [e1[0]])
        ring.release(i, [last_mm])


def per_partition_bias(P, C, vec_ap, n, dst_sb, tag):
    ap = vec_ap.rearrange("(c p) -> p c", p=128)
    sem = P.sem("dma_sync")
    P.op("sync", lambda e: e.dma_start(dst_sb[:, 0:n // 128], ap,
         allow_slow_non_contiguous=True), (), ((sem, 16),), "ppb:" + tag)
    return [(sem, sem.n)]


def broadcast_row(P, C, row_ap, n, dst_sb, tag, waits=()):
    src = row_ap if row_ap.ndim == 2 else row_ap.unsqueeze(0)
    sem = P.sem("dma_sync")
    P.op("sync", lambda e: e.dma_start(dst_sb[:, 0:n],
         src.partition_broadcast(128)), waits, ((sem, 16),), "bc:" + tag)
    return [(sem, sem.n)]


def emit_rstd(P, sums_col, sumsq_col, rstd_col, nmurstd_col, n, eps, waits, tag):
    g = P.op
    dv = P.sem("dve")
    sc = P.sem("act")
    inv = 1.0 / n
    g("vector", lambda e: e.tensor_scalar(nmurstd_col, sums_col, inv, None,
      ALU.mult), waits, ((dv, 1),), tag + ":mu")
    g("vector", lambda e: e.scalar_tensor_tensor(rstd_col, nmurstd_col, 1.0,
      nmurstd_col, ALU.mult, ALU.mult), [(dv, dv.n)], ((dv, 1),), tag + ":mu2")
    g("vector", lambda e: e.scalar_tensor_tensor(rstd_col, sumsq_col, inv,
      rstd_col, ALU.mult, ALU.subtract), [(dv, dv.n)], ((dv, 1),), tag + ":var")
    g("vector", lambda e: e.tensor_scalar(rstd_col, rstd_col, float(eps), None,
      ALU.add), [(dv, dv.n)], ((dv, 1),), tag + ":eps")
    g("scalar", lambda e: e.activation(rstd_col, rstd_col, AF.Ln, bias=0.0,
      scale=1.0), [(dv, dv.n)], ((sc, 1),), tag + ":ln")
    g("scalar", lambda e: e.activation(rstd_col, rstd_col, AF.Exp, bias=0.0,
      scale=-0.5), [(sc, sc.n)], ((sc, 1),), tag + ":exp")
    g("vector", lambda e: e.scalar_tensor_tensor(nmurstd_col, nmurstd_col, -1.0,
      rstd_col, ALU.mult, ALU.mult), [(sc, sc.n)], ((dv, 1),), tag + ":nmr")
    return [(P.sems["dve"], P.sems["dve"].n)]


def emit_cw_router(P, C, pb, mm_ev, scr, cols, tag):
    ac = P.sem("act"); dv = P.sem("dve")
    ne = 8
    pp = scr[:, 0:ne]; eqt = scr[:, ne:2 * ne]; cw = scr[:, 2 * ne:3 * ne]
    c_s = cols[:, 0:1]; c_m1 = cols[:, 1:2]; c_m2 = cols[:, 2:3]
    c_d = cols[:, 3:4]; c_r = cols[:, 4:5]
    e1 = P.op("scalar", (lambda p=pb: (lambda e: e.activation(pp, p[0:128, 0:ne],
        AF.Exp, bias=0.0, scale=1.0, accum_out=c_s)))(),
        [mm_ev, (dv, dv.n)], ((ac, 1),), tag + ":exp")
    P.op("vector", lambda e: e.reduce_max(c_m1, pp, axis=AX.X),
         [e1[0]], ((dv, 1),), tag + ":m1")
    P.op("vector", lambda e: e.tensor_scalar(eqt, pp, c_m1, None, ALU.is_ge),
         [(dv, dv.n)], ((dv, 1),), tag + ":eq")
    P.op("vector", lambda e: e.tensor_scalar(eqt, eqt, -1.0, 1.0, ALU.mult,
         ALU.add), [(dv, dv.n)], ((dv, 1),), tag + ":om")
    P.op("vector", lambda e: e.tensor_tensor(eqt, pp, eqt, ALU.mult),
         [(dv, dv.n)], ((dv, 1),), tag + ":pm")
    P.op("vector", lambda e: e.reduce_max(c_m2, eqt, axis=AX.X),
         [(dv, dv.n)], ((dv, 1),), tag + ":m2")
    P.op("vector", lambda e: e.tensor_scalar(c_d, c_s, 1e-9, None, ALU.mult),
         [(dv, dv.n)], ((dv, 1),), tag + ":ds")
    P.op("vector", lambda e: e.tensor_tensor(c_d, c_d, c_m1, ALU.add),
         [(dv, dv.n)], ((dv, 1),), tag + ":d1")
    P.op("vector", lambda e: e.tensor_tensor(c_d, c_d, c_m2, ALU.add),
         [(dv, dv.n)], ((dv, 1),), tag + ":d2")
    P.op("vector", lambda e: e.reciprocal(c_r, c_d), [(dv, dv.n)],
         ((dv, 1),), tag + ":rc")
    P.op("vector", lambda e: e.tensor_scalar(cw, pp, c_m2, None, ALU.is_ge),
         [(dv, dv.n)], ((dv, 1),), tag + ":sel")
    P.op("vector", lambda e: e.tensor_tensor(cw, cw, pp, ALU.mult),
         [(dv, dv.n)], ((dv, 1),), tag + ":cwp")
    ecw = P.op("vector", lambda e: e.tensor_scalar(cw, cw, c_r, None,
         ALU.mult), [(dv, dv.n)], ((dv, 1),), tag + ":cw")
    return cw, ecw[0]


def emit_cw_gate(P, C, pb, mm_ev, scr, cols, tag):
    ac = P.sem("act"); dv = P.sem("dve")
    ne = 6
    ll = scr[:, 0:ne]; eqt = scr[:, 8:8 + ne]; cw = scr[:, 16:16 + ne]
    c_m1 = cols[:, 1:2]; c_m2 = cols[:, 2:3]
    c_d = cols[:, 3:4]; c_r = cols[:, 4:5]; c_nm = cols[:, 5:6]
    e0 = P.op("scalar", (lambda p=pb: (lambda e: e.activation(ll,
        p[0:128, 0:ne], AF.Copy, bias=0.0, scale=1.0)))(),
        [mm_ev, (dv, dv.n)], ((ac, 1),), tag + ":cp")
    P.op("vector", lambda e: e.reduce_max(c_m1, ll, axis=AX.X),
         [e0[0]], ((dv, 1),), tag + ":m1")
    P.op("vector", lambda e: e.tensor_scalar(eqt, ll, c_m1, None, ALU.is_ge),
         [(dv, dv.n)], ((dv, 1),), tag + ":eq")
    P.op("vector", lambda e: e.tensor_scalar(eqt, eqt, -1e9, None, ALU.mult),
         [(dv, dv.n)], ((dv, 1),), tag + ":sc")
    P.op("vector", lambda e: e.tensor_tensor(eqt, eqt, ll, ALU.add),
         [(dv, dv.n)], ((dv, 1),), tag + ":lm")
    P.op("vector", lambda e: e.reduce_max(c_m2, eqt, axis=AX.X),
         [(dv, dv.n)], ((dv, 1),), tag + ":m2")
    P.op("vector", lambda e: e.tensor_scalar(c_nm, c_m1, -1.0, None, ALU.mult),
         [(dv, dv.n)], ((dv, 1),), tag + ":nm")
    P.op("vector", lambda e: e.tensor_tensor(c_d, c_m2, c_nm, ALU.add),
         [(dv, dv.n)], ((dv, 1),), tag + ":dm")
    e2 = P.op("scalar", lambda e: e.activation(c_d, c_d, AF.Exp, bias=0.0,
         scale=1.0), [(dv, dv.n)], ((ac, 1),), tag + ":ex")
    P.op("vector", lambda e: e.tensor_scalar(c_d, c_d, 1.0, None, ALU.add),
         [e2[0]], ((dv, 1),), tag + ":p1")
    P.op("vector", lambda e: e.reciprocal(c_r, c_d), [(dv, dv.n)],
         ((dv, 1),), tag + ":rc")
    e3 = P.op("scalar", (lambda: (lambda e: e.activation(ll, ll, AF.Exp,
        bias=c_nm, scale=1.0)))(), [(dv, dv.n)], ((ac, 1),), tag + ":z")
    P.op("vector", lambda e: e.tensor_scalar(cw, ll, c_m2, None, ALU.is_ge),
         [e3[0]], ((dv, 1),), tag + ":sel") if False else None
    P.op("vector", lambda e: e.tensor_scalar(c_d, c_d, -1.0, None, ALU.add),
         [(dv, dv.n)], ((dv, 1),), tag + ":em2")
    P.op("vector", lambda e: e.tensor_scalar(cw, ll, c_d, None, ALU.is_ge),
         [(dv, dv.n)], ((dv, 1),), tag + ":sel2")
    P.op("vector", lambda e: e.tensor_tensor(cw, cw, ll, ALU.mult),
         [(dv, dv.n)], ((dv, 1),), tag + ":cwz")
    ecw = P.op("vector", lambda e: e.tensor_scalar(cw, cw, c_r, None, ALU.mult),
         [(dv, dv.n)], ((dv, 1),), tag + ":cw")
    return cw, ecw[0]


def emit_p1_ln0(P, C):
    nc = P.nc
    bf = C["bf"]; f32 = C["f32"]
    x = C["in"]["x"]
    x0T = C["sc"]["x0T"]
    xn0 = C["sc"]["xn0"]
    idf, idb = C["ident_f32"], C["ident_bf"]
    ld = P.sem("p1_ld"); mm = P.sem("p1_mm"); ev = P.sem("p1_ev")
    ac = P.sem("act"); dv = P.sem("dve"); st = P.sem("p1_st")
    su = P.sem("setup")
    xt = [f32.alloc(1024, "xt0"), f32.alloc(1024, "xt1")]
    scr = f32.alloc(1024, "p1scr")
    xnf = [f32.alloc(1024, "xnf0"), f32.alloc(1024, "xnf1")]
    xnt = [bf.alloc(1024, "xnt0"), bf.alloc(1024, "xnt1")]
    cols = f32.alloc(8, "p1cols")
    evf = [bf.alloc(512, "evb0"), bf.alloc(512, "evb1")]
    evv = [f32.alloc(512, "evf0"), f32.alloc(512, "evf1")]
    xnfm = f32.alloc(1024, "xnfm")
    rt_scr = f32.alloc(24, "rt_scr")
    rt_cols = f32.alloc(8, "rt_cols")
    cwT_sb = f32.alloc(1024, "cwT")
    rw_nat = f32.alloc(1024, "rw_nat")
    rwT = f32.alloc(64, "rwT")
    psf = C["ps_f32"]; psb = C["ps_bf"]
    bank_ev = C.setdefault("bank_ev", [[] for _ in range(6)])

    ldrw = P.sem("p1_ldrw")
    P.dma("sync", rw_nat[0:8, :], C["in"]["L0_ffn_router_w"], (), "p1:rw", ldrw)
    rw_ld = (ldrw, ldrw.n)
    for c in range(DCH):
        P.op("tensor", (lambda cc=c: (lambda e: e.transpose(
            psf[5][0:128, 0:8], rw_nat[0:8, cc * 128:(cc + 1) * 128],
            idf[0:8, 0:8])))(), [rw_ld, (su, su.n)] + bank_ev[5], ((mm, 1),),
            f"p1:rwt{c}")
        e1 = P.op("vector", (lambda cc=c: (lambda e: e.tensor_copy(
            rwT[:, cc * 8:(cc + 1) * 8], psf[5][0:128, 0:8])))(),
            [(mm, mm.n)], ((dv, 1),), f"p1:rwe{c}")
        bank_ev[5] = [e1[0]]
    rw_ev = [(dv, dv.n)]

    lds = [P.sem("p1_ld0"), P.sem("p1_ld1")]
    for it in range(NT):
        b = it % 2
        xbuf = xt[b]
        ld = lds[b]
        w = [(mm, mm.n), (ac, ac.n)] if it >= 2 else ()
        P.dma("sync", xbuf[:, :], x[it * 128:(it + 1) * 128, :], w, f"p1:ld{it}", ld)
        lev = (ld, ld.n)
        P.op("scalar", (lambda xb=xbuf, c=cols: (lambda e: e.activation(
            scr[:, 0:1024], xb[:, :], AF.Copy, bias=0.0, scale=1.0,
            accum_out=c[:, 0:1])))(), [lev, (dv, dv.n)], ((ac, 1),), f"p1:sx{it}")
        P.op("scalar", (lambda xb=xbuf, c=cols: (lambda e: e.activation(
            scr[:, 0:1024], xb[:, :], AF.Square, bias=0.0, scale=1.0,
            accum_out=c[:, 1:2])))(), [(ac, ac.n)], ((ac, 1),), f"p1:sx2{it}")
        emit_rstd(P, cols[:, 0:1], cols[:, 1:2], cols[:, 2:3], cols[:, 3:4],
                  D, LN_EPS, [(ac, ac.n)], f"p1:st{it}")
        xnb = xnt[b]; xnfb = xnf[b]
        ef = P.op("scalar", (lambda xb=xbuf, xn=xnfb, c=cols: (lambda e:
            e.activation(xn[:, :], xb[:, :], AF.Identity, bias=c[:, 3:4],
                         scale=c[:, 2:3])))(), [(dv, dv.n)], ((ac, 1),),
            f"p1:xnf{it}")
        eb = P.op("scalar", (lambda xf=xnfb, xn=xnb: (lambda e: e.activation(
            xn[:, :], xf[:, :], AF.Copy, bias=0.0, scale=1.0)))(),
            [ef[0]], ((ac, 1),), f"p1:xnb{it}")
        xnev = eb[0]
        xnfev = ef[0]
        for half in range(2):
            pf = psf[half]
            pfn = psf[2 + half]
            pbz = psb[half]
            for q in range(4):
                c = half * 4 + q
                P.op("tensor", (lambda xb=xbuf, cc=c, p=pf, qq=q: (lambda e:
                    e.transpose(p[:, qq * 128:(qq + 1) * 128],
                                xb[:, cc * 128:(cc + 1) * 128], idf[:])))(),
                    [lev, (ev, ev.n)] if it * 2 + half >= 2 else [lev, (su, su.n)],
                    ((mm, 1),), f"p1:trf{it}.{c}")
                P.op("tensor", (lambda xn=xnb, cc=c, p=pbz, qq=q: (lambda e:
                    e.transpose(p[:, qq * 128:(qq + 1) * 128],
                                xn[:, cc * 128:(cc + 1) * 128], idb[:])))(),
                    [xnev, (ev, ev.n)] if it * 2 + half >= 2 else [xnev],
                    ((mm, 1),), f"p1:trb{it}.{c}")
                P.op("tensor", (lambda xn=xnfb, cc=c, p=pfn, qq=q: (lambda e:
                    e.transpose(p[:, qq * 128:(qq + 1) * 128],
                                xn[:, cc * 128:(cc + 1) * 128], idf[:])))(),
                    [xnfev, (ev, ev.n)] if it * 2 + half >= 2 else [xnfev],
                    ((mm, 1),), f"p1:trn{it}.{c}")
            mev = (mm, mm.n)
            ef2 = evv[half]
            eb2 = evf[half]
            sth_sems = [P.sem("p1_st0"), P.sem("p1_st1")]
            P.op("scalar", (lambda p=pf, d=ef2: (lambda e: e.activation(
                d[:, :], p[:, 0:512], AF.Copy, bias=0.0, scale=1.0)))(),
                [mev, (sth_sems[half], sth_sems[half].n)] if it * 2 + half >= 2 else [mev],
                ((ev, 1),), f"p1:evf{it}.{half}")
            P.op("vector", (lambda p=pbz, d=eb2: (lambda e: e.tensor_copy(
                d[:, :], p[:, 0:512])))(),
                [mev, (sth_sems[half], sth_sems[half].n)] if it * 2 + half >= 2 else [mev],
                ((ev, 1),), f"p1:evb{it}.{half}")
            P.op("vector", (lambda p=pfn, d=xnfm, hh=half: (lambda e:
                e.tensor_copy(d[:, hh * 512:(hh + 1) * 512], p[:, 0:512])))(),
                [mev, (mm, mm.n)], ((ev, 1),), f"p1:evn{it}.{half}")
            eev = (ev, ev.n)
            sth = sth_sems[half]
            for q in range(4):
                c = half * 4 + q
                P.dma("scalar", x0T[c, :, it * 128:(it + 1) * 128],
                      ef2[:, q * 128:(q + 1) * 128], [eev], f"p1:so{it}.{c}", sth)
                P.dma("scalar", xn0[c, :, it * 128:(it + 1) * 128],
                      eb2[:, q * 128:(q + 1) * 128], [eev], f"p1:sn{it}.{c}", sth)
        fm_ev = (ev, ev.n)
        rpb = psf[4]
        for c in range(DCH):
            P.op("tensor", (lambda cc=c: (lambda e: e.matmul(
                rpb[0:128, 0:8], xnfm[:, cc * 128:(cc + 1) * 128],
                rwT[:, cc * 8:(cc + 1) * 8], start=(cc == 0),
                stop=(cc == DCH - 1), skip_group_check=True)))(),
                ([fm_ev] + rw_ev + bank_ev[4]) if c == 0 else (),
                ((mm, 1),) if c == DCH - 1 else (), f"p1:rt{it}.{c}")
        rt_mm = (mm, mm.n)
        cw_ap, cw_ev = emit_cw_router(P, C, rpb, rt_mm, rt_scr, rt_cols,
                                      f"p1rt{it}")
        bank_ev[4] = [cw_ev]
        P.op("tensor", (lambda: (lambda e: e.transpose(psf[5][0:8, 0:128],
            cw_ap, idf[:])))(), [cw_ev] + bank_ev[5], ((mm, 1),),
            f"p1:ct{it}")
        e9 = P.op("vector", (lambda itt=it: (lambda e: e.tensor_copy(
            cwT_sb[0:8, (itt % 8) * 128:(itt % 8 + 1) * 128],
            psf[5][0:8, 0:128])))(),
            [(mm, mm.n), (P.sem("p1_stc"), P.sem("p1_stc").n)], ((dv, 1),),
            f"p1:ce{it}")
        bank_ev[5] = [e9[0]]
        if it % 8 == 7:
            ts8 = it // 8
            stc = P.sem("p1_stc")
            P.dma("scalar", C["sc"]["cw0"][:, ts8 * 1024:(ts8 + 1) * 1024],
                  cwT_sb[0:8, 0:1024], [e9[0]], f"p1:cs{ts8}", stc)
    P.op("sync", None, [(P.sem("p1_st0"), P.sem("p1_st0").n),
         (P.sem("p1_st1"), P.sem("p1_st1").n),
         (P.sem("p1_stc"), P.sem("p1_stc").n),
         (P.sem("dma_scalar"), P.sem("dma_scalar").n)], (), "p1:fin")
    P.barrier("p1")


class Ring:
    def __init__(self, slices):
        self.s = slices
        self.ev = [[] for _ in slices]
        self.i = 0

    def take(self):
        i = self.i % len(self.s)
        self.i += 1
        return i, self.s[i], list(self.ev[i])

    def release(self, i, evs):
        self.ev[i] = list(evs)


def emit_qkv(P, C, layer):
    nc = P.nc
    bf = C["bf"]; f32 = C["f32"]
    IN = C["in"]; SC = C["sc"]
    xn_src = SC["xn0"] if layer == 0 else SC["zn1"]
    qsc = SC["q0"] if layer == 0 else SC["q1"]
    ksc = SC["k0"] if layer == 0 else SC["k1"]
    vsc = SC["v0"] if layer == 0 else SC["v1"]
    mm = P.sem("qkv_mm")
    ac = P.sem("act"); dv = P.sem("dve")
    su = P.sem("setup")
    xn_sems = [P.sem("qkv_xn0"), P.sem("qkv_xn1")]
    qkst_sems = [P.sem("qkv_qs0"), P.sem("qkv_qs1"), P.sem("qkv_qs2")]
    vst_sems = [P.sem("qkv_vs0"), P.sem("qkv_vs1")]
    aug_st = P.sem("qkv_aug")
    psf = C["ps_f32"]
    bank_ev = C.setdefault("bank_ev", [[] for _ in range(6)])

    if layer == 0:
        wqT = bf.alloc(8192, "wqT"); wkT = bf.alloc(8192, "wkT")
        wvT = bf.alloc(8192, "wvT")
        load_weight_T(P, C, IN["L0_attn_wq"], 1024, 1024, wqT, "wq")
        load_weight_T(P, C, IN["L0_attn_wk"], 1024, 1024, wkT, "wk")
        load_weight_T(P, C, IN["L0_attn_wv"], 1024, 1024, wvT, "wv")
        def lhsT_tile(which, ic, oc):
            t = {"q": wqT, "k": wkT, "v": wvT}[which]
            return t[:, ic * 1024 + oc * 128: ic * 1024 + (oc + 1) * 128]
        def v_rhs(ic, half):
            return wvT[:, ic * 1024 + half * 512: ic * 1024 + half * 512 + 512]
        bq_pp = f32.alloc(8, "bq"); bk_pp = f32.alloc(8, "bk")
        per_partition_bias(P, C, IN["L0_attn_bq"], 1024, bq_pp, "bq")
        per_partition_bias(P, C, IN["L0_attn_bk"], 1024, bk_pp, "bk")
        bvB = f32.alloc(1024, "bvB")
        broadcast_row(P, C, IN["L0_attn_bv"], 1024, bvB, "bvB")
        cs_sb = bf.alloc(2048, "cos"); sn_sb = bf.alloc(2048, "sin")
        perm_sb = bf.alloc(128, "perm")
        P.dma("sync", cs_sb[:, :], IN["rope_cos"], (), "cos")
        P.dma("sync", sn_sb[:, :], IN["rope_sin"], (), "sin")
        P.dma("sync", perm_sb[:, :], IN["rope_perm"], (), "perm")
    else:
        wT = bf.alloc(24576, "wqkvT")
        load_weight_T(P, C, IN["L1_attn_w_qkv"], 3072, 1024, wT, "wqkv")
        def lhsT_tile(which, ic, oc):
            base = {"q": 0, "k": 1024, "v": 2048}[which]
            return wT[:, ic * 3072 + base + oc * 128: ic * 3072 + base + (oc + 1) * 128]
        def v_rhs(ic, half):
            return wT[:, ic * 3072 + 2048 + half * 512: ic * 3072 + 2048 + half * 512 + 512]
        slopes_sb = f32.alloc(16, "slopes")
        P.op("sync", lambda e: e.dma_start(slopes_sb[0:1, 0:16],
             IN["L1_attn_alibi_slopes"].unsqueeze(0)), (),
             ((P.sem("dma_sync"), 16),), "slopes")
    wt_ev = P.sem("wt_ev")
    dsy = P.sem("dma_sync")
    w_ready = [(wt_ev, wt_ev.n), (dsy, dsy.n)]

    xnr = Ring([bf.alloc(DCH * 512, f"xnc{i}") for i in range(2)])

    stg = Ring([bf.alloc(512, f"qk_s{i}") for i in range(3)])
    rtg = Ring([bf.alloc(512, f"qk_r{i}") for i in range(2)])
    t1g = Ring([bf.alloc(512, f"qk_t{i}") for i in range(2)])
    outg = Ring([bf.alloc(512, f"qk_o{i}") for i in range(3)])
    vst = Ring([bf.alloc(1040, f"vst{i}") for i in range(2)])
    for i in range(2):
        P.op("gpsimd", (lambda t=vst.s[i]: (lambda e: e.memset(t[:, :], 1.0)))(),
             (), ((su, 1),), f"vones{i}")
    ones_ev = [(su, su.n)]
    bias_ev = [(dsy, dsy.n)]

    for tc in range(8):
        s = tc // 4
        pos0 = (tc % 4) * 512
        ix, xn_c, relx = xnr.take()
        ldx = xn_sems[ix % 2]
        for c in range(DCH):
            P.dma("sync", xn_c[:, c * 512:(c + 1) * 512],
                  xn_src[c][:, tc * 512:(tc + 1) * 512],
                  relx if c == 0 else (), f"qkv{layer}:xn{tc}.{c}", ldx)
        xn_ev = [(ldx, ldx.n)]
        xn_consumers = []

        def xn_rhs(ic):
            return xn_c[:, ic * 512:(ic + 1) * 512]

        def xn_lhsT(ic, q):
            return xn_c[:, ic * 512 + q * 128: ic * 512 + q * 128 + 128]

        for which in ("q", "k"):
            for oc in range(8):
                bank = (oc + (0 if which == "q" else 1)) % 4
                pb = psf[bank]
                for ic in range(DCH):
                    waits = []
                    if ic == 0:
                        waits = w_ready + xn_ev + bank_ev[bank]
                    P.op("tensor", (lambda l=lhsT_tile(which, ic, oc),
                        r=xn_rhs(ic), p=pb, st_=(ic == 0), sp=(ic == DCH - 1):
                        (lambda e: e.matmul(p[:], l, r, start=st_, stop=sp)))(),
                        waits, ((mm, 1),) if ic == DCH - 1 else (),
                        f"qkv{layer}:{which}mm{oc}.{tc}")
                mm_ev = (mm, mm.n)
                xn_consumers.append(mm_ev)
                i0, sbuf_o, rel = stg.take()
                if layer == 0:
                    b_pp = bq_pp if which == "q" else bk_pp
                    fn = (lambda p=pb, d=sbuf_o, b=b_pp, o=oc: (lambda e:
                        e.activation(d[:, :], p[:], AF.Identity,
                                     bias=b[:, o:o + 1], scale=1.0)))()
                else:
                    sc_ = 0.125 if which == "q" else 1.0
                    fn = (lambda p=pb, d=sbuf_o, sc2=sc_: (lambda e:
                        e.activation(d[:, :], p[:], AF.Copy, bias=0.0,
                                     scale=sc2)))()
                eva = P.op("scalar", fn, [mm_ev] + rel + bias_ev, ((ac, 1),),
                           f"qkv{layer}:{which}ev{oc}.{tc}")
                bank_ev[bank] = [eva[0]]
                if layer == 0:
                    rbank = 4 + oc % 2
                    rpb = psf[rbank]
                    P.op("tensor", (lambda d=sbuf_o, p=rpb: (lambda e:
                        e.matmul(p[:], perm_sb[:, :], d[:, :], start=True,
                                 stop=True)))(),
                        [eva[0]] + bank_ev[rbank], ((mm, 1),),
                        f"qkv0:{which}rot{oc}.{tc}")
                    rot_mm = (mm, mm.n)
                    i1, rot_o, rel1 = rtg.take()
                    ebv = P.op("scalar", (lambda p=rpb, d=rot_o: (lambda e:
                        e.activation(d[:, :], p[:], AF.Copy, bias=0.0,
                                     scale=1.0)))(), [rot_mm] + rel1,
                        ((ac, 1),), f"qkv0:{which}rev{oc}.{tc}")
                    bank_ev[rbank] = [ebv[0]]
                    i2, t1_o, rel2 = t1g.take()
                    e1 = P.op("vector", (lambda a=sbuf_o, d=t1_o,
                        p0=pos0: (lambda e: e.tensor_tensor(d[:, :], a[:, :],
                        cs_sb[:, p0:p0 + 512], ALU.mult)))(),
                        [eva[0]] + rel2, ((dv, 1),), f"qkv0:{which}c{oc}.{tc}")
                    stg.release(i0, [rot_mm, e1[0]])
                    i3, out_o, rel3 = outg.take()
                    e2 = P.op("vector", (lambda a=rot_o, d=out_o,
                        p0=pos0: (lambda e: e.tensor_tensor(d[:, :], a[:, :],
                        sn_sb[:, p0:p0 + 512], ALU.mult)))(),
                        [ebv[0]] + rel3, ((dv, 1),), f"qkv0:{which}s{oc}.{tc}")
                    rtg.release(i1, [e2[0]])
                    e3 = P.op("vector", (lambda a=t1_o, b=out_o: (lambda e:
                        e.tensor_tensor(a[:, :], a[:, :], b[:, :], ALU.add)))(),
                        [e1[0], e2[0]], ((dv, 1),), f"qkv0:{which}a{oc}.{tc}")
                    outg.release(i3, [e3[0]])
                    final = t1_o
                    fev = e3[0]
                    out_slot = i2
                    def relf(evs, _i=i2):
                        t1g.release(_i, evs)
                else:
                    final = sbuf_o
                    fev = eva[0]
                    out_slot = i0
                    def relf(evs, _i=i0):
                        stg.release(_i, evs)
                dst = qsc if which == "q" else ksc
                slot_sem = qkst_sems[out_slot % 3]
                d1 = P.dma("sync", dst[s, 2 * oc, 0:64, pos0:pos0 + 512],
                           final[0:64, :], [fev], f"qkv{layer}:{which}o1",
                           slot_sem)
                d2 = P.dma("sync", dst[s, 2 * oc + 1, 0:64, pos0:pos0 + 512],
                           final[64:128, :], [fev], f"qkv{layer}:{which}o2",
                           slot_sem)
                relf([(slot_sem, slot_sem.n)])

        for q in range(4):
            tt = tc * 4 + q
            iv, vbuf, relv = vst.take()
            evs_half = []
            for half in range(2):
                bank = (tt * 2 + half) % 4
                pb = psf[bank]
                for ic in range(DCH):
                    waits = []
                    if ic == 0:
                        waits = w_ready + xn_ev + bank_ev[bank]
                    P.op("tensor", (lambda l=xn_lhsT(ic, q), r=v_rhs(ic, half),
                        p=pb, st_=(ic == 0), sp=(ic == DCH - 1): (lambda e:
                        e.matmul(p[:], l, r, start=st_, stop=sp)))(),
                        waits, ((mm, 1),) if ic == DCH - 1 else (),
                        f"qkv{layer}:vmm{tt}.{half}")
                mm_ev = (mm, mm.n)
                xn_consumers.append(mm_ev)
                dstv = vbuf[:, half * 520:(half + 1) * 520].rearrange(
                    "p (h j) -> p h j", h=8, j=65)[:, :, 0:64]
                if layer == 0:
                    fn = (lambda p=pb, d=dstv, h=half: (lambda e:
                        e.tensor_tensor(d, p[:], bvB[:, h * 512:(h + 1) * 512],
                                        ALU.add)))()
                    eng = "vector"; sem_ = dv
                else:
                    fn = (lambda p=pb, d=dstv: (lambda e: e.activation(
                        d, p[:], AF.Copy, bias=0.0, scale=1.0)))()
                    eng = "scalar"; sem_ = ac
                w2 = [mm_ev] + relv + ones_ev + (bias_ev if layer == 0 else [])
                eh = P.op(eng, fn, w2, ((sem_, 1),), f"qkv{layer}:vev{tt}.{half}")
                bank_ev[bank] = [eh[0]]
                evs_half.append(eh[0])
                relv = []
            vs_sem = vst_sems[iv % 2]
            dd = P.dma("sync", vsc[tt], vbuf[:, :], evs_half,
                       f"qkv{layer}:vo{tt}", vs_sem)
            vst.release(iv, [dd[0]])
        xnr.release(ix, xn_consumers)

    if layer == 1:
        qpos = C["qpos_row"]
        ar = f32.alloc(4096, "augf")
        ab = bf.alloc(4096, "augb")
        ones_row = bf.alloc(2048, "ones_row")
        P.op("gpsimd", lambda e: e.memset(ones_row[0:1, :], 1.0), (),
             ((su, 1),), "aug:ones")
        one_ev = (su, su.n)
        for h in range(H0):
            for side, sgn in (("k", 1.0), ("q", -1.0)):
                rf = ar[0:1, 0:2048]
                hf = ar[0:1, 2048:4096]
                hb = ab[0:1, 0:2048]
                lb = ab[0:1, 2048:4096]
                sc_ap = slopes_sb[0:1, h:h + 1]
                prev = [(ac, ac.n), (dv, dv.n), (aug_st, aug_st.n),
                        (dsy, dsy.n)]
                P.op("scalar", (lambda r=rf, sa=sc_ap: (lambda e:
                    e.activation(r, qpos[0:1, :], AF.Identity, bias=0.0,
                                 scale=sa)))(), prev, ((ac, 1),),
                    f"aug:{side}{h}a")
                if sgn < 0:
                    P.op("scalar", (lambda r=rf: (lambda e: e.activation(
                        r, r, AF.Copy, bias=0.0, scale=-1.0)))(),
                        [(ac, ac.n)], ((ac, 1),), f"aug:{side}{h}n")
                P.op("scalar", (lambda r=rf, d=hb: (lambda e: e.activation(
                    d, r, AF.Copy, bias=0.0, scale=1.0)))(), [(ac, ac.n)],
                    ((ac, 1),), f"aug:{side}{h}h")
                P.op("scalar", (lambda r2=hb, d=hf: (lambda e: e.activation(
                    d, r2, AF.Copy, bias=0.0, scale=1.0)))(), [(ac, ac.n)],
                    ((ac, 1),), f"aug:{side}{h}hf")
                P.op("vector", (lambda a=rf, b=hf, d=lb: (lambda e:
                    e.tensor_tensor(d, a, b, ALU.subtract)))(),
                    [(ac, ac.n)], ((dv, 1),), f"aug:{side}{h}lo")
                dvev = (dv, dv.n)
                dstt = ksc if side == "k" else qsc
                r0 = 64 if side == "k" else 66
                for s in range(NSEQ):
                    P.dma("sync", dstt[s, h, r0:r0 + 1, :], hb, [dvev],
                          f"aug:{side}{h}s{s}h", aug_st)
                    P.dma("sync", dstt[s, h, r0 + 1:r0 + 2, :], lb, [dvev],
                          f"aug:{side}{h}s{s}l", aug_st)
        for s in range(NSEQ):
            for h in range(H0):
                for r in (66, 67):
                    P.dma("sync", ksc[s, h, r:r + 1, :], ones_row[0:1, :],
                          [one_ev], f"aug:ok{s}.{h}.{r}", aug_st)
                for r in (64, 65):
                    P.dma("sync", qsc[s, h, r:r + 1, :], ones_row[0:1, :],
                          [one_ev], f"aug:oq{s}.{h}.{r}", aug_st)
    fin_waits = [(sx, sx.n) for sx in qkst_sems + vst_sems + [aug_st]]
    P.op("sync", None, fin_waits, (), f"qkv{layer}:fin")
    P.barrier(f"qkv{layer}")


def emit_attn(P, C, layer):
    nc = P.nc
    bf = C["bf"]; f32 = C["f32"]
    IN = C["in"]; SC = C["sc"]
    qsc = SC["q0" if layer == 0 else "q1"]
    ksc = SC["k0" if layer == 0 else "k1"]
    vsc = SC["v0" if layer == 0 else "v1"]
    ssc = SC["s0" if layer == 0 else "s1"]
    KR = 64 if layer == 0 else 68
    mm = P.sem("at_mm")
    ac = P.sem("act"); dv = P.sem("dve")
    ldv = P.sem("at_v")
    q_sems = [P.sem("at_q0"), P.sem("at_q1")]
    k_sems = [P.sem("at_k0"), P.sem("at_k1")]
    s_sems = [P.sem("at_s0"), P.sem("at_s1")]
    su = P.sem("setup"); dsy = P.sem("dma_sync")
    psf = C["ps_f32"]
    bank_ev = C.setdefault("bank_ev", [[] for _ in range(6)])

    woT = bf.alloc(8192, "woT")
    if layer == 0:
        load_weight_T(P, C, IN["L0_attn_wo"], 1024, 1024, woT, "wo")
        bo_pp = f32.alloc(8, "bo")
        per_partition_bias(P, C, IN["L0_attn_bo"], 1024, bo_pp, "bo")
    else:
        load_weight_T(P, C, IN["L1_attn_w_out"], 1024, 1024, woT, "wout")
        bo_pp = None
    wt_ev = P.sem("wt_ev")
    w_ready = [(wt_ev, wt_ev.n), (dsy, dsy.n)]

    v_sb = bf.alloc(NT * 1040, "v_sb")
    for tt in range(NT):
        P.dma("sync", v_sb[:, tt * 1040:(tt + 1) * 1040],
              vsc[tt], (), f"at{layer}:v{tt}", ldv)
    v_ev = [(ldv, ldv.n)]

    def v_tile(s, kt, h):
        col = (s * 16 + kt) * 1040 + h * 65
        return v_sb[:, col:col + 65]

    ones_bb = bf.alloc(64, "ones_bb")
    P.op("gpsimd", lambda e: e.memset(ones_bb[0:2, :], 1.0), [(su, su.n)],
         ((su, 1),), "at:ones")

    den = f32.alloc(1536, "den")
    rec2 = bf.alloc(1024, "rec2")
    recS = f32.alloc(512, "recS")
    attn_fm = bf.alloc(16384, "attn_fm")
    sev = None

    qg = Ring([bf.alloc(2048, f"qg{i}") for i in range(2)])
    kg = Ring([bf.alloc(2048, f"kg{i}") for i in range(2)])

    def load_qk(s, h):
        iq, qb, relq = qg.take()
        ik, kb, relk = kg.take()
        if not relq:
            relq = [(mm, mm.n)]
        if not relk:
            relk = [(mm, mm.n)]
        P.dma("sync", qb[0:KR, :], qsc[s, h, 0:KR, :], relq, f"at:q{s}.{h}",
              q_sems[iq % 2])
        P.dma("sync", kb[0:KR, :], ksc[s, h, 0:KR, :], relk, f"at:k{s}.{h}",
              k_sems[ik % 2])
        return iq, qb, ik, kb, [(q_sems[iq % 2], q_sems[iq % 2].n),
                                (k_sems[ik % 2], k_sems[ik % 2].n)]

    pend = load_qk(0, 0)
    s4r = Ring([f32.alloc(2048, f"s4_{i}") for i in range(2)])
    u4r = Ring([f32.alloc(2048, f"u4_{i}") for i in range(2)])
    a4r = Ring([bf.alloc(2048, f"a4_{i}") for i in range(2)])
    if layer == 1:
        mask4 = f32.alloc(2048, "mask4")
        for dlt in range(4):
            P.op("gpsimd", (lambda d=dlt: (lambda e: e.memset(
                 mask4[:, d * 512:(d + 1) * 512], 1e9)))(), [(su, su.n)],
                 ((su, 1),), f"at:M{dlt}a")
            P.op("gpsimd", (lambda d=dlt: (lambda e:
                 e.affine_select(mask4[:, d * 512:(d + 1) * 512],
                 mask4[:, d * 512:(d + 1) * 512], pattern=[[1, 512]],
                 compare_op=ALU.is_ge, fill=C["filln"], base=-128 * d,
                 channel_multiplier=-1)))(), [(su, su.n)], ((su, 1),),
                 f"at:M{dlt}b")
    mask_ev = [(su, su.n)]

    if layer == 1:
        negt = s4r.s[1][:, 1024:1536]
        nt_u = s4r.s[0][:, 0:512]
        nt_r = s4r.s[0][:, 512:1024]
        nt_a = u4r.s[0][:, 0:512]
        nt_b = u4r.s[0][:, 512:1024]
        atn_neg = bf.alloc(512, "atn_neg")
        P.op("gpsimd", lambda e: e.memset(negt[:, :], -10000.0), [(su, su.n)],
             ((su, 1),), "at:neg0")
        nev = [(su, su.n)]
        P.op("scalar", lambda e: e.activation(nt_u, negt[:, :],
             AF.Square, bias=0.0, scale=1.0), nev, ((ac, 1),), "at:neg1")
        P.op("scalar", lambda e: e.activation(nt_r, nt_u,
             AF.Ln, bias=1.0, scale=1.0), [(ac, ac.n)], ((ac, 1),), "at:neg2")
        P.op("scalar", lambda e: e.activation(nt_r, nt_r,
             AF.Exp, bias=0.0, scale=-0.5), [(ac, ac.n)], ((ac, 1),), "at:neg3")
        P.op("vector", lambda e: e.tensor_tensor(nt_a, nt_r, nt_r, ALU.mult),
             [(ac, ac.n)], ((dv, 1),), "at:nw1")
        P.op("vector", lambda e: e.tensor_tensor(nt_b, nt_a, nt_u, ALU.mult),
             [(dv, dv.n)], ((dv, 1),), "at:nw2")
        P.op("vector", lambda e: e.tensor_tensor(nt_b, nt_b, nt_a, ALU.add),
             [(dv, dv.n)], ((dv, 1),), "at:nw3")
        P.op("vector", lambda e: e.tensor_scalar(nt_b, nt_b, -0.5,
             1.5, ALU.mult, ALU.add), [(dv, dv.n)], ((dv, 1),), "at:nw4")
        P.op("vector", lambda e: e.tensor_tensor(nt_r, nt_r, nt_b, ALU.mult),
             [(dv, dv.n)], ((dv, 1),), "at:nw5")
        P.op("vector", lambda e: e.tensor_tensor(negt[:, :], negt[:, :],
             nt_r, ALU.mult), [(dv, dv.n)], ((dv, 1),), "at:neg4")
        ex9 = P.op("scalar", lambda e: e.activation(atn_neg[:, :], negt[:, :],
             AF.Relu, bias=1.0, scale=1.0), [(dv, dv.n)], ((ac, 1),), "at:neg5")
        neg_ready = [(ac, ac.n)]
        s4r.ev[0] = [(dv, dv.n)]
        s4r.ev[1] = [(dv, dv.n)]
        u4r.ev[0] = [(dv, dv.n)]

    for s in range(NSEQ):
        fm_guard = [(mm, mm.n)] if s else []
        for h in range(H0):
            iq, qb, ik, kb, qk_ev = pend
            if not (s == NSEQ - 1 and h == H0 - 1):
                nh = h + 1 if h < H0 - 1 else 0
                ns = s if h < H0 - 1 else s + 1
                pend = load_qk(ns, nh)
            q_consumers = []
            for qc in range(4):
                kmax = 4 * (qc + 1)
                nb = qc + 1
                ot_bank = 4
                otp = psf[ot_bank]
                stA = {}
                stB = {}
                for b in range(nb + 2):
                    if b < nb:
                        for j in range(4):
                            kt = 4 * b + j
                            pb = psf[j]
                            P.op("tensor", (lambda
                                l=kb[0:KR, kt * 128:(kt + 1) * 128],
                                r=qb[0:KR, qc * 512:(qc + 1) * 512], p=pb:
                                (lambda e: e.matmul(p[:], l, r, start=True,
                                 stop=True, skip_group_check=True)))(),
                                qk_ev + bank_ev[j] + v_ev + mask_ev,
                                ((mm, 1),), f"at{layer}:s{s}.{h}.{qc}.{kt}")
                            q_consumers.append((mm, mm.n))
                        sc_mm = (mm, mm.n)
                        diag = (b == nb - 1)
                        if layer == 0:
                            ia, a4, rela = a4r.take()
                            ee = None
                            for j in range(4):
                                ee = P.op("scalar", (lambda p=psf[j],
                                    d=a4[:, j * 512:(j + 1) * 512]: (lambda e:
                                    e.activation(d, p[:], AF.Exp, bias=0.0,
                                    scale=0.125)))(),
                                    ([sc_mm] + rela) if j == 0 else [sc_mm],
                                    ((ac, 1),), f"at0:e{s}.{h}.{qc}.{b}.{j}")
                                bank_ev[j] = [ee[0]]
                            aev = ee[0]
                            if diag:
                                em = P.op("gpsimd", (lambda t=a4: (lambda e:
                                    e.affine_select(t[:, :], t[:, :],
                                    pattern=[[-128, 4], [1, 512]],
                                    compare_op=ALU.is_ge, fill=C["fill0"],
                                    base=0, channel_multiplier=-1)))(),
                                    [aev], ((su, 1),),
                                    f"at0:m{s}.{h}.{qc}.{b}")
                                aev = em[0]
                            stB[b] = (a4, ia, None, None, aev, diag)
                        else:
                            isf, s4, rels = s4r.take()
                            eev = None
                            for j in range(4):
                                if diag:
                                    eev = P.op("vector", (lambda p=psf[j],
                                        d=s4[:, j * 512:(j + 1) * 512],
                                        m_=mask4[:, j * 512:(j + 1) * 512]:
                                        (lambda e: e.tensor_tensor(d, p[:], m_,
                                        ALU.min)))(),
                                        ([sc_mm] + rels) if j == 0 else [sc_mm],
                                        ((dv, 1),),
                                        f"at1:mn{s}.{h}.{qc}.{b}.{j}")
                                else:
                                    eev = P.op("vector", (lambda p=psf[j],
                                        d=s4[:, j * 512:(j + 1) * 512]:
                                        (lambda e: e.tensor_copy(d, p[:])))(),
                                        ([sc_mm] + rels) if j == 0 else [sc_mm],
                                        ((dv, 1),),
                                        f"at1:cp{s}.{h}.{qc}.{b}.{j}")
                                bank_ev[j] = [eev[0]]
                            stA[b] = (s4, isf, eev[0], diag)
                    if layer == 1 and (b - 1) in stA:
                        s4p, isfp, s_evp, diagp = stA.pop(b - 1)
                        iu, u4, relu = u4r.take()
                        ia, a4, rela = a4r.take()
                        eu = P.op("scalar", (lambda a=s4p, d=u4: (lambda e:
                            e.activation(d[:, :], a[:, :], AF.Square,
                            bias=0.0, scale=1.0)))(), [s_evp] + relu,
                            ((ac, 1),), f"at1:sq{s}.{h}.{qc}.{b - 1}")
                        el = P.op("scalar", (lambda d=u4: (lambda e:
                            e.activation(d[:, :], d[:, :], AF.Ln, bias=1.0,
                            scale=1.0)))(), [eu[0]], ((ac, 1),),
                            f"at1:ln{s}.{h}.{qc}.{b - 1}")
                        er = P.op("scalar", (lambda d=u4: (lambda e:
                            e.activation(d[:, :], d[:, :], AF.Exp,
                            bias=0.0, scale=-0.5)))(), [el[0]], ((ac, 1),),
                            f"at1:ex{s}.{h}.{qc}.{b - 1}")
                        et = P.op("vector", (lambda a=s4p, b_=u4: (lambda e:
                            e.tensor_tensor(b_[:, :], a[:, :], b_[:, :],
                            ALU.mult)))(), [er[0]], ((dv, 1),),
                            f"at1:t{s}.{h}.{qc}.{b - 1}")
                        s4r.release(isfp, [et[0]])
                        stB[b - 1] = (a4, ia, u4, iu, et[0], diagp)
                        C.setdefault("at1_rela", {})[ia] = rela
                    bfin = b - (2 if layer == 1 else 1)
                    if bfin in stB:
                        a4p, iap, u4p, iup, evp, diagp = stB.pop(bfin)
                        if layer == 1:
                            ep = P.op("scalar", (lambda a=u4p, d=a4p:
                                (lambda e: e.activation(d[:, :], a[:, :],
                                AF.Relu, bias=1.0, scale=1.0)))(),
                                [evp] + C["at1_rela"].get(iap, []), ((ac, 1),),
                                f"at1:p{s}.{h}.{qc}.{bfin}")
                            u4r.release(iup, [ep[0]])
                            evp = ep[0]
                            if diagp:
                                em = P.op("gpsimd", (lambda t=a4p: (lambda e:
                                    e.affine_select(t[:, :], t[:, :],
                                    pattern=[[-128, 4], [1, 512]],
                                    compare_op=ALU.is_ge, fill=C["fill0"],
                                    base=0, channel_multiplier=-1)))(),
                                    [evp], ((su, 1),),
                                    f"at1:z{s}.{h}.{qc}.{bfin}")
                                evp = em[0]
                        emv = None
                        last_b = (bfin == nb - 1) and (layer == 0 or kmax == 16)
                        for j in range(4):
                            kt = 4 * bfin + j
                            emv = P.op("tensor", (lambda
                                l=v_tile(s, kt, h),
                                r=a4p[:, j * 512:(j + 1) * 512], p=otp,
                                st_=(kt == 0), sp=(last_b and j == 3):
                                (lambda e: e.matmul(
                                p[0:65, :], l, r, start=st_, stop=sp,
                                skip_group_check=True)))(),
                                ([evp] + (bank_ev[ot_bank] if kt == 0 else [])
                                 + v_ev), ((mm, 1),),
                                f"at{layer}:av{s}.{h}.{qc}.{kt}")
                        a4r.release(iap, [emv[0]])
                if layer == 1:
                    for kt in range(kmax, 16):
                        P.op("tensor", (lambda l=v_tile(s, kt, h), p=otp,
                            sp=(kt == 15): (lambda e: e.matmul(
                            p[0:65, :], l, atn_neg[:, :],
                            start=False, stop=sp,
                            skip_group_check=True)))(),
                            neg_ready + v_ev, ((mm, 1),),
                            f"at1:avm{s}.{h}.{qc}.{kt}")
                ot_mm = (mm, mm.n)
                e1 = P.op("scalar", (lambda p=otp: (lambda e: e.activation(
                    den[0:1, 0:512], p[64:65, 0:512], AF.Copy, bias=0.0,
                    scale=1.0)))(), [ot_mm, (dv, dv.n)], ((ac, 1),),
                    f"at{layer}:dn{s}.{h}.{qc}")
                if layer == 1:
                    e1 = P.op("vector", lambda e: e.tensor_scalar(
                        den[0:1, 0:512], den[0:1, 0:512], 2e-6, None, ALU.add),
                        [e1[0]], ((dv, 1),), f"at1:eps{s}.{h}.{qc}")
                e2 = P.op("vector", lambda e: e.reciprocal(den[0:1, 512:1024],
                    den[0:1, 0:512]), [e1[0]], ((dv, 1),),
                    f"at{layer}:rc{s}.{h}.{qc}")
                e3 = P.op("scalar", lambda e: e.activation(rec2[0:1, 0:512],
                    den[0:1, 512:1024], AF.Copy, bias=0.0, scale=1.0),
                    [e2[0]] + bank_ev[2], ((ac, 1),), f"at{layer}:rh{s}.{h}.{qc}")
                e4 = P.op("scalar", lambda e: e.activation(den[0:1, 1024:1536],
                    rec2[0:1, 0:512], AF.Copy, bias=0.0, scale=1.0),
                    [e3[0]], ((ac, 1),), f"at{layer}:rhf{s}.{h}.{qc}")
                e5 = P.op("vector", lambda e: e.tensor_tensor(rec2[0:1,
                    512:1024], den[0:1, 512:1024], den[0:1, 1024:1536],
                    ALU.subtract), [e4[0]], ((dv, 1),),
                    f"at{layer}:rl{s}.{h}.{qc}")
                rb = psf[2]
                P.op("tensor", lambda e: e.matmul(rb[0:64, 0:512],
                    ones_bb[0:1, 0:64], rec2[0:1, 0:512], start=True,
                    stop=False, skip_group_check=True),
                    [e3[0]] + bank_ev[2], ((mm, 1),), f"at{layer}:b1{s}.{h}.{qc}")
                P.op("tensor", lambda e: e.matmul(rb[0:64, 0:512],
                    ones_bb[0:1, 0:64], rec2[0:1, 512:1024], start=False,
                    stop=True, skip_group_check=True),
                    [e5[0]], ((mm, 1),), f"at{layer}:b2{s}.{h}.{qc}")
                bmv = (mm, mm.n)
                e6 = P.op("scalar", lambda e: e.activation(recS[0:64, 0:512],
                    rb[0:64, 0:512], AF.Copy, bias=0.0, scale=1.0),
                    [bmv, (dv, dv.n)], ((ac, 1),), f"at{layer}:re{s}.{h}.{qc}")
                bank_ev[2] = [e6[0]]
                dstp = attn_fm[64 * (h % 2):64 * (h % 2) + 64,
                               (h // 2) * 2048 + qc * 512:
                               (h // 2) * 2048 + qc * 512 + 512]
                e7 = P.op("vector", (lambda p=otp, d=dstp: (lambda e:
                    e.tensor_tensor(d, p[0:64, 0:512], recS[0:64, 0:512],
                    ALU.mult)))(), [e6[0]] + fm_guard, ((dv, 1),),
                    f"at{layer}:o{s}.{h}.{qc}")
                fm_guard = []
                bank_ev[ot_bank] = [e7[0]]
            qg.release(iq, [q_consumers[-1]])
            kg.release(ik, [q_consumers[-1]])
        sev = Ring([u4r.s[1][:, 0:512]])
        sev.ev[0] = list(u4r.ev[1]) + C.get("sev_rel", [])
        fm_ev = [(dv, dv.n)]
        for oc in range(8):
            for tc2 in range(4):
                bank = 4 + (oc * 4 + tc2) % 2
                pb = psf[bank]
                for ic in range(DCH):
                    waits = []
                    if ic == 0:
                        waits = w_ready + fm_ev + bank_ev[bank]
                    P.op("tensor", (lambda l=woT[:, ic * 1024 + oc * 128:
                        ic * 1024 + (oc + 1) * 128],
                        r=attn_fm[:, ic * 2048 + tc2 * 512:
                                  ic * 2048 + tc2 * 512 + 512],
                        p=pb, st_=(ic == 0), sp=(ic == DCH - 1):
                        (lambda e: e.matmul(p[:], l, r, start=st_, stop=sp,
                         skip_group_check=True)))(), waits,
                        ((mm, 1),) if ic == DCH - 1 else (),
                        f"at{layer}:wo{s}.{oc}.{tc2}")
                wo_mm = (mm, mm.n)
                iw, wbuf, relw = sev.take()
                if layer == 0:
                    fn = (lambda p=pb, d=wbuf, o=oc: (lambda e: e.activation(
                        d[:, :], p[:], AF.Identity, bias=bo_pp[:, o:o + 1],
                        scale=1.0)))()
                else:
                    fn = (lambda p=pb, d=wbuf: (lambda e: e.activation(
                        d[:, :], p[:], AF.Copy, bias=0.0, scale=1.0)))()
                ew = P.op("scalar", fn, [wo_mm] + relw, ((ac, 1),),
                          f"at{layer}:we{s}.{oc}.{tc2}")
                bank_ev[bank] = [ew[0]]
                ssem = s_sems[iw % 2]
                dd = P.dma("scalar", ssc[oc][:, s * 2048 + tc2 * 512:
                           s * 2048 + tc2 * 512 + 512], wbuf[:, :], [ew[0]],
                           f"at{layer}:ws{s}.{oc}.{tc2}", ssem)
                sev.release(iw, [(ssem, ssem.n)])
                C["sev_rel"] = [(ssem, ssem.n)]
                u4r.ev[1] = list(set(u4r.ev[1] + [(ssem, ssem.n)]))
    P.op("sync", None, [(s_sems[0], s_sems[0].n), (s_sems[1], s_sems[1].n),
         (P.sem("dma_scalar"), P.sem("dma_scalar").n)], (), f"at{layer}:fin")
    P.barrier(f"at{layer}")


def emit_moe0(P, C):
    nc = P.nc
    bf = C["bf"]; f32 = C["f32"]
    IN = C["in"]; SC = C["sc"]
    w1 = IN["L0_ffn_w1"]; w2 = IN["L0_ffn_w2"]
    ld = P.sem("moe_lds"); mm = P.sem("moe_mm")
    cw_sem = P.sem("moe_cw")
    w1_sems = [P.sem("moe_w10"), P.sem("moe_w11")]
    w2_sems = [P.sem("moe_w20"), P.sem("moe_w21")]
    ac = P.sem("act"); dv = P.sem("dve"); st = P.sem("moe_st")
    su = P.sem("setup"); dsy = P.sem("dma_sync"); dgp = P.sem("dma_gpsimd")
    dsc = P.sem("dma_scalar")
    psf = C["ps_f32"]
    bank_ev = C.setdefault("bank_ev", [[] for _ in range(6)])

    xn_sb = Ring([bf.alloc(8192, "mxn0")])
    w1r = Ring([bf.alloc(8192, f"mw1{i}") for i in range(2)])
    w2r = Ring([bf.alloc(4096, f"mw2{i}") for i in range(2)])
    h_sb = bf.alloc(32768, "h_sb")
    y_acc = f32.alloc(8192, "y_acc")
    cwB = Ring([f32.alloc(1024, "cwB0")])
    gtmp = f32.alloc(1024, "gtmp")
    ytmp = gtmp[:, 0:512]
    stmp = f32.alloc(1024, "stmp")
    xtmp = f32.alloc(1024, "xtmp")

    xn0 = SC["xn0"]
    for ts in range(4):
        tsl = slice(ts * 1024, (ts + 1) * 1024)
        ixn, xn_ts, relx = xn_sb.take()
        for c in range(DCH):
            P.dma("sync", xn_ts[:, c * 1024:(c + 1) * 1024],
                  xn0[c][:, tsl], relx if c == 0 else (), f"m0:xn{ts}.{c}", ld)
        xn_ev = [(ld, ld.n)]
        cw_done = []
        ez = P.op("gpsimd", lambda e: e.memset(y_acc[:, :], 0.0),
                  [(dv, dv.n), (ac, ac.n)], ((su, 1),), f"m0:z{ts}")
        acc_ev = [(su, su.n)]
        xn_consumers = []
        for ex in range(E0):
            icb, cwb, relc = cwB.take()
            P.dma("gpsimd", cwb[:, :],
                  SC["cw0"][ex:ex + 1, tsl].partition_broadcast(128),
                  cw_done + relc, f"m0:cwb{ts}.{ex}", cw_sem)
            cw_ev = [(cw_sem, cw_sem.n)]
            h_done = []
            for hb in range(4):
                iw, w1b, rel1 = w1r.take()
                src = w1[ex][:, hb * 1024:(hb + 1) * 1024].rearrange(
                    "(ic p) c -> p ic c", p=128)
                w1s = w1_sems[iw % 2]
                P.dma("gpsimd", w1b[:, :].rearrange("p (ic c) -> p ic c", ic=8),
                      src, rel1, f"m0:w1{ts}.{ex}.{hb}", w1s)
                w1_ev = (w1s, w1s.n)
                for ht in range(8):
                    for tcl in range(2):
                        bank = (ht * 2 + tcl) % 2
                        pb = psf[bank]
                        for ic in range(DCH):
                            P.op("tensor", (lambda l=w1b[:, ic * 1024 + ht * 128:
                                ic * 1024 + ht * 128 + 128],
                                r=xn_ts[:, ic * 1024 + tcl * 512:
                                        ic * 1024 + tcl * 512 + 512],
                                p=pb, st_=(ic == 0), sp=(ic == DCH - 1):
                                (lambda e: e.matmul(p[:], l, r, start=st_,
                                 stop=sp, skip_group_check=True)))(),
                                ([w1_ev] + xn_ev + bank_ev[bank]) if ic == 0 else (),
                                ((mm, 1),) if ic == DCH - 1 else (),
                                f"m0:h{ts}.{ex}.{hb}.{ht}.{tcl}")
                        h_mm = (mm, mm.n)
                        if ic == DCH - 1:
                            xn_consumers.append(h_mm)
                        dst = h_sb[:, (hb * 8 + ht) * 1024 + tcl * 512:
                                   (hb * 8 + ht) * 1024 + tcl * 512 + 512]
                        eg = P.op("scalar", (lambda p=pb, d=dst: (lambda e:
                            e.activation(d, p[:], AF.Gelu, bias=0.0,
                            scale=1.0)))(), [h_mm], ((ac, 1),),
                            f"m0:g{ts}.{ex}.{hb}.{ht}.{tcl}")
                        bank_ev[bank] = [eg[0]]
                        h_done.append(eg[0])
                w1r.release(iw, [h_done[-1]])
            h_ev = [(ac, ac.n)]
            if DEBUG_MOE == 'h' and ts == 0 and ex == 0:
                for hcd in range(32):
                    P.dma("scalar", SC["dbg"][hcd],
                          h_sb[:, hcd * 1024:(hcd + 1) * 1024], h_ev,
                          f"dbg{hcd}", st)
            for dt in range(8):
                iw2, w2b, rel2 = w2r.take()
                src = w2[ex][:, dt * 128:(dt + 1) * 128].rearrange(
                    "(hc p) c -> p hc c", p=128)
                w2s = w2_sems[iw2 % 2]
                P.dma("gpsimd", w2b[:, :].rearrange("p (hc c) -> p hc c", hc=32),
                      src, rel2, f"m0:w2{ts}.{ex}.{dt}", w2s)
                w2_ev = (w2s, w2s.n)
                y_mms = []
                for tcl in range(2):
                    bank = 2 + (dt * 2 + tcl) % 2
                    pb = psf[bank]
                    for hc in range(32):
                        P.op("tensor", (lambda l=w2b[:, hc * 128:(hc + 1) * 128],
                            r=h_sb[:, hc * 1024 + tcl * 512:
                                   hc * 1024 + tcl * 512 + 512],
                            p=pb, st_=(hc == 0), sp=(hc == 31): (lambda e:
                            e.matmul(p[:], l, r, start=st_, stop=sp,
                             skip_group_check=True)))(),
                            ([w2_ev] + h_ev + bank_ev[bank]) if hc == 0 else (),
                            ((mm, 1),) if hc == 31 else (),
                            f"m0:y{ts}.{ex}.{dt}.{tcl}")
                    y_mm = (mm, mm.n)
                    y_mms.append(y_mm)
                    et = P.op("vector", (lambda p=pb,
                        c_=cwb[:, tcl * 512:(tcl + 1) * 512]: (lambda e:
                        e.tensor_tensor(ytmp, p[:], c_, ALU.mult)))(),
                        [y_mm, (st, st.n)] + cw_ev + acc_ev + [(dv, dv.n)],
                        ((dv, 1),), f"m0:ym{ts}.{ex}.{dt}.{tcl}")
                    ea = P.op("vector", (lambda d=y_acc[:, dt * 1024 + tcl * 512:
                        dt * 1024 + tcl * 512 + 512]: (lambda e:
                        e.tensor_tensor(d, d, ytmp, ALU.add)))(),
                        [(dv, dv.n)], ((dv, 1),), f"m0:ya{ts}.{ex}.{dt}.{tcl}")
                    bank_ev[bank] = [ea[0]]
                w2r.release(iw2, y_mms)
            cwB.release(icb, [(P.sems["dve"], P.sems["dve"].n)])
            if DEBUG_MOE and ts == 0 and ex < 4:
                for dtd in range(8):
                    P.dma("scalar", SC["dbg"][ex * 8 + dtd],
                          y_acc[:, dtd * 1024:(dtd + 1) * 1024],
                          [(dv, dv.n)], f"dbgy{ex}.{dtd}", st)
        xn_sb.release(ixn, xn_consumers[-8:])
        for dt in range(8):
            esl = P.op("scalar", (lambda s_=y_acc[:, dt * 1024:(dt + 1) * 1024]:
                (lambda e: e.activation(gtmp[:, :], s_, AF.Silu, bias=0.0,
                scale=1.0)))(), [(dv, dv.n), (st, st.n)], ((ac, 1),),
                f"m0:si{ts}.{dt}")
            cmb = P.sem("moe_cmb")
            P.dma("sync", stmp[:, :], SC["s0"][dt][:, tsl], [(dv, dv.n)],
                  f"m0:ls{ts}.{dt}", cmb)
            P.dma("sync", xtmp[:, :], SC["x0T"][dt][:, tsl], [(dv, dv.n)],
                  f"m0:lx{ts}.{dt}", cmb)
            lev = (cmb, cmb.n)
            em = P.op("vector", lambda e: e.tensor_tensor(gtmp[:, :], gtmp[:, :],
                stmp[:, :], ALU.mult), [esl[0], lev], ((dv, 1),),
                f"m0:sg{ts}.{dt}")
            ex2 = P.op("vector", lambda e: e.tensor_tensor(gtmp[:, :], gtmp[:, :],
                xtmp[:, :], ALU.add), [(dv, dv.n)], ((dv, 1),),
                f"m0:x1{ts}.{dt}")
            P.dma("scalar", SC["x1T"][dt][:, tsl], gtmp[:, :], [ex2[0]],
                  f"m0:so{ts}.{dt}", st)
    P.op("sync", None, [(st, st.n), (dsc, dsc.n)], (), "m0:fin")
    P.barrier("moe0")


def emit_p3_attn0(P, C):
    emit_attn(P, C, 0)


def emit_p7_attn1(P, C):
    emit_attn(P, C, 1)


def emit_p2_qkv0(P, C):
    emit_qkv(P, C, 0)


def emit_p6_qkv1(P, C):
    emit_qkv(P, C, 1)


def emit_ln1(P, C):
    nc = P.nc
    bf = C["bf"]; f32 = C["f32"]
    IN = C["in"]; SC = C["sc"]
    ld = P.sem("l1_ld"); ldg = P.sem("l1_ldg"); mm = P.sem("l1_mm")
    ac = P.sem("act"); dv = P.sem("dve"); st = P.sem("l1_st")
    su = P.sem("setup"); dsy = P.sem("dma_sync")
    psf = C["ps_f32"]
    bank_ev = C.setdefault("bank_ev", [[] for _ in range(6)])
    idf = C["ident_f32"]

    ones_col = bf.alloc(1, "ones_col")
    P.op("gpsimd", lambda e: e.memset(ones_col[:, 0:1], 1.0), (), ((su, 1),),
         "l1:ones")
    g_pp = f32.alloc(8, "g_pp"); b_pp = f32.alloc(8, "b_pp")
    per_partition_bias(P, C, IN["L1_ffn_ln_g"], 1024, g_pp, "lng")
    per_partition_bias(P, C, IN["L1_ffn_ln_b"], 1024, b_pp, "lnb")
    gw_sb = f32.alloc(48, "gw")
    for c in range(DCH):
        P.dma("sync", gw_sb[:, c * 6:(c + 1) * 6],
              IN["L1_ffn_gate_w"][c * 128:(c + 1) * 128, :], (), f"l1:gw{c}")
    gbB = f32.alloc(6, "gbB")
    broadcast_row(P, C, IN["L1_ffn_gate_b"], 6, gbB, "gbB")
    w_ev = [(dsy, dsy.n), (su, su.n)]

    x1c = f32.alloc(4096, "x1c")
    xb = bf.alloc(4096, "xb")
    sq = bf.alloc(512, "sq")
    rows = f32.alloc(4 * 512, "rows")
    z2c = f32.alloc(4096, "z2c")
    bcA = f32.alloc(512, "bcA")
    bcB = f32.alloc(512, "bcB")
    rt_scr = f32.alloc(24, "g_scr")
    rt_cols = f32.alloc(8, "g_cols")
    cwT_sb = f32.alloc(512 * 2, "cw1T")
    lnrows = C["sc"]["lnrows"]

    def rowslice(i):
        return rows[0:1, i * 512:(i + 1) * 512]

    for tc in range(8):
        tsl = slice(tc * 512, (tc + 1) * 512)
        for c in range(DCH):
            P.dma("sync", x1c[:, c * 512:(c + 1) * 512],
                  SC["x1T"][c][:, tsl],
                  [(mm, mm.n), (ac, ac.n), (dv, dv.n), (st, st.n)] if c == 0 else (),
                  f"l1:x{tc}.{c}", ld)
        lev = [(ld, ld.n)]
        for phase in range(2):
            for c in range(DCH):
                P.op("vector", (lambda cc=c: (lambda e: e.tensor_copy(
                    xb[:, cc * 512:(cc + 1) * 512],
                    x1c[:, cc * 512:(cc + 1) * 512])))(),
                    (lev + [(mm, mm.n), (st, st.n)]) if c == 0 else (), ((dv, 1),),
                    f"l1:cb{phase}.{tc}.{c}")
            cb_ev = [(dv, dv.n)]
            pb = psf[4]
            for c in range(DCH):
                P.op("tensor", (lambda cc=c: (lambda e: e.matmul(
                    pb[0:1, 0:512], ones_col[:, 0:1],
                    xb[:, cc * 512:(cc + 1) * 512], start=(cc == 0),
                    stop=(cc == DCH - 1), skip_group_check=True)))(),
                    (cb_ev + w_ev + bank_ev[4]) if c == 0 else (),
                    ((mm, 1),) if c == DCH - 1 else (), f"l1:sx{phase}.{tc}")
            sx_mm = (mm, mm.n)
            pb2 = psf[5]
            for c in range(DCH):
                eq = P.op("vector", (lambda cc=c: (lambda e: e.tensor_tensor(
                    sq[:, :], xb[:, cc * 512:(cc + 1) * 512],
                    xb[:, cc * 512:(cc + 1) * 512], ALU.mult)))(),
                    [(mm, mm.n), (dv, dv.n)], ((dv, 1),),
                    f"l1:sq{phase}.{tc}.{c}")
                P.op("tensor", (lambda cc=c: (lambda e: e.matmul(
                    pb2[0:1, 0:512], ones_col[:, 0:1], sq[:, :],
                    start=(cc == 0), stop=(cc == DCH - 1),
                    skip_group_check=True)))(),
                    ([eq[0]] + bank_ev[5]) if c == 0 else [eq[0]],
                    ((mm, 1),), f"l1:s2{phase}.{tc}.{c}")
            s2_mm = (mm, mm.n)
            e1 = P.op("scalar", lambda e: e.activation(rowslice(0),
                pb[0:1, 0:512], AF.Copy, bias=0.0, scale=1.0 / D),
                [sx_mm, (dv, dv.n)], ((ac, 1),), f"l1:mu{phase}.{tc}")
            bank_ev[4] = [e1[0]]
            e2 = P.op("scalar", lambda e: e.activation(rowslice(1),
                pb2[0:1, 0:512], AF.Copy, bias=0.0, scale=1.0 / D),
                [s2_mm, (ac, ac.n)], ((ac, 1),), f"l1:m2{phase}.{tc}")
            bank_ev[5] = [e2[0]]
            P.op("vector", lambda e: e.scalar_tensor_tensor(rowslice(2),
                rowslice(0), 1.0, rowslice(0), ALU.mult, ALU.mult),
                [e2[0], (st, st.n)], ((dv, 1),), f"l1:v0{phase}.{tc}")
            P.op("vector", lambda e: e.tensor_tensor(rowslice(2), rowslice(1),
                rowslice(2), ALU.subtract), [(dv, dv.n)], ((dv, 1),),
                f"l1:v1{phase}.{tc}")
            P.op("vector", lambda e: e.tensor_scalar(rowslice(2), rowslice(2),
                float(LN_EPS), None, ALU.add), [(dv, dv.n)], ((dv, 1),),
                f"l1:v2{phase}.{tc}")
            e3 = P.op("scalar", lambda e: e.activation(rowslice(2), rowslice(2),
                AF.Ln, bias=0.0, scale=1.0), [(dv, dv.n)], ((ac, 1),),
                f"l1:v3{phase}.{tc}")
            e4 = P.op("scalar", lambda e: e.activation(rowslice(2), rowslice(2),
                AF.Exp, bias=0.0, scale=-0.5), [e3[0]], ((ac, 1),),
                f"l1:v4{phase}.{tc}")
            e5 = P.op("vector", lambda e: e.scalar_tensor_tensor(rowslice(3),
                rowslice(0), -1.0, rowslice(2), ALU.mult, ALU.mult),
                [e4[0]], ((dv, 1),), f"l1:v5{phase}.{tc}")
            d1 = P.dma("scalar", lnrows[0:1, tsl], rowslice(2), [e5[0]],
                       f"l1:r2{phase}.{tc}", st)
            d2 = P.dma("scalar", lnrows[1:2, tsl], rowslice(3), [e5[0]],
                       f"l1:r3{phase}.{tc}", st)
            b1 = P.dma("gpsimd", bcA[:, :],
                       lnrows[0:1, tsl].partition_broadcast(128),
                       [d2[0], (dv, dv.n)], f"l1:bA{phase}.{tc}", ldg)
            b2 = P.dma("gpsimd", bcB[:, :],
                       lnrows[1:2, tsl].partition_broadcast(128),
                       [d2[0], (dv, dv.n)], f"l1:bB{phase}.{tc}", ldg)
            bev = [(ldg, ldg.n)]
            zdst = x1c if phase == 0 else z2c
            for c in range(DCH):
                P.op("vector", (lambda cc=c, zz=zdst: (lambda e:
                    e.tensor_tensor(zz[:, cc * 512:(cc + 1) * 512],
                    x1c[:, cc * 512:(cc + 1) * 512], bcA[:, :], ALU.mult)))(),
                    bev if c == 0 else (), ((dv, 1),), f"l1:n0{phase}.{tc}.{c}")
                P.op("vector", (lambda cc=c, zz=zdst: (lambda e:
                    e.tensor_tensor(zz[:, cc * 512:(cc + 1) * 512],
                    zz[:, cc * 512:(cc + 1) * 512], bcB[:, :], ALU.add)))(),
                    [(dv, dv.n)], ((dv, 1),), f"l1:n1{phase}.{tc}.{c}")
            nev = [(dv, dv.n)]
            if phase == 0:
                for c in range(DCH):
                    P.dma("scalar", SC["z1f"][c][:, tsl],
                          x1c[:, c * 512:(c + 1) * 512], nev, f"l1:zf{tc}.{c}",
                          st)
                for c in range(DCH):
                    eb = P.op("scalar", (lambda cc=c: (lambda e: e.activation(
                        xb[:, cc * 512:(cc + 1) * 512],
                        x1c[:, cc * 512:(cc + 1) * 512], AF.Copy, bias=0.0,
                        scale=1.0)))(), nev + [(mm, mm.n)], ((ac, 1),),
                        f"l1:zb{tc}.{c}")
                    P.dma("scalar", SC["zn1"][c][:, tsl],
                          xb[:, c * 512:(c + 1) * 512], [eb[0]],
                          f"l1:zn{tc}.{c}", st)
            else:
                for c in range(DCH):
                    ea2 = P.op("vector", (lambda cc=c: (lambda e:
                        e.tensor_scalar(z2c[:, cc * 512:(cc + 1) * 512],
                        z2c[:, cc * 512:(cc + 1) * 512], g_pp[:, cc:cc + 1],
                        b_pp[:, cc:cc + 1], ALU.mult, ALU.add)))(),
                        [(dv, dv.n)], ((dv, 1),), f"l1:af{tc}.{c}")
                    eb2 = P.op("scalar", (lambda cc=c: (lambda e: e.activation(
                        xb[:, cc * 512:(cc + 1) * 512],
                        z2c[:, cc * 512:(cc + 1) * 512], AF.Copy, bias=0.0,
                        scale=1.0)))(), [ea2[0]], ((ac, 1),), f"l1:ab{tc}.{c}")
                    P.dma("scalar", SC["zn2"][c][:, tsl],
                          xb[:, c * 512:(c + 1) * 512], [eb2[0]],
                          f"l1:z2{tc}.{c}", st)
        for q in range(4):
            pb = psf[4]
            for c in range(DCH):
                P.op("tensor", (lambda cc=c, qq=q: (lambda e: e.matmul(
                    pb[0:128, 0:6], z2c[:, cc * 512 + qq * 128:
                                        cc * 512 + qq * 128 + 128],
                    gw_sb[:, cc * 6:(cc + 1) * 6], start=(cc == 0),
                    stop=(cc == DCH - 1), skip_group_check=True)))(),
                    ([(dv, dv.n), (ac, ac.n)] + w_ev + bank_ev[4]) if c == 0 else (),
                    ((mm, 1),) if c == DCH - 1 else (), f"l1:gt{tc}.{q}")
            gt_mm = (mm, mm.n)
            cw_ap, cw_ev = emit_cw_gate_biased(P, C, pb, gt_mm, rt_scr, rt_cols,
                                               gbB, f"l1g{tc}.{q}")
            bank_ev[4] = [cw_ev]
            P.op("tensor", (lambda: (lambda e: e.transpose(psf[5][0:6, 0:128],
                cw_ap, idf[:])))(), [cw_ev] + bank_ev[5], ((mm, 1),),
                f"l1:ct{tc}.{q}")
            e9 = P.op("vector", (lambda qq=q: (lambda e: e.tensor_copy(
                cwT_sb[0:6, qq * 128:(qq + 1) * 128], psf[5][0:6, 0:128])))(),
                [(mm, mm.n), (st, st.n)], ((dv, 1),), f"l1:ce{tc}.{q}")
            bank_ev[5] = [e9[0]]
        P.dma("scalar", C["sc"]["cw1"][:, tsl], cwT_sb[0:6, 0:512],
              [(P.sems["dve"], P.sems["dve"].n)], f"l1:cs{tc}", st)
    P.op("sync", None, [(st, st.n), (P.sem("dma_scalar"), P.sem("dma_scalar").n)],
         (), "l1:fin")
    P.barrier("ln1")


def emit_cw_gate_biased(P, C, pb, mm_ev, scr, cols, gbB, tag):
    ac = P.sem("act"); dv = P.sem("dve")
    ne = 6
    ll = scr[:, 0:ne]; eqt = scr[:, 8:8 + ne]; cw = scr[:, 16:16 + ne]
    c_m1 = cols[:, 1:2]; c_m2 = cols[:, 2:3]
    c_d = cols[:, 3:4]; c_r = cols[:, 4:5]; c_nm = cols[:, 5:6]
    e0 = P.op("vector", (lambda p=pb: (lambda e: e.tensor_tensor(ll,
        p[0:128, 0:ne], gbB[:, 0:ne], ALU.add)))(),
        [mm_ev, (dv, dv.n), (ac, ac.n)], ((dv, 1),), tag + ":b")
    P.op("vector", lambda e: e.reduce_max(c_m1, ll, axis=AX.X),
         [e0[0]], ((dv, 1),), tag + ":m1")
    P.op("vector", lambda e: e.tensor_scalar(eqt, ll, c_m1, None, ALU.is_ge),
         [(dv, dv.n)], ((dv, 1),), tag + ":eq")
    P.op("vector", lambda e: e.tensor_scalar(eqt, eqt, -1e9, None, ALU.mult),
         [(dv, dv.n)], ((dv, 1),), tag + ":sc")
    P.op("vector", lambda e: e.tensor_tensor(eqt, eqt, ll, ALU.add),
         [(dv, dv.n)], ((dv, 1),), tag + ":lm")
    P.op("vector", lambda e: e.reduce_max(c_m2, eqt, axis=AX.X),
         [(dv, dv.n)], ((dv, 1),), tag + ":m2")
    P.op("vector", lambda e: e.tensor_scalar(c_nm, c_m1, -1.0, None, ALU.mult),
         [(dv, dv.n)], ((dv, 1),), tag + ":nm")
    P.op("vector", lambda e: e.tensor_tensor(c_d, c_m2, c_nm, ALU.add),
         [(dv, dv.n)], ((dv, 1),), tag + ":dm")
    e2 = P.op("scalar", lambda e: e.activation(c_d, c_d, AF.Exp, bias=0.0,
         scale=1.0), [(dv, dv.n)], ((ac, 1),), tag + ":ex")
    P.op("vector", lambda e: e.tensor_scalar(c_d, c_d, 1.0, None, ALU.add),
         [e2[0]], ((dv, 1),), tag + ":p1")
    P.op("vector", lambda e: e.reciprocal(c_r, c_d), [(dv, dv.n)],
         ((dv, 1),), tag + ":rc")
    P.op("vector", lambda e: e.tensor_scalar(eqt, ll, c_m2, None, ALU.is_ge),
         [(dv, dv.n)], ((dv, 1),), tag + ":sel")
    e3 = P.op("scalar", (lambda: (lambda e: e.activation(ll, ll, AF.Exp,
        bias=c_nm, scale=1.0)))(), [(dv, dv.n)], ((ac, 1),), tag + ":z")
    P.op("vector", lambda e: e.tensor_tensor(cw, eqt, ll, ALU.mult),
         [e3[0]], ((dv, 1),), tag + ":cwz")
    ecw = P.op("vector", lambda e: e.tensor_scalar(cw, cw, c_r, None, ALU.mult),
         [(dv, dv.n)], ((dv, 1),), tag + ":cw")
    return cw, ecw[0]


def _ffn1(P, C, n_exp, hid, w1_of, w2_of, b1_vecs, b2_vecs, cw_name, out_name,
          out_f32, tag):
    nc = P.nc
    bf = C["bf"]; f32 = C["f32"]
    IN = C["in"]; SC = C["sc"]
    ld = P.sem("f1_lds"); mm = P.sem("f1_mm")
    cw_sem = P.sem(tag + "cws")
    w1_sems = [P.sem(tag + "w1s0"), P.sem(tag + "w1s1")]
    w2_sems = [P.sem(tag + "w2s0"), P.sem(tag + "w2s1")]
    st_sems = [P.sem(tag + "ss0"), P.sem(tag + "ss1")]
    ac = P.sem("act"); dv = P.sem("dve"); st = P.sem("f1_st")
    su = P.sem("setup")
    psf = C["ps_f32"]
    bank_ev = C.setdefault("bank_ev", [[] for _ in range(6)])
    nhb = hid // 1024
    nhc = hid // 128

    b1_pp = f32.alloc(n_exp * nhc, "b1pp") if b1_vecs else None
    b2_pp = f32.alloc(n_exp * 8, "b2pp") if b2_vecs else None
    for e in range(n_exp):
        if b1_vecs:
            per_partition_bias(P, C, b1_vecs(e), hid,
                               b1_pp[:, e * nhc:(e + 1) * nhc], f"{tag}b1{e}")
        if b2_vecs:
            per_partition_bias(P, C, b2_vecs(e), 1024,
                               b2_pp[:, e * 8:(e + 1) * 8], f"{tag}b2{e}")
    dsy = P.sem("dma_sync")
    b_ev = [(dsy, dsy.n)]

    xn_sb = Ring([bf.alloc(8192, tag + "xn")])
    w1r = Ring([bf.alloc(8192, f"{tag}w1{i}") for i in range(2)])
    w2r = Ring([bf.alloc(4096, f"{tag}w2{i}") for i in range(2)])
    h_sb = bf.alloc(nhc * 1024, tag + "h")
    if n_exp > 1:
        y_acc = f32.alloc(8192, tag + "ya")
        cwB = Ring([f32.alloc(1024, tag + "cwB")])
        ytmp = f32.alloc(512, tag + "yt")
    ost = Ring([(bf if not out_f32 else f32).alloc(1024, f"{tag}os{i}")
                for i in range(2)])
    out_sc = SC[out_name]
    zn2 = SC["zn2"]

    for ts in range(4):
        tsl = slice(ts * 1024, (ts + 1) * 1024)
        ixn, xn_ts, relx = xn_sb.take()
        for c in range(DCH):
            P.dma("sync", xn_ts[:, c * 1024:(c + 1) * 1024],
                  zn2[c][:, tsl], relx if c == 0 else (), f"{tag}xn{ts}.{c}", ld)
        xn_ev = [(ld, ld.n)]
        if n_exp > 1:
            ez = P.op("gpsimd", lambda e: e.memset(y_acc[:, :], 0.0),
                      [(dv, dv.n), (ac, ac.n)], ((su, 1),), f"{tag}z{ts}")
            acc_ev = [(su, su.n)]
        xn_last = []
        for ex in range(n_exp):
            if n_exp > 1:
                icb, cwb, relc = cwB.take()
                P.dma("gpsimd", cwb[:, :],
                      SC[cw_name][ex:ex + 1, tsl].partition_broadcast(128),
                      relc, f"{tag}cwb{ts}.{ex}", cw_sem)
                cw_ev = [(cw_sem, cw_sem.n)]
            for hb in range(nhb):
                iw, w1b, rel1 = w1r.take()
                srcw = w1_of(ex)[:, hb * 1024:(hb + 1) * 1024].rearrange(
                    "(ic p) c -> p ic c", p=128)
                w1s = w1_sems[iw % 2]
                P.dma("gpsimd", w1b[:, :].rearrange("p (ic c) -> p ic c", ic=8),
                      srcw, rel1, f"{tag}w1{ts}.{ex}.{hb}", w1s)
                w1_ev = (w1s, w1s.n)
                h_done = []
                for ht in range(8):
                    for tcl in range(2):
                        bank = tcl
                        pb = psf[bank]
                        for ic in range(DCH):
                            P.op("tensor", (lambda l=w1b[:, ic * 1024 + ht * 128:
                                ic * 1024 + ht * 128 + 128],
                                r=xn_ts[:, ic * 1024 + tcl * 512:
                                        ic * 1024 + tcl * 512 + 512],
                                p=pb, st_=(ic == 0), sp=(ic == DCH - 1):
                                (lambda e: e.matmul(p[:], l, r, start=st_,
                                 stop=sp, skip_group_check=True)))(),
                                ([w1_ev] + xn_ev + bank_ev[bank] + b_ev)
                                if ic == 0 else (),
                                ((mm, 1),) if ic == DCH - 1 else (),
                                f"{tag}h{ts}.{ex}.{hb}.{ht}.{tcl}")
                        h_mm = (mm, mm.n)
                        xn_last.append(h_mm)
                        dst = h_sb[:, (hb * 8 + ht) * 1024 + tcl * 512:
                                   (hb * 8 + ht) * 1024 + tcl * 512 + 512]
                        hc_g = hb * 8 + ht
                        if b1_vecs:
                            fng = (lambda p=pb, d=dst, col=b1_pp[:,
                                ex * nhc + hc_g:ex * nhc + hc_g + 1]:
                                (lambda e: e.activation(d, p[:], AF.Gelu,
                                bias=col, scale=1.0)))()
                        else:
                            fng = (lambda p=pb, d=dst: (lambda e: e.activation(
                                d, p[:], AF.Gelu, bias=0.0, scale=1.0)))()
                        eg = P.op("scalar", fng, [h_mm], ((ac, 1),),
                                  f"{tag}g{ts}.{ex}.{hb}.{ht}.{tcl}")
                        bank_ev[bank] = [eg[0]]
                        h_done.append(eg[0])
                w1r.release(iw, [h_done[-1]])
            h_ev = [(ac, ac.n)]
            for dt in range(8):
                iw2, w2b, rel2 = w2r.take()
                srcw2 = w2_of(ex)[:, dt * 128:(dt + 1) * 128].rearrange(
                    "(hc p) c -> p hc c", p=128)
                w2s = w2_sems[iw2 % 2]
                P.dma("gpsimd",
                      w2b[:, 0:nhc * 128].rearrange("p (hc c) -> p hc c",
                                                    hc=nhc),
                      srcw2, rel2, f"{tag}w2{ts}.{ex}.{dt}", w2s)
                w2_ev = (w2s, w2s.n)
                y_mms = []
                for tcl in range(2):
                    bank = 2 + tcl
                    pb = psf[bank]
                    for hc in range(nhc):
                        P.op("tensor", (lambda l=w2b[:, hc * 128:(hc + 1) * 128],
                            r=h_sb[:, hc * 1024 + tcl * 512:
                                   hc * 1024 + tcl * 512 + 512],
                            p=pb, st_=(hc == 0), sp=(hc == nhc - 1): (lambda e:
                            e.matmul(p[:], l, r, start=st_, stop=sp,
                             skip_group_check=True)))(),
                            ([w2_ev] + h_ev + bank_ev[bank]) if hc == 0 else (),
                            ((mm, 1),) if hc == nhc - 1 else (),
                            f"{tag}y{ts}.{ex}.{dt}.{tcl}")
                    y_mm = (mm, mm.n)
                    y_mms.append(y_mm)
                    if n_exp > 1:
                        et = P.op("vector", (lambda p=pb,
                            c_=cwb[:, tcl * 512:(tcl + 1) * 512],
                            col=b2_pp[:, ex * 8 + dt:ex * 8 + dt + 1]:
                            (lambda e: e.scalar_tensor_tensor(ytmp[:, :], p[:],
                            col, c_, ALU.add, ALU.mult)))(),
                            [y_mm] + cw_ev + acc_ev + [(dv, dv.n)], ((dv, 1),),
                            f"{tag}ym{ts}.{ex}.{dt}.{tcl}")
                        ea = P.op("vector", (lambda d=y_acc[:,
                            dt * 1024 + tcl * 512:dt * 1024 + tcl * 512 + 512]:
                            (lambda e: e.tensor_tensor(d, d, ytmp[:, :],
                            ALU.add)))(), [(dv, dv.n)], ((dv, 1),),
                            f"{tag}ya{ts}.{ex}.{dt}.{tcl}")
                        bank_ev[bank] = [ea[0]]
                    else:
                        io, obuf, relo = ost.take()
                        col = b2_pp[:, dt:dt + 1] if b2_vecs else None
                        if col is not None:
                            fne = (lambda p=pb, d=obuf, cc=col, t2=tcl:
                                (lambda e: e.activation(
                                d[:, t2 * 512:(t2 + 1) * 512], p[:],
                                AF.Identity, bias=cc, scale=1.0)))()
                        else:
                            fne = (lambda p=pb, d=obuf, t2=tcl: (lambda e:
                                e.activation(d[:, t2 * 512:(t2 + 1) * 512],
                                p[:], AF.Copy, bias=0.0, scale=1.0)))()
                        ee = P.op("scalar", fne, [y_mm] + relo, ((ac, 1),),
                                  f"{tag}oe{ts}.{ex}.{dt}.{tcl}")
                        bank_ev[bank] = [ee[0]]
                        if tcl == 1:
                            ssem = st_sems[io % 2]
                            dd = P.dma("scalar", out_sc[dt][:, tsl], obuf[:, :],
                                       [ee[0], (ac, ac.n)], f"{tag}st{ts}.{dt}",
                                       ssem)
                            ost.release(io, [(ssem, ssem.n)])
                        else:
                            ost.release(io, [])
                            ost.i -= 1
                w2r.release(iw2, y_mms)
            if n_exp > 1:
                cwB.release(icb, [(P.sems["dve"], P.sems["dve"].n)])
        xn_sb.release(ixn, xn_last[-8:])
        if n_exp > 1:
            for dt in range(8):
                io, obuf, relo = ost.take()
                ec = P.op("scalar", (lambda s_=y_acc[:, dt * 1024:(dt + 1) * 1024],
                    d=obuf: (lambda e: e.activation(d[:, :], s_, AF.Copy,
                    bias=0.0, scale=1.0)))(), [(dv, dv.n)] + relo, ((ac, 1),),
                    f"{tag}oc{ts}.{dt}")
                ssem = st_sems[io % 2]
                dd = P.dma("scalar", out_sc[dt][:, tsl], obuf[:, :], [ec[0]],
                           f"{tag}st{ts}.{dt}", ssem)
                ost.release(io, [(ssem, ssem.n)])
    P.op("sync", None, [(st_sems[0], st_sems[0].n),
         (st_sems[1], st_sems[1].n)], (), tag + "fin")
    P.barrier(tag)


def emit_abel(P, C):
    IN = C["in"]
    _ffn1(P, C, 1, A4, lambda e: IN["L1_ffn_aw1"], lambda e: IN["L1_ffn_aw2"],
          lambda e: IN["L1_ffn_ab1"], lambda e: IN["L1_ffn_ab2"], None, "abel",
          True, "ab:")


def emit_moe1(P, C):
    IN = C["in"]
    _ffn1(P, C, E1, H1FF, lambda e: IN["L1_ffn_ew1"][e],
          lambda e: IN["L1_ffn_ew2"][e], lambda e: IN["L1_ffn_eb1"][e],
          lambda e: IN["L1_ffn_eb2"][e], "cw1", "moe1", False, "m1:")


def emit_expm(P, C):
    nc = P.nc
    bf = C["bf"]; f32 = C["f32"]
    IN = C["in"]; SC = C["sc"]
    ld = P.sem("ex_ld"); mm = P.sem("ex_mm")
    ac = P.sem("act"); dv = P.sem("dve"); st = P.sem("ex_st")
    su = P.sem("setup")
    psf = C["ps_f32"]; psb = C["ps_bf"]
    bank_ev = C.setdefault("bank_ev", [[] for _ in range(6)])
    idb = C["ident_bf"]

    A = bf.alloc(8192, "exA")
    A2 = bf.alloc(8192, "exA2")
    A3 = bf.alloc(8192, "exA3")
    A4_ = bf.alloc(8192, "exA4")
    H = bf.alloc(8192, "exH")
    H2 = bf.alloc(8192, "exH2")
    Pm = bf.alloc(8192, "exP")
    Qm = bf.alloc(8192, "exQ")

    def sqmm(lhsT_sb, rhs_sb, out_sb, tagm, post=None, post_arg=None):
        ops_ev = []
        for dt in range(8):
            for nh2 in range(2):
                bank = (dt * 2 + nh2) % 2
                pb = psf[bank]
                for ic in range(DCH):
                    P.op("tensor", (lambda l=lhsT_sb[:, ic * 1024 + dt * 128:
                        ic * 1024 + dt * 128 + 128],
                        r=rhs_sb[:, ic * 1024 + nh2 * 512:
                                 ic * 1024 + nh2 * 512 + 512],
                        p=pb, st_=(ic == 0), sp=(ic == DCH - 1): (lambda e:
                        e.matmul(p[:], l, r, start=st_, stop=sp,
                         skip_group_check=True)))(),
                        ([(dv, dv.n), (ac, ac.n)] + bank_ev[bank])
                        if ic == 0 else (),
                        ((mm, 1),) if ic == DCH - 1 else (),
                        f"{tagm}.{dt}.{nh2}")
                mev = (mm, mm.n)
                dst = out_sb[:, dt * 1024 + nh2 * 512:
                             dt * 1024 + nh2 * 512 + 512]
                if post is None:
                    epv = P.op("scalar", (lambda p=pb, d=dst: (lambda e:
                        e.activation(d, p[:], AF.Copy, bias=0.0,
                        scale=1.0)))(), [mev], ((ac, 1),), f"{tagm}e{dt}.{nh2}")
                else:
                    xs = post_arg[:, dt * 1024 + nh2 * 512:
                                  dt * 1024 + nh2 * 512 + 512]
                    epv = P.op("vector", (lambda p=pb, d=dst, x_=xs: (lambda e:
                        e.tensor_tensor(d, p[:], x_, ALU.add)))(), [mev],
                        ((dv, 1),), f"{tagm}e{dt}.{nh2}")
                bank_ev[bank] = [epv[0]]
                ops_ev.append(epv[0])
        return [(P.sems["act"], P.sems["act"].n), (P.sems["dve"], P.sems["dve"].n)]

    def lincomb(dst, terms, tagl, idc=None):
        first = True
        for cf, tl in terms:
            if first:
                P.op("vector", (lambda d=dst, t_=tl, c_=cf: (lambda e:
                    e.tensor_scalar(d[:, :], t_[:, :], c_, None, ALU.mult)))(),
                    [(dv, dv.n), (ac, ac.n), (mm, mm.n)], ((dv, 1),),
                    tagl + "0")
                first = False
            else:
                P.op("vector", (lambda d=dst, t_=tl, c_=cf: (lambda e:
                    e.scalar_tensor_tensor(d[:, :], t_[:, :], c_, d[:, :],
                    ALU.mult, ALU.add)))(), [(dv, dv.n)], ((dv, 1),),
                    tagl + "x")
        if idc is not None:
            for cc2 in range(DCH):
                dsl = dst[:, cc2 * 1024 + cc2 * 128: cc2 * 1024 + (cc2 + 1) * 128]
                P.op("vector", (lambda d=dsl, c_=idc: (lambda e:
                    e.scalar_tensor_tensor(d, idb[:, :], c_, d, ALU.mult,
                    ALU.add)))(), [(dv, dv.n), (su, su.n)], ((dv, 1),),
                    tagl + f"i{cc2}")
        return [(P.sems["dve"], P.sems["dve"].n)]

    def transpose_full(src_sb, dst_sb, tagt):
        for ic in range(DCH):
            for jc in range(DCH):
                kk = ic * DCH + jc
                pbz = psb[kk % 2]
                P.op("tensor", (lambda s_=src_sb[:, ic * 1024 + jc * 128:
                    ic * 1024 + jc * 128 + 128], p=pbz: (lambda e:
                    e.transpose(p[:, 0:128], s_, idb[:])))(),
                    [(dv, dv.n), (su, su.n)] + bank_ev[4 + kk % 2], ((mm, 1),),
                    f"{tagt}t{ic}.{jc}")
                mev = (mm, mm.n)
                ee = P.op("vector", (lambda p=pbz, d=dst_sb[:,
                    jc * 1024 + ic * 128: jc * 1024 + ic * 128 + 128]:
                    (lambda e: e.tensor_copy(d, p[:, 0:128])))(), [mev],
                    ((dv, 1),), f"{tagt}e{ic}.{jc}")
                bank_ev[4 + kk % 2] = [ee[0]]
        return [(P.sems["dve"], P.sems["dve"].n)]

    c = [1.0]
    for i in range(1, 13):
        c.append(c[-1] / i)
    for gi in range(4):
        src = IN["L1_ffn_gens"][gi].rearrange("(ic p) c -> p ic c", p=128)
        P.dma("gpsimd", A[:, :].rearrange("p (ic c) -> p ic c", ic=8), src,
              [(mm, mm.n), (dv, dv.n), (ac, ac.n)], f"ex:g{gi}", ld)
        gev = [(ld, ld.n)]
        P.op("vector", lambda e: e.tensor_scalar(A[:, :], A[:, :], 0.25, None,
             ALU.mult), gev + [(mm, mm.n)], ((dv, 1),), f"ex:a{gi}")
        sqmm(A, A, A2, f"ex:A2.{gi}")
        P.op("vector", lambda e: e.tensor_scalar(A2[:, :], A2[:, :], -1.0, None,
             ALU.mult), [(dv, dv.n), (ac, ac.n)], ((dv, 1),), f"ex:a2n{gi}")
        sqmm(A2, A, A3, f"ex:A3.{gi}")
        sqmm(A2, A2, A4_, f"ex:A4.{gi}")
        lincomb(H, [(0.0, A)], f"ex:q3.{gi}", idc=c[12])
        sqmm(A4_, H, H2, f"ex:h1.{gi}")
        lincomb(H, [(c[9], A), (c[10], A2), (c[11], A3),
                    (1.0, H2)], f"ex:q2.{gi}", idc=c[8])
        sqmm(A4_, H, H2, f"ex:h2.{gi}")
        lincomb(H, [(c[5], A), (c[6], A2), (c[7], A3),
                    (1.0, H2)], f"ex:q1.{gi}", idc=c[4])
        sqmm(A4_, H, H2, f"ex:h3.{gi}")
        lincomb(H, [(c[1], A), (c[2], A2), (c[3], A3),
                    (1.0, H2)], f"ex:q0.{gi}", idc=c[0])
        transpose_full(H, H2, f"ex:tr.{gi}")
        sqmm(H2, H, A2, f"ex:s1.{gi}")
        sqmm(H, H2, A3, f"ex:s1t.{gi}")
        sqmm(A3, A2, H, f"ex:s2.{gi}")
        if gi == 0:
            P.op("vector", lambda e: e.tensor_copy(Pm[:, :], H[:, :]),
                 [(ac, ac.n), (dv, dv.n)], ((dv, 1),), "ex:p0")
            transpose_full(H, Qm, "ex:q0t")
        else:
            sqmm(Qm, H, H2, f"ex:pu.{gi}")
            qev = sqmm(H, Qm, A2, f"ex:qu.{gi}")
            P.op("vector", lambda e: e.tensor_copy(Pm[:, :], H2[:, :]),
                 [(ac, ac.n), (dv, dv.n), (mm, mm.n)], ((dv, 1),), f"ex:pc{gi}")
            P.op("vector", lambda e: e.tensor_copy(Qm[:, :], A2[:, :]),
                 [(dv, dv.n)], ((dv, 1),), f"ex:qc{gi}")
    fev = [(P.sems["dve"], P.sems["dve"].n)]
    for ic2 in range(DCH):
        P.dma("scalar", SC["rtot"][ic2],
              Pm[:, ic2 * 1024:(ic2 + 1) * 1024], fev, f"ex:st{ic2}", st)
    P.op("sync", None, [(st, st.n)], (), "ex:fin")
    P.barrier("expm")


def emit_rotate(P, C):
    nc = P.nc
    bf = C["bf"]; f32 = C["f32"]
    SC = C["sc"]
    ld = P.sem("ro_ld"); mm = P.sem("ro_mm")
    ac = P.sem("act"); dv = P.sem("dve")
    m_sems = [P.sem("ro_m0"), P.sem("ro_m1")]
    s_sems = [P.sem("ro_s0"), P.sem("ro_s1")]
    psf = C["ps_f32"]
    bank_ev = C.setdefault("bank_ev", [[] for _ in range(6)])
    rt = bf.alloc(8192, "ro_rt")
    for ic in range(DCH):
        P.dma("sync", rt[:, ic * 1024:(ic + 1) * 1024], SC["rtot"][ic], (),
              f"ro:w{ic}", ld)
    mo = Ring([bf.alloc(4096, f"ro_m{i}") for i in range(2)])
    ev_st = Ring([f32.alloc(512, f"ro_e{i}") for i in range(2)])
    w_ev = [(ld, ld.n)]
    for tc in range(8):
        tsl = slice(tc * 512, (tc + 1) * 512)
        im, mb, relm = mo.take()
        msem = m_sems[im % 2]
        for c in range(DCH):
            P.dma("sync", mb[:, c * 512:(c + 1) * 512], SC["moe1"][c][:, tsl],
                  relm if c == 0 else (), f"ro:m{tc}.{c}", msem)
        m_ev = [(msem, msem.n)]
        mm_last = []
        for dt in range(8):
            bank = dt % 2
            pb = psf[bank]
            for ic in range(DCH):
                P.op("tensor", (lambda l=rt[:, ic * 1024 + dt * 128:
                    ic * 1024 + dt * 128 + 128],
                    r=mb[:, ic * 512:(ic + 1) * 512], p=pb, st_=(ic == 0),
                    sp=(ic == DCH - 1): (lambda e: e.matmul(p[:], l, r,
                    start=st_, stop=sp, skip_group_check=True)))(),
                    (w_ev + m_ev + bank_ev[bank]) if ic == 0 else (),
                    ((mm, 1),) if ic == DCH - 1 else (), f"ro:mm{tc}.{dt}")
            mev = (mm, mm.n)
            mm_last.append(mev)
            io, obuf, relo = ev_st.take()
            ee = P.op("scalar", (lambda p=pb, d=obuf: (lambda e: e.activation(
                d[:, :], p[:], AF.Copy, bias=0.0, scale=1.0)))(),
                [mev] + relo, ((ac, 1),), f"ro:e{tc}.{dt}")
            bank_ev[bank] = [ee[0]]
            ssem = s_sems[io % 2]
            dd = P.dma("scalar", SC["nonabel"][dt][:, tsl], obuf[:, :],
                       [ee[0]], f"ro:s{tc}.{dt}", ssem)
            ev_st.release(io, [(ssem, ssem.n)])
        mo.release(im, mm_last[-1:])
    P.op("sync", None, [(s_sems[0], s_sems[0].n), (s_sems[1], s_sems[1].n)],
         (), "ro:fin")
    P.barrier("rot")


def emit_final(P, C):
    nc = P.nc
    bf = C["bf"]; f32 = C["f32"]
    IN = C["in"]; SC = C["sc"]
    ld = P.sem("fn_ld"); ldg = P.sem("fn_ldg"); mm = P.sem("fn_mm")
    ac = P.sem("act"); dv = P.sem("dve"); st = P.sem("fn_st")
    o_sems = [P.sem("fn_o0"), P.sem("fn_o1")]
    su = P.sem("setup"); dsy = P.sem("dma_sync")
    psf = C["ps_f32"]; psb = C["ps_bf"]
    bank_ev = C.setdefault("bank_ev", [[] for _ in range(6)])
    idf = C["ident_f32"]
    out = C["out"]

    gcw = bf.alloc(16, "gcw")
    P.op("gpsimd", lambda e: e.dma_start(gcw[:, 0:16],
         IN["L1_ffn_gc_w"].rearrange("(k p) one -> p (k one)", p=128),
         allow_slow_non_contiguous=True), (), ((P.sem("dma_gpsimd"), 16),),
         "fn:gcw")
    gcb = f32.alloc(1, "gcb")
    P.op("sync", lambda e: e.dma_start(gcb[0:1, 0:1],
         IN["L1_ffn_gc_b"].unsqueeze(0)), (), ((P.sem("dma_sync"), 16),),
         "fn:gcb")
    w_ev = [(dsy, dsy.n), (P.sem("dma_gpsimd"), P.sem("dma_gpsimd").n)]

    ab_f = f32.alloc(2048, "fn_ab")
    na_f = f32.alloc(2048, "fn_na")
    z_f = f32.alloc(2048, "fn_z")
    s_f = f32.alloc(2048, "fn_s")
    x_f = f32.alloc(2048, "fn_x")
    ab_b = bf.alloc(2048, "fn_abb")
    na_b = bf.alloc(2048, "fn_nab")
    grow = f32.alloc(256 * 3, "fn_gr")
    gB = f32.alloc(256, "fn_gB")
    ot_st = Ring([f32.alloc(256, f"fn_o{i}") for i in range(2)])
    gc_sc = SC["gate1"]

    for tb in range(16):
        tsl = slice(tb * 256, (tb + 1) * 256)
        guard = [(mm, mm.n), (dv, dv.n), (ac, ac.n), (st, st.n)]
        for c in range(DCH):
            P.dma("sync", ab_f[:, c * 256:(c + 1) * 256], SC["abel"][c][:, tsl],
                  guard if c == 0 else (), f"fn:a{tb}.{c}", ld)
            P.dma("sync", na_f[:, c * 256:(c + 1) * 256],
                  SC["nonabel"][c][:, tsl], (), f"fn:n{tb}.{c}", ld)
            P.dma("sync", z_f[:, c * 256:(c + 1) * 256], SC["z1f"][c][:, tsl],
                  (), f"fn:z{tb}.{c}", ld)
            P.dma("sync", s_f[:, c * 256:(c + 1) * 256], SC["s1"][c][:, tsl],
                  (), f"fn:s{tb}.{c}", ld)
            P.dma("sync", x_f[:, c * 256:(c + 1) * 256], SC["x1T"][c][:, tsl],
                  (), f"fn:x{tb}.{c}", ld)
        lev = [(ld, ld.n)]
        cast_ev = []
        for c in range(DCH):
            e1 = P.op("vector", (lambda cc=c: (lambda e: e.tensor_copy(
                ab_b[:, cc * 256:(cc + 1) * 256],
                ab_f[:, cc * 256:(cc + 1) * 256])))(),
                (lev + [(mm, mm.n)]) if c == 0 else (), ((dv, 1),),
                f"fn:cb{tb}.{c}")
            e2 = P.op("vector", (lambda cc=c: (lambda e: e.tensor_copy(
                na_b[:, cc * 256:(cc + 1) * 256],
                na_f[:, cc * 256:(cc + 1) * 256])))(), (), ((dv, 1),),
                f"fn:cn{tb}.{c}")
        cast_ev = [(dv, dv.n)]
        pb = psf[4]
        for c in range(DCH):
            P.op("tensor", (lambda cc=c: (lambda e: e.matmul(pb[0:1, 0:256],
                gcw[:, cc:cc + 1], ab_b[:, cc * 256:(cc + 1) * 256],
                start=(cc == 0), stop=False, skip_group_check=True)))(),
                (cast_ev + w_ev + bank_ev[4]) if c == 0 else (),
                ((mm, 1),), f"fn:g1{tb}.{c}")
        for c in range(DCH):
            P.op("tensor", (lambda cc=c: (lambda e: e.matmul(pb[0:1, 0:256],
                gcw[:, 8 + cc:8 + cc + 1], na_b[:, cc * 256:(cc + 1) * 256],
                start=False, stop=(cc == DCH - 1), skip_group_check=True)))(),
                (), ((mm, 1),), f"fn:g2{tb}.{c}")
        g_mm = (mm, mm.n)
        e3 = P.op("scalar", lambda e: e.activation(grow[0:1, 0:256],
            pb[0:1, 0:256], AF.Sigmoid, bias=gcb[0:1, 0:1], scale=1.0),
            [g_mm, (st, st.n)], ((ac, 1),), f"fn:sg{tb}")
        bank_ev[4] = [e3[0]]
        d1 = P.dma("scalar", gc_sc[0:1, tsl], grow[0:1, 0:256], [e3[0]],
                   f"fn:gs{tb}", st)
        b1 = P.dma("gpsimd", gB[:, :], gc_sc[0:1, tsl].partition_broadcast(128),
                   [d1[0], (dv, dv.n)], f"fn:gb{tb}", ldg)
        gbev = [(ldg, ldg.n)]
        for c in range(DCH):
            sl2 = slice(c * 256, (c + 1) * 256)
            P.op("vector", (lambda s_=sl2: (lambda e: e.tensor_tensor(
                na_f[:, s_], na_f[:, s_], ab_f[:, s_], ALU.subtract)))(),
                gbev if c == 0 else [(dv, dv.n)], ((dv, 1),), f"fn:d{tb}.{c}")
            P.op("vector", (lambda s_=sl2: (lambda e: e.tensor_tensor(
                na_f[:, s_], na_f[:, s_], gB[:, :], ALU.mult)))(),
                [(dv, dv.n)], ((dv, 1),), f"fn:m{tb}.{c}")
            P.op("vector", (lambda s_=sl2: (lambda e: e.tensor_tensor(
                na_f[:, s_], na_f[:, s_], ab_f[:, s_], ALU.add)))(),
                [(dv, dv.n)], ((dv, 1),), f"fn:mx{tb}.{c}")
            P.op("vector", (lambda s_=sl2: (lambda e: e.tensor_tensor(
                na_f[:, s_], na_f[:, s_], z_f[:, s_], ALU.add)))(),
                [(dv, dv.n)], ((dv, 1),), f"fn:f{tb}.{c}")
            e5 = P.op("scalar", (lambda s_=sl2: (lambda e: e.activation(
                na_f[:, s_], na_f[:, s_], AF.Silu, bias=0.0, scale=1.0)))(),
                [(dv, dv.n)], ((ac, 1),), f"fn:si{tb}.{c}")
            P.op("vector", (lambda s_=sl2: (lambda e: e.tensor_tensor(
                na_f[:, s_], na_f[:, s_], s_f[:, s_], ALU.mult)))(),
                [e5[0]], ((dv, 1),), f"fn:sm{tb}.{c}")
            P.op("vector", (lambda s_=sl2: (lambda e: e.tensor_tensor(
                na_f[:, s_], na_f[:, s_], x_f[:, s_], ALU.add)))(),
                [(dv, dv.n)], ((dv, 1),), f"fn:x2{tb}.{c}")
        x2_ev = [(dv, dv.n)]
        for q in range(2):
            for c in range(DCH):
                kk = q * DCH + c
                pbf = psf[kk % 2]
                P.op("tensor", (lambda s_=na_f[:, c * 256 + q * 128:
                    c * 256 + q * 128 + 128], p=pbf: (lambda e: e.transpose(
                    p[:, 0:128], s_, idf[:])))(),
                    x2_ev + bank_ev[kk % 2] + [(su, su.n)], ((mm, 1),),
                    f"fn:t{tb}.{q}.{c}")
                mev = (mm, mm.n)
                io, obuf, relo = ot_st.take()
                ee = P.op("vector", (lambda p=pbf, d=obuf: (lambda e:
                    e.tensor_copy(d[:, 0:128], p[:, 0:128])))(), [mev] + relo,
                    ((dv, 1),), f"fn:te{tb}.{q}.{c}")
                bank_ev[kk % 2] = [ee[0]]
                osem = o_sems[io % 2]
                dd = P.dma("scalar", out[tb * 256 + q * 128:
                           tb * 256 + (q + 1) * 128, c * 128:(c + 1) * 128],
                           obuf[:, 0:128], [ee[0]], f"fn:o{tb}.{q}.{c}", osem)
                ot_st.release(io, [(osem, osem.n)])
    P.op("sync", None, [(o_sems[0], o_sems[0].n), (o_sems[1], o_sems[1].n),
         (st, st.n)], (), "fn:fin")
    P.barrier("final")


SCRATCH_SPECS = {
    "x0T": ([DCH, 128, T], F32),
    "xn0": ([DCH, 128, T], BF16),
    "q0": ([NSEQ, H0, 64, SEQ], BF16),
    "k0": ([NSEQ, H0, 64, SEQ], BF16),
    "v0": ([NT, 128, 1040], BF16),
    "q1": ([NSEQ, H0, 68, SEQ], BF16),
    "k1": ([NSEQ, H0, 68, SEQ], BF16),
    "v1": ([NT, 128, 1040], BF16),
    "s0": ([DCH, 128, T], F32),
    "s1": ([DCH, 128, T], F32),
    "x1T": ([DCH, 128, T], F32),
    "z1f": ([DCH, 128, T], F32),
    "zn1": ([DCH, 128, T], BF16),
    "zn2": ([DCH, 128, T], BF16),
    "abel": ([DCH, 128, T], F32),
    "nonabel": ([DCH, 128, T], F32),
    "moe0": ([DCH, 128, T], F32),
    "moe1": ([DCH, 128, T], BF16),
    "rtot": ([DCH, 128, D], BF16),
    "cw0": ([E0, T], F32),
    "cw1": ([E1, T], F32),
    "gate1": ([1, T], F32),
    "dbg": ([32, 128, 1024], BF16),
    "lnrows": ([2, T], F32),
}


def build_program(phases, mode="fused", scratch_in=(), scratch_out=()):
    nc = bass.Bass("TRN2", target_bir_lowering=False)
    C = {"in": {}, "sc": {}}
    C["in"]["x"] = nc.dram_tensor("x", [T, D], F32, kind="ExternalInput").ap()
    for nm, shape in INPUT_WEIGHT_SPECS:
        C["in"][nm] = nc.dram_tensor(nm, list(shape), F32, kind="ExternalInput").ap()
    for nm, arr_shape, dt in CONST_SPECS:
        C["in"][nm] = nc.dram_tensor(nm, list(arr_shape), dt, kind="ExternalInput").ap()
    C["out"] = nc.dram_tensor("out", [T, D], F32, kind="ExternalOutput").ap()
    for nm, (shape, dt) in SCRATCH_SPECS.items():
        kind = "Internal"
        if nm in scratch_in:
            kind = "ExternalInput"
        elif nm in scratch_out:
            kind = "ExternalOutput"
        C["sc"][nm] = nc.dram_tensor(nm, list(shape), dt, kind=kind).ap()

    import contextlib
    stack = contextlib.ExitStack()
    bf_t = stack.enter_context(nc.sbuf_tensor("pool_bf", [128, 73300], BF16))
    f32_t = stack.enter_context(nc.sbuf_tensor("pool_f32", [128, 12400], F32))
    C["bf"] = Pool(bf_t, 73300)
    C["f32"] = Pool(f32_t, 12400)
    C["ident_bf"] = stack.enter_context(nc.sbuf_tensor("ident_bf", [128, 128], BF16))
    C["ident_f32"] = stack.enter_context(nc.sbuf_tensor("ident_f32", [128, 128], F32))
    C["qpos_row"] = stack.enter_context(nc.sbuf_tensor("qpos_row", [1, SEQ], F32))
    C["wt_stage"] = [stack.enter_context(nc.sbuf_tensor(f"wts{i}", [128, 1024], BF16))
                     for i in range(2)]
    C["ps_f32"] = [stack.enter_context(nc.psum_tensor(f"psf{i}", [128, 512], F32))
                   for i in range(6)]
    C["ps_bf"] = [stack.enter_context(nc.psum_tensor(f"psb{i}", [128, 1024], BF16))
                  for i in range(2)]

    P = Prog(nc)
    emit_setup(P, C)
    P.barrier("setup")
    for ph in phases:
        C["bf"].reset()
        C["f32"].reset()
        ph(P, C)
    with nc.Block() as block:
        P.emit(block)
    stack.close()
    return nc, P


INPUT_WEIGHT_SPECS = [
    ("L0_attn_wq", (1024, 1024)), ("L0_attn_bq", (1024,)),
    ("L0_attn_wk", (1024, 1024)), ("L0_attn_bk", (1024,)),
    ("L0_attn_wv", (1024, 1024)), ("L0_attn_bv", (1024,)),
    ("L0_attn_wo", (1024, 1024)), ("L0_attn_bo", (1024,)),
    ("L0_ffn_router_w", (8, 1024)),
    ("L0_ffn_w1", (8, 1024, 4096)), ("L0_ffn_w2", (8, 4096, 1024)),
    ("L1_attn_w_qkv", (3072, 1024)), ("L1_attn_w_out", (1024, 1024)),
    ("L1_attn_alibi_slopes", (16,)),
    ("L1_ffn_ln_g", (1024,)), ("L1_ffn_ln_b", (1024,)),
    ("L1_ffn_aw1", (1024, 4096)), ("L1_ffn_ab1", (4096,)),
    ("L1_ffn_aw2", (4096, 1024)), ("L1_ffn_ab2", (1024,)),
    ("L1_ffn_gate_w", (1024, 6)), ("L1_ffn_gate_b", (6,)),
    ("L1_ffn_ew1", (6, 1024, 2048)), ("L1_ffn_eb1", (6, 2048)),
    ("L1_ffn_ew2", (6, 2048, 1024)), ("L1_ffn_eb2", (6, 1024)),
    ("L1_ffn_gens", (4, 1024, 1024)),
    ("L1_ffn_gc_w", (2048, 1)), ("L1_ffn_gc_b", (1,)),
]

CONST_SPECS = [
    ("rope_cos", (128, SEQ), BF16),
    ("rope_sin", (128, SEQ), BF16),
    ("rope_perm", (128, 128), BF16),
    ("ones2_64", (2, 64), BF16),
]


ALL_PHASES = [emit_p1_ln0, emit_p2_qkv0, emit_p3_attn0, emit_moe0,
              emit_ln1, emit_p6_qkv1, emit_p7_attn1, emit_abel,
              emit_moe1, emit_expm, emit_rotate, emit_final]


def kernel(**inputs):
    consts = host_constants()
    x_full = np.ascontiguousarray(np.asarray(inputs["L0_attn_x"], dtype=np.float32))
    shards = x_full.reshape(NCORES, T, D)
    base_map = {nm: np.ascontiguousarray(np.asarray(inputs[nm], np.float32))
                for nm, _ in INPUT_WEIGHT_SPECS}
    for nm, _, _ in CONST_SPECS:
        base_map[nm] = consts[nm]
    nc, P = build_program(ALL_PHASES)
    in_maps = []
    for c in range(NCORES):
        m = dict(base_map)
        m["x"] = shards[c]
        in_maps.append(m)
    res = run_bass_kernel_spmd(nc, in_maps, core_ids=list(range(NCORES)))
    out = np.concatenate([r["out"][None] for r in res.results], axis=0)
    return out.reshape(16, 2048, 1024)
